# version 02
# baseline (speedup 1.0000x reference)
import contextlib
import numpy as np
import ml_dtypes
import concourse.bass as bass
import concourse.mybir as mybir
from concourse.bass_utils import run_bass_kernel_spmd

F32 = mybir.dt.float32
BF16 = mybir.dt.bfloat16
I32 = mybir.dt.int32
AF = mybir.ActivationFunctionType
ALU = mybir.AluOpType
AX = mybir.AxisListType

NCORES = 8
DEBUG = False
L0F_PART = 99
L0F_G = 99
T = 2048
D = 1024
NTB = T // 512
NTC = T // 128
NDC = D // 128
BIG = 1.0e4
NEG = -1.0e30
EPS_RMS = 1e-6
EPS_LN = 1e-5



def _perm_l0_q():
    cols = []
    for c in range(4):
        for j in range(4):
            h = 4 * c + j
            cols.extend(h * 64 + r for r in range(32))
    for c in range(4):
        for j in range(4):
            h = 4 * c + j
            cols.extend(h * 64 + 32 + r for r in range(32))
    return np.array(cols)


def _perm_l1_qk():
    cols = []
    for c in range(8):
        for hh in (2 * c, 2 * c + 1):
            cols.extend(hh * 64 + 2 * i for i in range(32))
            cols.extend(hh * 64 + 2 * i + 1 for i in range(32))
    return np.array(cols)


def _rope_tables():
    dh = 64
    inv = 1.0 / (10000.0 ** (np.arange(0, dh, 2, dtype=np.float64) / dh))
    tt = np.arange(T, dtype=np.float64)[:, None] * inv[None, :]
    cos = np.cos(tt).astype(np.float32)
    sin = np.sin(tt).astype(np.float32)
    cos128 = np.tile(np.ascontiguousarray(cos.T), (4, 1))
    sin128 = np.tile(np.ascontiguousarray(sin.T), (4, 1))
    sign = np.ones((128, 1), np.float32)
    sign[0:32] = -1.0
    sign[64:96] = -1.0
    return np.ascontiguousarray(cos128), np.ascontiguousarray(sin128 * sign)


def _masks():
    out = np.zeros((4, 128, 512), np.float32)
    for v in range(4):
        for p in range(128):
            out[v, p, v * 128 + p + 1:] = NEG
    return out


def host_prepare(inp):
    f32 = lambda a: np.ascontiguousarray(np.asarray(a, dtype=np.float32))
    m = {}
    m["wq"] = f32(np.asarray(inp["L0_attn_Wq"])[:, _perm_l0_q()])
    m["wkv"] = f32(inp["L0_attn_Wkv"])
    m["wk"] = f32(inp["L0_attn_Wk"])
    m["wv"] = f32(inp["L0_attn_Wv"])
    perm_o = np.array([h * 64 + c for h in range(16) for c in range(32)])
    m["wo"] = f32(np.asarray(inp["L0_attn_Wo"])[perm_o, :])
    m["wi"] = f32(np.asarray(inp["L0_ffn_wi"]).T)
    m["wo2"] = f32(np.asarray(inp["L0_ffn_wo"]).T)
    m["acat"] = f32(np.asarray(inp["L0_ffn_lora_A"]).reshape(128, 1024).T)
    m["bcat"] = f32(np.asarray(inp["L0_ffn_lora_B"]).transpose(0, 2, 1).reshape(128, 4096))
    m["gw"] = f32(np.asarray(inp["L0_ffn_gate_w"]).T)
    wqkv = np.asarray(inp["L1_attn_Wqkv"])
    pq = _perm_l1_qk()
    m["wqkv"] = f32(np.concatenate(
        [wqkv[:, 0:1024][:, pq], wqkv[:, 1024:2048][:, pq], wqkv[:, 2048:3072]], axis=1))
    m["wproj"] = f32(inp["L1_attn_Wproj"])
    m["wfc"] = f32(np.asarray(inp["L1_ffn_W_fc"]).T)
    m["wpr"] = f32(np.asarray(inp["L1_ffn_W_proj"]).T)
    m["afc"] = f32(np.asarray(inp["L1_ffn_A_fc"]).T)
    m["bfcT"] = f32(np.asarray(inp["L1_ffn_B_fc"]).T)
    m["apr"] = f32(np.asarray(inp["L1_ffn_A_proj"]).T)
    m["bprT"] = f32(np.asarray(inp["L1_ffn_B_proj"]).T)

    sc = {}
    cols = []

    def put(name, arr):
        arr = np.asarray(arr, np.float32)
        if arr.ndim == 1:
            arr = arr[:, None]
        pad = np.zeros((128, arr.shape[1]), np.float32)
        pad[: arr.shape[0]] = arr
        sc[name] = (sum(c.shape[1] for c in cols), arr.shape[1])
        cols.append(pad)

    put("qg_rope", np.tile(np.asarray(inp["L0_attn_q_g"])[:32], 4))
    put("qg_pad", np.tile(np.asarray(inp["L0_attn_q_g"])[32:], 4))
    put("kg", np.tile(np.asarray(inp["L0_attn_k_g"]), 4))
    put("gate_b", inp["L0_ffn_gate_b"])
    put("bi", np.asarray(inp["L0_ffn_bi"]).reshape(32, 128).T)
    put("bfc_b", np.asarray(inp["L1_ffn_b_fc"]).reshape(32, 128).T)
    put("bpr_b", np.asarray(inp["L1_ffn_b_proj"]).reshape(8, 128).T)
    m["small"] = f32(np.concatenate(cols, axis=1))
    m["small_cols"] = sc

    rows = np.zeros((1, 1664), np.float32)
    rows[0, 0:1024] = np.asarray(inp["L0_ffn_bo"], np.float32)
    rows[0, 1024:1152] = 1.0
    rows[0, 1152:1664] = 2.0
    m["rows"] = rows
    m["id128"] = np.eye(128, dtype=np.float32)
    m["iota"] = np.concatenate(
        [np.tile(np.arange(8, dtype=np.float32), (128, 1)),
         np.tile(np.arange(8, dtype=np.float32) + BIG, (128, 1))], axis=1)
    e4 = np.zeros((128, 4), np.float32)
    for p in range(128):
        e4[p, p // 32] = 1.0
    m["e4"] = e4
    m["e4t"] = f32(e4.T)
    e16 = np.zeros((8, 128), np.float32)
    for e in range(8):
        e16[e, e * 16:(e + 1) * 16] = 1.0
    m["e16"] = e16
    m["masks"] = _masks()
    cos128, sin128s = _rope_tables()
    m["cos128"] = cos128
    m["sin128"] = sin128s
    return m


CONST_NAMES = ["small", "rows", "id128", "iota", "e4", "e4t", "masks",
               "cos128", "sin128", "e16"]


ENGS = ["gpsimd", "tensor", "scalar", "vector", "sync"]


class Sem:
    def __init__(self, ctx, nc, name):
        self.h = ctx.enter_context(nc.semaphore(name))
        self.n = 0

    def tok(self):
        return (self, self.n)


class Script:
    def __init__(self, prog):
        self.prog = prog
        self.ops = {e: [] for e in ENGS}

    def add(self, eng, emit, waits=(), inc=None, inck=1, fence=False, dma=False):
        incs = []
        tok = None
        fw = None
        if fence:
            b = self.prog.bmp[eng]
            b.n += 1
            incs.append((b, 1))
            fw = (b, b.n)
            tok = fw
        elif inc is not None:
            inc.n += inck
            incs.append((inc, inck))
            tok = (inc, inc.n)
        self.ops[eng].append((emit, [w for w in waits if w is not None], incs, fw, dma, tok))
        return tok

    def fence_all(self):
        p = self.prog
        return ([p.pe.tok(), p.act.tok(), p.dve.tok(), p.pool.tok(),
                 p.dmaW.tok(), p.dmaS.tok()] +
                [p.bmp[e].tok() for e in ENGS])

    def fn_for(self, eng):
        ops = self.ops[eng]
        eng_name = eng
        if not ops:
            return None

        def f(e):
            pending = None
            last_tok = None
            last_nondma_i = None
            for i in range(len(ops) - 1, -1, -1):
                if not ops[i][4]:
                    last_nondma_i = i
                    break
            for i, (emit, waits, incs, fw, dma, tok) in enumerate(ops):
                if pending is not None:
                    e.wait_ge(pending[0].h, pending[1])
                pending = fw
                for s, v in waits:
                    e.wait_ge(s.h, v)
                ins = emit(e)
                if i == last_nondma_i and not incs:
                    b = self.prog.bmp[eng_name]
                    b.n += 1
                    incs = [(b, 1)]
                    tok = (b, b.n)
                for s, k in incs:
                    ins.then_inc(s.h, k)
                if i == last_nondma_i:
                    last_tok = tok
            return last_tok
        return f


class Prog:
    def __init__(self, meta, weight_names):
        self.nc = bass.Bass("TRN2", target_bir_lowering=False, debug=False)
        self.ctx = contextlib.ExitStack()
        self.meta = meta
        nc = self.nc
        self.hbm = {}
        self.hbm["x_in"] = nc.dram_tensor("x_in", [D, T], F32, kind="ExternalInput").ap()
        self.hbm["x_out"] = nc.dram_tensor("x_out", [D, T], F32, kind="ExternalOutput").ap()
        self.input_names = ["x_in"]
        for name in weight_names:
            self.add_input(name)
        self.hbm["scr"] = nc.dram_tensor("scr", [16, 128], F32, kind="Internal").ap()
        self.hbm["wq_fc"] = nc.dram_tensor("wq_fc", [128, 8 * 4096], BF16, kind="Internal").ap()
        self.hbm["wq_pr"] = nc.dram_tensor("wq_pr", [128, 32 * 1024], BF16, kind="Internal").ap()
        if DEBUG:
            for i in range(4):
                self.hbm[f"dbg{i}"] = nc.dram_tensor(f"dbg{i}", [128, 2048], F32,
                                                     kind="ExternalOutput").ap()
        c = self.ctx
        self.X = c.enter_context(nc.sbuf_tensor("gX", [128, 8 * T], F32))
        self.XB = c.enter_context(nc.sbuf_tensor("gXB", [128, 8 * T], BF16))
        scn = meta["small"].shape[1]
        self.SMALL = c.enter_context(nc.sbuf_tensor("gSMALL", [128, scn], F32))
        self.ROWS = c.enter_context(nc.sbuf_tensor("gROWS", [1, 1664], F32))
        self.ID = c.enter_context(nc.sbuf_tensor("gID", [128, 128], F32))
        self.IDB = c.enter_context(nc.sbuf_tensor("gIDB", [128, 128], BF16))
        self.IOTA = c.enter_context(nc.sbuf_tensor("gIOTA", [128, 16], F32))
        self.E4 = c.enter_context(nc.sbuf_tensor("gE4", [128, 4], F32))
        self.E4T = c.enter_context(nc.sbuf_tensor("gE4T", [4, 128], F32))
        self.E16 = c.enter_context(nc.sbuf_tensor("gE16", [8, 128], BF16))
        self.ONEC = c.enter_context(nc.sbuf_tensor("gONEC", [128, 1], F32))
        self.bar = Sem(c, nc, "s_bar")
        self.dmaW = Sem(c, nc, "s_dmaW")
        self.dmaS = Sem(c, nc, "s_dmaS")
        self.pe = Sem(c, nc, "s_pe")
        self.act = Sem(c, nc, "s_act")
        self.dve = Sem(c, nc, "s_dve")
        self.pool = Sem(c, nc, "s_pool")
        self.bmp = {e: Sem(c, nc, "s_bmp_" + e) for e in ENGS}

    def add_input(self, name):
        if name not in self.hbm:
            a = self.meta[name]
            self.hbm[name] = self.nc.dram_tensor(
                name, list(a.shape), F32, kind="ExternalInput").ap()
            self.input_names.append(name)
        return self.hbm[name]

    def small(self, name):
        off, w = self.meta["small_cols"][name]
        return self.SMALL[:, off:off + w]

    def rows(self, name, n):
        offs = {"bo": 0, "ones": 1024, "twos": 1152}
        return self.ROWS[0:1, offs[name]:offs[name] + n]


def run_stage(prog, script):
    bar = prog.bar
    enter = bar.n
    wN, sN = prog.dmaW.n, prog.dmaS.n

    with prog.nc.Block() as block:
        for name in ENGS:
            fn = script.fn_for(name)

            def emit(eng, fn=fn):
                eng.wait_ge(bar.h, enter)
                last_tok = fn(eng) if fn else None
                if last_tok is not None:
                    eng.wait_ge(last_tok[0].h, last_tok[1])
                eng.wait_ge(prog.dmaW.h, wN)
                eng.wait_ge(prog.dmaS.h, sN)
                eng.sem_inc(bar.h, 1)

            getattr(block, name)(emit)
    bar.n += 5




def E_dma(out, in_):
    return lambda e: e.dma_start(out=out, in_=in_)


def E_mm(out, lhsT, rhs, start, stop, tp=None):
    if tp is None:
        return lambda e: e.matmul(out, lhsT, rhs, start=start, stop=stop)
    return lambda e: e.matmul(out, lhsT, rhs, start=start, stop=stop,
                              tile_position=tp, skip_group_check=True)


def E_tr(out, in_, ident):
    return lambda e: e.transpose(out, in_, ident)


def E_act(out, in_, func, bias=0.0, scale=1.0, accum=None):
    if accum is None:
        return lambda e: e.activation(out, in_, func, bias=bias, scale=scale)
    return lambda e: e.activation(out, in_, func, bias=bias, scale=scale,
                                  accum_out=accum)


def E_tt(out, a, b, op):
    return lambda e: e.tensor_tensor(out, a, b, op)


def E_ts(out, a, s1, op0, s2=None, op1=None):
    if op1 is None:
        return lambda e: e.tensor_scalar(out, a, s1, None, op0)
    return lambda e: e.tensor_scalar(out, a, s1, s2, op0, op1)


def E_stt(out, in0, s, in1, op0, op1):
    return lambda e: e.scalar_tensor_tensor(out, in0, s, in1, op0, op1)


def E_copy(out, in_):
    return lambda e: e.tensor_copy(out, in_)


def E_red(out, in_, op, negate=False, absval=False):
    return lambda e: e.tensor_reduce(out, in_, AX.X, op,
                                     apply_absolute_value=absval, negate=negate)


def E_redC(out, in_, op):
    return lambda e: e.tensor_reduce(out, in_, AX.C, op)


def E_recip(out, in_):
    return lambda e: e.reciprocal(out, in_)


def E_memset(ap, v):
    return lambda e: e.memset(ap, v)


def load_fm(S, prog, sb, name, col0=0):
    hbm_ap = prog.add_input(name)
    ktot, mtot = hbm_ap.shape
    nk = (ktot + 127) // 128
    if ktot <= 128:
        return S.add("gpsimd", E_dma(sb[0:ktot, col0:col0 + mtot], hbm_ap),
                     inc=prog.dmaW, inck=16, dma=True)
    return S.add("gpsimd",
                 E_dma(sb[:, col0:col0 + nk * mtot].rearrange("p (c m) -> p c m", c=nk),
                       hbm_ap.rearrange("(c p) m -> p c m", p=128)),
                 inc=prog.dmaW, inck=16, dma=True)




def emit_ln(S, prog, sb, z_ready, refresh_xb=True, dbg=None):
    pe, act, dve = prog.pe, prog.act, prog.dve
    X, XB = prog.X, prog.XB
    p_st, p_rb, p_mb = sb["p_st"], sb["p_rb"], sb["p_mb"]
    SQ, LNR, LNR2 = sb["SQ"], sb["LNR"], sb["LNR2"]
    st_free = []
    b_free = []
    sq_war = None
    for tb in range(NTB):
        o = tb * 512
        prev_mmq = {0: None, 1: None}
        g_sum = g_sq = None
        for dc in range(NDC):
            xap = X[:, dc * T + o: dc * T + o + 512]
            h = (dc % 2) * 512
            w = ([z_ready, sq_war] if dc == 0 else []) + \
                ([prev_mmq[dc % 2]] if prev_mmq[dc % 2] is not None else [])
            sq_tok = S.add("vector", E_tt(SQ[:, h:h + 512], xap, xap, ALU.mult),
                           waits=w, fence=True)
            g_sum = S.add("tensor", E_mm(p_st[0:1, 0:512], prog.ONEC[:, :], xap,
                                         dc == 0, dc == NDC - 1),
                          waits=([z_ready] + st_free if dc == 0 else ()),
                          inc=pe)
            g_sq = S.add("tensor", E_mm(p_st[0:1, 512:1024], prog.ONEC[:, :],
                                        SQ[:, h:h + 512], dc == 0, dc == NDC - 1),
                         waits=[sq_tok], inc=pe)
            prev_mmq[dc % 2] = g_sq
        rstd_ap = LNR[0:1, 0:512]
        m_ap = LNR2[0:1, 0:512]
        ms_ap = LNR[32:33, 0:512]
        m2_ap = LNR2[32:33, 0:512]
        t_m = S.add("vector", E_ts(m_ap, p_st[0:1, 0:512], 1.0 / D, ALU.mult),
                    waits=[g_sum], fence=True)
        t_ms = S.add("vector", E_ts(ms_ap, p_st[0:1, 512:1024], 1.0 / D, ALU.mult),
                     waits=[g_sq], fence=True)
        st_free = [t_m, t_ms]
        S.add("vector", E_tt(m2_ap, m_ap, m_ap, ALU.mult), fence=True)
        S.add("vector", E_tt(ms_ap, ms_ap, m2_ap, ALU.subtract), fence=True)
        t_var = S.add("vector", E_ts(ms_ap, ms_ap, EPS_LN, ALU.add),
                      inc=dve, fence=True)
        S.add("scalar", E_act(m2_ap, ms_ap, AF.Ln), waits=[t_var], fence=True)
        t_re = S.add("scalar", E_act(ms_ap, m2_ap, AF.Exp, scale=-0.5),
                     fence=True)
        t_rstd = S.add("vector", E_copy(rstd_ap, ms_ap), waits=[t_re], fence=True)
        if dbg is not None:
            S.add("vector", E_copy(dbg[0:1, o:o + 512], m_ap), waits=[t_m])
            S.add("vector", E_copy(dbg[32:33, o:o + 512], ms_ap), waits=[t_var])
            S.add("vector", E_copy(dbg[64:65, o:o + 512], rstd_ap), waits=[t_rstd], fence=True)
        S.add("tensor", E_mm(p_rb[:, :], prog.rows("ones", 128), rstd_ap, True, True),
              waits=[t_rstd] + b_free)
        g_b = S.add("tensor", E_mm(p_mb[:, :], prog.rows("ones", 128), m_ap, True, True),
                    waits=[t_m], inc=pe)
        b_free = []
        for dc in range(NDC):
            xap = X[:, dc * T + o: dc * T + o + 512]
            h = (dc % 2) * 512
            S.add("vector", E_tt(SQ[:, h:h + 512], xap, p_mb[:, :], ALU.subtract),
                  waits=([g_b] if dc == 0 else ()))
            n2 = S.add("vector", E_tt(xap, SQ[:, h:h + 512], p_rb[:, :], ALU.mult),
                       inc=dve)
            if refresh_xb:
                S.add("vector", E_copy(XB[:, dc * T + o: dc * T + o + 512], xap),
                      inc=dve)
            b_free = [n2]
            sq_war = n2
    return prog.dve.tok()


def ln_buffers(prog, sctx, tag):
    nc = prog.nc
    return dict(
        SQ=sctx.enter_context(nc.sbuf_tensor(tag + "_lnsq", [128, 1024], F32)),
        LNR=sctx.enter_context(nc.sbuf_tensor(tag + "_lnr", [33, 512], F32)),
        LNR2=sctx.enter_context(nc.sbuf_tensor(tag + "_lnr2", [33, 512], F32)),
        p_st=sctx.enter_context(nc.psum_tensor(tag + "_pst", [1, 1024], F32)),
        p_rb=sctx.enter_context(nc.psum_tensor(tag + "_prb2", [128, 512], F32)),
        p_mb=sctx.enter_context(nc.psum_tensor(tag + "_pmb", [128, 512], F32)),
    )




def stage_load(prog):
    S = Script(prog)
    m = prog.meta
    dmaS, dmaW, dve = prog.dmaS, prog.dmaW, prog.dve
    S.add("sync", E_dma(prog.X[:, :].rearrange("p (c t) -> p c t", c=NDC),
                        prog.hbm["x_in"].rearrange("(c p) t -> p c t", p=128)),
          inc=dmaS, inck=16, dma=True)
    for hbname, sb in [("small", prog.SMALL), ("rows", prog.ROWS), ("id128", prog.ID),
                       ("iota", prog.IOTA), ("e4", prog.E4), ("e4t", prog.E4T)]:
        ap = prog.add_input(hbname)
        rr = sb[0:ap.shape[0], 0:ap.shape[1]]
        S.add("sync", E_dma(rr, ap), inc=dmaS, inck=16, dma=True)
    S.add("gpsimd", E_dma(prog.IDB[:, :], prog.add_input("id128")), inc=dmaW, inck=16, dma=True)
    S.add("gpsimd", E_dma(prog.E16[0:8, :], prog.add_input("e16")), inc=dmaW, inck=16, dma=True)
    xw = dmaS.tok()
    S.add("vector", E_memset(prog.ONEC[:, :], 1.0))
    S.add("vector", E_copy(prog.XB[:, :], prog.X[:, :]), waits=[xw], inc=dve)
    return S


def stage_store(prog):
    S = Script(prog)
    S.add("sync", E_dma(prog.hbm["x_out"].rearrange("(c p) t -> p c t", p=128),
                        prog.X[:, :].rearrange("p (c t) -> p c t", c=NDC)),
          inc=prog.dmaS, inck=16, dma=True)
    tok = prog.dmaS.tok()
    S.add("sync", lambda e: e.nop(), waits=[tok])
    return S




def stage_l0_attn(prog, sctx):
    nc = prog.nc
    S = Script(prog)
    X, XB = prog.X, prog.XB
    al = lambda name, shape, dt: sctx.enter_context(nc.sbuf_tensor(name, shape, dt))
    pe, act, dve, dmaW = prog.pe, prog.act, prog.dve, prog.dmaW

    WQ = al("a0_wq", [128, 8 * 1024], BF16)
    WKV = al("a0_wkv", [128, 8 * 128], BF16)
    WK = al("a0_wk", [128, 512], BF16)
    WV = al("a0_wv", [128, 512], BF16)
    WO = al("a0_wo", [128, 4 * 1024], BF16)
    QN = al("a0_qn", [128, 4 * 2048], BF16)
    KN = al("a0_kn", [128, 4 * 2048], BF16)
    VT = al("a0_vt", [128, 16 * 512], BF16)
    QF = al("a0_qf", [128, 512], F32)
    QP = al("a0_qp", [128, 512], F32)
    VB = al("a0_vb", [128, 2048], BF16)
    KVB = al("a0_kvb", [128, 512], BF16)
    SQ2 = al("a0_sq", [128, 1024], F32)
    MSK = al("a0_msk", [128, 4 * 512], BF16)
    RST = al("a0_rst", [4, 512], F32)
    RSF = al("a0_rsf", [4, 512], F32)
    NM = al("a0_nm", [128, 4], F32)
    SR = al("a0_sr", [128, 4], F32)
    OC = XB
    EB = XB[:, 8192:10240]
    PT = XB[:, 10240:12288]

    psc = lambda name, shape, dt=F32: sctx.enter_context(nc.psum_tensor(name, shape, dt))


    for sb, name in [(WQ, "wq"), (WKV, "wkv"), (WK, "wk"), (WV, "wv"), (WO, "wo")]:
        load_fm(S, prog, sb, name)
    mask_ap = prog.add_input("masks")
    S.add("gpsimd", E_dma(MSK[:, :].rearrange("p (v q) -> p v q", v=4),
                          mask_ap.rearrange("v p q -> p v q")), inc=dmaW, inck=16, dma=True)
    wl = prog.dmaW.tok()

    qg_rope = prog.small("qg_rope")
    kg = prog.small("kg")

    with contextlib.ExitStack() as p1:
        p_a = p1.enter_context(nc.psum_tensor("a0_pa", [128, 512], F32))
        p_b = p1.enter_context(nc.psum_tensor("a0_pb", [128, 512], F32))
        p_ss = p1.enter_context(nc.psum_tensor("a0_pss", [4, 512], F32))
        p_rb = p1.enter_context(nc.psum_tensor("a0_prb", [128, 512], F32))
        p_t = p1.enter_context(nc.psum_tensor("a0_ptr", [128, 1024], BF16))

        a_free = b_free = ss_free = rb_free = t_free = None
        sq_free = rst_free = qf_free = kvb_free = None
        for c in range(4):
            vdone = None
            for tb in range(4):
                oq = tb * 512
                xsl = lambda dc: XB[:, dc * T + oq: dc * T + oq + 512]
                for dc in range(NDC):
                    g_qr = S.add("tensor",
                                 E_mm(p_a[:, :], WQ[:, dc * 1024 + c * 128: dc * 1024 + c * 128 + 128],
                                      xsl(dc), dc == 0, dc == NDC - 1),
                                 waits=([wl, a_free] if dc == 0 else ()),
                                 inc=(pe if dc == NDC - 1 else None))
                for dc in range(NDC):
                    g_qp = S.add("tensor",
                                 E_mm(p_b[:, :], WQ[:, dc * 1024 + (4 + c) * 128: dc * 1024 + (4 + c) * 128 + 128],
                                      xsl(dc), dc == 0, dc == NDC - 1),
                                 waits=([b_free] if dc == 0 else ()),
                                 inc=(pe if dc == NDC - 1 else None))
                e_qr = S.add("scalar", E_act(QF[:, 0:512], p_a[:, :], AF.Identity),
                             waits=[g_qr, qf_free], inc=act)
                s_qr = S.add("vector", E_tt(SQ2[:, 0:512], QF[:, 0:512], QF[:, 0:512], ALU.mult),
                             waits=[e_qr, sq_free], fence=True)
                e_qp = S.add("vector", E_copy(QP[:, 0:512], p_b[:, :]),
                             waits=[g_qp], fence=True)
                s_qp = S.add("vector", E_tt(SQ2[:, 512:1024], QP[:, 0:512], QP[:, 0:512], ALU.mult),
                             fence=True)
                a_free = s_qr
                b_free = e_qp
                S.add("tensor", E_mm(p_ss[0:4, :], prog.E4[:, :], SQ2[:, 0:512], True, False),
                      waits=[s_qr, s_qp, ss_free])
                g_ss = S.add("tensor", E_mm(p_ss[0:4, :], prog.E4[:, :], SQ2[:, 512:1024],
                                            False, True), inc=pe)
                v1 = S.add("vector", E_ts(RSF[0:4, :], p_ss[0:4, :], 1.0 / 64, ALU.mult,
                                          EPS_RMS, ALU.add),
                           waits=[g_ss, rst_free], inc=dve, fence=True)
                r1 = S.add("scalar", E_act(RSF[0:4, :], RSF[0:4, :], AF.Ln),
                           waits=[v1], fence=True)
                r2a = S.add("scalar", E_act(RSF[0:4, :], RSF[0:4, :], AF.Exp, scale=-0.5),
                            fence=True)
                r2 = S.add("vector", E_copy(RST[0:4, :], RSF[0:4, :]),
                           waits=[r2a, rst_free], fence=True)
                ss_free = v1
                g_rb = S.add("tensor", E_mm(p_rb[:, :], prog.E4T[0:4, :], RST[0:4, :],
                                            True, True), waits=[r2, rb_free], inc=pe)
                n_q = S.add("vector",
                            E_stt(QN[:, c * 2048 + oq: c * 2048 + oq + 512],
                                  QF[:, 0:512], qg_rope, p_rb[:, :],
                                  ALU.mult, ALU.mult),
                            waits=[g_rb, e_qr], inc=dve, fence=True)
                rb_free = n_q
                for dc in range(NDC):
                    g_kv = S.add("tensor",
                                 E_mm(p_a[:, :], WKV[:, dc * 128: dc * 128 + 128],
                                      xsl(dc), dc == 0, dc == NDC - 1),
                                 waits=([e_qr] if dc == 0 else ()),
                                 inc=(pe if dc == NDC - 1 else None))
                e_kv = S.add("vector", E_copy(KVB[:, 0:512], p_a[:, :]),
                             waits=[g_kv, kvb_free], fence=True)
                a_free = e_kv
                g_k = S.add("tensor", E_mm(p_b[:, :], WK[:, c * 128: c * 128 + 128],
                                           KVB[:, 0:512], True, True),
                            waits=[e_kv, b_free, n_q], inc=pe)
                e_k = S.add("scalar", E_act(QF[:, 0:512], p_b[:, :], AF.Identity),
                            waits=[g_k], inc=act)
                s_k = S.add("vector", E_tt(SQ2[:, 0:512], QF[:, 0:512], QF[:, 0:512], ALU.mult),
                            waits=[g_ss, e_k], fence=True)
                b_free = s_k
                g_ss2 = S.add("tensor", E_mm(p_ss[0:4, :], prog.E4[:, :], SQ2[:, 0:512],
                                             True, True), waits=[s_k, ss_free], inc=pe)
                sq_free = g_ss2
                v2 = S.add("vector", E_ts(RSF[0:4, :], p_ss[0:4, :], 1.0 / 32, ALU.mult,
                                          EPS_RMS, ALU.add),
                           waits=[g_ss2, g_rb], inc=dve, fence=True)
                r3 = S.add("scalar", E_act(RSF[0:4, :], RSF[0:4, :], AF.Ln),
                           waits=[v2], fence=True)
                r4a = S.add("scalar", E_act(RSF[0:4, :], RSF[0:4, :], AF.Exp, scale=-0.5),
                            fence=True)
                r4 = S.add("vector", E_copy(RST[0:4, :], RSF[0:4, :]),
                           waits=[r4a, g_rb], fence=True)
                ss_free = v2
                g_rb2 = S.add("tensor", E_mm(p_rb[:, :], prog.E4T[0:4, :], RST[0:4, :],
                                             True, True), waits=[r4, rb_free], inc=pe)
                rst_free = g_rb2
                n_k = S.add("vector",
                            E_stt(KN[:, c * 2048 + oq: c * 2048 + oq + 512],
                                  QF[:, 0:512], kg, p_rb[:, :],
                                  ALU.mult, ALU.mult),
                            waits=[g_rb2, e_k], inc=dve, fence=True)
                rb_free = n_k
                qf_free = n_k
                g_v = S.add("tensor", E_mm(p_b[:, :], WV[:, c * 128: c * 128 + 128],
                                           KVB[:, 0:512], True, True),
                            waits=[b_free], inc=pe)
                e_v = S.add("vector", E_copy(VB[:, oq:oq + 512], p_b[:, :]),
                            waits=[g_v], fence=True)
                b_free = e_v
                kvb_free = g_v
                vdone = e_v
            for grp in range(2):
                for k in range(8):
                    sc8 = grp * 8 + k
                    g_t = S.add("tensor", E_tr(p_t[:, k * 128:k * 128 + 128],
                                               VB[:, sc8 * 128:sc8 * 128 + 128],
                                               prog.IDB[:, :]),
                                waits=([vdone, t_free] if k == 0 else ()),
                                inc=(pe if k == 7 else None))
                vt3 = VT[:, :].rearrange("p (s c r) -> p s c r", s=16, c=4)
                ev = S.add("vector",
                           E_copy(vt3[:, grp * 8:grp * 8 + 8, c, :],
                                  p_t[:, :].rearrange("p (s r) -> p s r", s=8)),
                           waits=[g_t], inc=dve)
                t_free = ev

    ph1 = S.fence_all()
    with contextlib.ExitStack() as p2:
        p_s = p2.enter_context(nc.psum_tensor("a0_ps", [128, 2048], F32))
        p_t2 = p2.enter_context(nc.psum_tensor("a0_pt2", [128, 1024], BF16))
        p_pv = p2.enter_context(nc.psum_tensor("a0_ppv", [32, 512], F32))
        s_free = None
        t2_free = None
        pv_free = None
        first = True
        for c in range(4):
            for j in range(4):
                qrow = QN[32 * j:32 * j + 32, :]
                krow = KN[32 * j:32 * j + 32, :]
                for tqc in range(NTC):
                    nsg = tqc // 4 + 1
                    nsc = tqc + 1
                    v = tqc % 4
                    for sg in range(nsg):
                        w = []
                        if sg == 0:
                            w = [s_free] + (ph1 if first else [])
                            first = False
                        g_s = S.add("tensor",
                                    E_mm(p_s[:, sg * 512:sg * 512 + 512],
                                         qrow[:, c * 2048 + tqc * 128: c * 2048 + tqc * 128 + 128],
                                         krow[:, c * 2048 + sg * 512: c * 2048 + sg * 512 + 512],
                                         True, True, tp=(32 * j, 0)),
                                    waits=w, inc=(pe if sg == nsg - 1 else None))
                    t_msk = S.add("vector", E_tt(p_s[:, (nsg - 1) * 512:(nsg - 1) * 512 + 512],
                                                 p_s[:, (nsg - 1) * 512:(nsg - 1) * 512 + 512],
                                                 MSK[:, v * 512:v * 512 + 512], ALU.add),
                                  waits=[g_s], fence=True)
                    t_nm = S.add("vector", E_red(NM[:, 0:1], p_s[:, 0:nsg * 512], ALU.max,
                                                 negate=True), fence=True)
                    t_nmm = S.add("vector", E_ts(NM[:, 1:2], NM[:, 0:1], 0.125, ALU.mult),
                                  inc=dve, fence=True)
                    e_e = S.add("scalar", E_act(EB[:, 0:nsg * 512], p_s[:, 0:nsg * 512],
                                                AF.Exp, bias=NM[:, 1:2], scale=0.125,
                                                accum=SR[:, 0:1]),
                                waits=[t_nmm], inc=act, fence=True)
                    s_free = e_e
                    t_rs = S.add("vector", E_recip(SR[:, 1:2], SR[:, 0:1]),
                                 waits=[e_e], fence=True)
                    t_nrm = S.add("vector", E_ts(EB[:, 0:nsg * 512], EB[:, 0:nsg * 512],
                                                 SR[:, 1:2], ALU.mult),
                                  inc=dve, fence=True)
                    ptok = []
                    for grp in range((nsc + 7) // 8):
                        kmax = min(8, nsc - grp * 8)
                        for k in range(kmax):
                            sc8 = grp * 8 + k
                            g_t = S.add("tensor", E_tr(p_t2[:, k * 128:k * 128 + 128],
                                                       EB[:, sc8 * 128:sc8 * 128 + 128],
                                                       prog.IDB[:, :]),
                                        waits=([t_nrm, t2_free] if k == 0 else ()),
                                        inc=(pe if k == kmax - 1 else None))
                        ev = S.add("vector",
                                   E_copy(PT[:, grp * 8 * 128: grp * 8 * 128 + kmax * 128],
                                          p_t2[:, 0:kmax * 128]),
                                   waits=[g_t], inc=dve)
                        t2_free = ev
                        ptok.append(ev)
                    for sc8 in range(nsc):
                        w = []
                        if sc8 == 0:
                            w = [ptok[-1], pv_free]
                        g_pv = S.add("tensor",
                                     E_mm(p_pv[0:32, 0:128],
                                          VT[:, sc8 * 512 + c * 128 + 32 * j: sc8 * 512 + c * 128 + 32 * j + 32],
                                          PT[:, sc8 * 128:sc8 * 128 + 128],
                                          sc8 == 0, sc8 == nsc - 1),
                                     waits=w, inc=(pe if sc8 == nsc - 1 else None))
                    pv_free = S.add("vector",
                                    E_copy(OC[32 * j:32 * j + 32,
                                              c * 2048 + tqc * 128: c * 2048 + tqc * 128 + 128],
                                           p_pv[0:32, 0:128]),
                                    waits=[g_pv], fence=True)

    ph2 = S.fence_all()
    with contextlib.ExitStack() as p3:
        p_o = p3.enter_context(nc.psum_tensor("a0_po", [128, 512], F32))
        lnb = dict(
            SQ=SQ2,
            LNR=p3.enter_context(nc.sbuf_tensor("a0_lnr", [33, 512], F32)),
            LNR2=p3.enter_context(nc.sbuf_tensor("a0_lnr2", [33, 512], F32)),
            p_st=p3.enter_context(nc.psum_tensor("a0_pst", [1, 1024], F32)),
            p_rb=p3.enter_context(nc.psum_tensor("a0_prb2", [128, 512], F32)),
            p_mb=p3.enter_context(nc.psum_tensor("a0_pmb", [128, 512], F32)),
        )
        o_free = None
        zt = None
        for tb in range(4):
            for dc in range(NDC):
                for kc in range(4):
                    w = []
                    if kc == 0:
                        w = ([o_free] if o_free else []) + (ph2 if (tb == 0 and dc == 0) else [])
                    g_o = S.add("tensor",
                                E_mm(p_o[:, :], WO[:, kc * 1024 + dc * 128: kc * 1024 + dc * 128 + 128],
                                     OC[:, kc * 2048 + tb * 512: kc * 2048 + tb * 512 + 512],
                                     kc == 0, kc == 3),
                                waits=w, inc=(pe if kc == 3 else None))
                zt = S.add("vector", E_tt(X[:, dc * T + tb * 512: dc * T + tb * 512 + 512],
                                          p_o[:, :],
                                          X[:, dc * T + tb * 512: dc * T + tb * 512 + 512],
                                          ALU.add),
                           waits=[g_o], inc=dve)
                o_free = zt
        if DEBUG:
            for i, src_ap in enumerate([XB[:, 0:2048], XB[:, 2048:4096],
                                        XB[:, 4096:6144], XB[:, 6144:8192]]):
                S.add("gpsimd", E_dma(prog.hbm[f"dbg{i}"], src_ap),
                      waits=[prog.pe.tok(), prog.act.tok(), prog.dve.tok()] + [prog.bmp[e].tok() for e in ENGS],
                      inc=prog.dmaW, inck=16, dma=True)
        emit_ln(S, prog, lnb, zt)
    return S


def stage_l0_ffn(prog, sctx):
    nc = prog.nc
    S = Script(prog)
    X, XB = prog.X, prog.XB
    al = lambda name, shape, dt: sctx.enter_context(nc.sbuf_tensor(name, shape, dt))
    pe, act, dve, dmaW = prog.pe, prog.act, prog.dve, prog.dmaW

    GWT = al("f0_gwt", [128, 64], BF16)
    ACT_ = al("f0_acat", [128, 1024], BF16)
    BCT = al("f0_bct", [128, 4096], BF16)
    WSL = al("f0_wsl", [128, 2 * 1024], BF16)
    LGT = al("f0_lgt", [128, 128], F32)
    E8 = al("f0_e8", [128, 8], F32)
    C8 = al("f0_c8", [128, 8], F32)
    L2T = al("f0_l2", [128, 8], F32)
    G1 = al("f0_g1", [128, 8], F32)
    SRW = al("f0_srw", [128, 2], F32)
    JNK = al("f0_jnk", [128, 8], F32)
    SEL1 = al("f0_sel1", [128, 128], F32)
    SEL2 = al("f0_sel2", [128, 128], F32)
    SELT1 = al("f0_selt1", [8, 2048], BF16)
    SELT2 = al("f0_selt2", [8, 2048], BF16)
    VSA = al("f0_vsa", [128, 16], F32)
    VST = al("f0_vst", [16, 128], F32)
    VSB = al("f0_vsb", [128, 2048], F32)
    VROW = VSB[0:1, 0:2048]
    XA = al("f0_xa", [128, 2048], F32)
    LG = XA[0:8, 0:2048]
    XS1 = al("f0_xs1", [128, 2048], BF16)
    XS2 = al("f0_xs2", [128, 2048], BF16)
    H = al("f0_h", [128, 32 * 512], BF16)
    BASE = al("f0_base", [128, 512], BF16)
    T1 = al("f0_t1", [128, 512], F32)
    T2 = al("f0_t2", [128, 512], F32)
    R1 = al("f0_r1", [128, 512], BF16)
    R2 = al("f0_r2", [128, 512], BF16)

    load_fm(S, prog, GWT, "gw")
    load_fm(S, prog, ACT_, "acat")
    load_fm(S, prog, BCT, "bcat")
    wl = prog.dmaW.tok()
    bi_col = prog.small("bi")
    gate_b = prog.small("gate_b")

    sel_free = None
    e_vsb = None
    e_xa = None
    e_xm = None
    t1_war = None
    with contextlib.ExitStack() as pA:
        p_g = pA.enter_context(nc.psum_tensor("f0_pg", [8, 512], F32))
        p_lgt = pA.enter_context(nc.psum_tensor("f0_plgt", [128, 128], F32))
        p_sel = pA.enter_context(nc.psum_tensor("f0_psel", [8, 512], F32))
        p_vs = pA.enter_context(nc.psum_tensor("f0_pvs", [16, 128], F32))
        p_bc = pA.enter_context(nc.psum_tensor("f0_pbc", [128, 512], F32))
        p_m = pA.enter_context(nc.psum_tensor("f0_pm", [128, 512], F32))
        p_xa = pA.enter_context(nc.psum_tensor("f0_pxa", [128, 512], F32))
        g_free = None
        for tb in range(4):
            for dc in range(NDC):
                g_g = S.add("tensor", E_mm(p_g[0:8, :], GWT[:, dc * 8: dc * 8 + 8],
                                           XB[:, dc * T + tb * 512: dc * T + tb * 512 + 512],
                                           dc == 0, dc == NDC - 1),
                            waits=([wl, g_free] if dc == 0 else ()),
                            inc=(pe if dc == NDC - 1 else None))
            g_free = S.add("vector", E_ts(LG[0:8, tb * 512:tb * 512 + 512], p_g[0:8, :],
                                          gate_b[0:8, :], ALU.add),
                           waits=[g_g], fence=True)
        tr_toks = []
        for tc in range(NTC):
            g_t = S.add("tensor", E_tr(p_lgt[:, tc * 8: tc * 8 + 8],
                                       LG[0:8, tc * 128: tc * 128 + 128],
                                       prog.ID[0:8, 0:8]),
                        waits=([g_free] if tc == 0 else ()),
                        inc=(pe if tc == NTC - 1 else None))
        if L0F_G <= 1:
            return S
        e_lgt = S.add("vector", E_copy(LGT[:, :], p_lgt[:, :]), waits=[g_t], fence=True)
        if L0F_G <= 2:
            return S
        iota = prog.IOTA[:, 0:8]
        iotaB = prog.IOTA[:, 8:16]
        for tc in range(NTC):
            lgt = LGT[:, tc * 8: tc * 8 + 8]
            t1 = S.add("vector", E_red(G1[:, 0:1], lgt, ALU.max), fence=True)
            t2 = S.add("vector", E_ts(G1[:, 1:2], G1[:, 0:1], -1.0, ALU.mult), fence=True)
            t3 = S.add("vector", E_ts(E8[:, :], lgt, G1[:, 0:1], ALU.is_equal), fence=True)
            t4 = S.add("vector", E_stt(C8[:, :], E8[:, :], -BIG, iotaB, ALU.mult, ALU.add), fence=True)
            t5 = S.add("vector", E_red(G1[:, 2:3], C8[:, :], ALU.min), fence=True)
            t6 = S.add("vector", E_ts(SEL1[:, tc * 8: tc * 8 + 8], iota, G1[:, 2:3],
                                      ALU.is_equal), fence=True)
            t7 = S.add("vector", E_stt(L2T[:, :], SEL1[:, tc * 8: tc * 8 + 8], NEG, lgt,
                                       ALU.mult, ALU.add), fence=True)
            t8 = S.add("vector", E_red(G1[:, 3:4], L2T[:, :], ALU.max), fence=True)
            t9 = S.add("vector", E_ts(E8[:, :], L2T[:, :], G1[:, 3:4], ALU.is_equal), fence=True)
            ta = S.add("vector", E_stt(C8[:, :], E8[:, :], -BIG, iotaB, ALU.mult, ALU.add), fence=True)
            tb_ = S.add("vector", E_red(G1[:, 4:5], C8[:, :], ALU.min), fence=True)
            tcx = S.add("vector", E_ts(SEL2[:, tc * 8: tc * 8 + 8], iota, G1[:, 4:5],
                                       ALU.is_equal), fence=True)
            s1 = S.add("scalar", E_act(JNK[:, :], lgt, AF.Exp, bias=G1[:, 1:2],
                                       accum=SRW[:, 0:1]), waits=[t8], fence=True)
            s2 = S.add("scalar", E_act(G1[:, 5:6], G1[:, 3:4], AF.Exp, bias=G1[:, 1:2]),
                       fence=True)
            v1 = S.add("vector", E_recip(SRW[:, 1:2], SRW[:, 0:1]), waits=[s1], fence=True)
            v2 = S.add("vector", E_stt(VSA[:, tc:tc + 1], G1[:, 5:6], 1.0, SRW[:, 1:2],
                                       ALU.add, ALU.mult), waits=[s2], fence=True)
        if L0F_G <= 3:
            return S
        for src_t, dst in [(SEL1, SELT1), (SEL2, SELT2)]:
            for grp in range(4):
                for k in range(4):
                    tc = grp * 4 + k
                    g_t = S.add("tensor", E_tr(p_sel[0:8, k * 128: k * 128 + 128],
                                               src_t[:, tc * 8: tc * 8 + 8],
                                               prog.ID[:, :]),
                                waits=([v2, sel_free] if k == 0 else ()),
                                inc=(pe if k == 3 else None))
                sel_free = S.add("vector", E_copy(dst[0:8, grp * 512: grp * 512 + 512],
                                                  p_sel[0:8, :]),
                                 waits=[g_t], fence=True)
        if L0F_G <= 4:
            return S
        g_vt = S.add("tensor", E_tr(p_vs[0:16, :], VSA[:, :], prog.ID[:, :]),
                     waits=[v2], inc=pe)
        e_vt = S.add("vector", E_copy(VST[0:16, :], p_vs[0:16, :]), waits=[g_vt], fence=True)
        d1 = S.add("gpsimd", E_dma(prog.hbm["scr"], VST[0:16, :]), waits=[e_vt],
                   inc=dmaW, inck=16, dma=True)
        d1t = prog.dmaW.tok()
        S.add("gpsimd", E_dma(VROW[0:1, :], prog.hbm["scr"].rearrange("p n -> (p n)")[None, :]),
              waits=[d1t], inc=dmaW, inck=16, dma=True)
        vrow_tok = prog.dmaW.tok()
        if L0F_G <= 5:
            return S
        xs_toks = []
        for tb in range(4):
            g_b = S.add("tensor", E_mm(p_bc[:, :], prog.rows("ones", 128),
                                       VROW[0:1, tb * 512: tb * 512 + 512], True, True),
                        waits=[vrow_tok] + ([e_vsb] if tb > 0 else []), inc=pe)
            e_vsb = S.add("vector", E_copy(VSB[:, tb * 512: tb * 512 + 512], p_bc[:, :]),
                          waits=[g_b], fence=True)
            for dc in range(NDC):
                g_xa = S.add("tensor", E_mm(p_xa[:, :], ACT_[:, dc * 128: dc * 128 + 128],
                                            XB[:, dc * T + tb * 512: dc * T + tb * 512 + 512],
                                            dc == 0, dc == NDC - 1),
                             waits=(([wl] + ([e_xa] if e_xa else [])) if dc == 0 else ()),
                             inc=(pe if dc == NDC - 1 else None))
            e_xa = S.add("vector", E_copy(XA[:, tb * 512: tb * 512 + 512], p_xa[:, :]),
                         waits=[g_xa], fence=True)
            for selt, xs in [(SELT1, XS1), (SELT2, XS2)]:
                g_m = S.add("tensor", E_mm(p_m[:, :], prog.E16[0:8, :],
                                           selt[0:8, tb * 512: tb * 512 + 512], True, True),
                            waits=[sel_free, e_xm], inc=pe)
                e_xm = S.add("vector", E_tt(xs[:, tb * 512: tb * 512 + 512], p_m[:, :],
                                            XA[:, tb * 512: tb * 512 + 512], ALU.mult),
                             waits=[g_m, e_xa], fence=True)
            xs_toks.append(e_xm)

    if L0F_PART == 0:
        return S
    phA = S.fence_all()
    with contextlib.ExitStack() as pB:
        P8 = [pB.enter_context(nc.psum_tensor(f"f0_p8_{i}", [128, 512], F32))
              for i in range(8)]
        p_h, p_l = P8[0], P8[1]
        wi_hbm = prog.add_input("wi").rearrange("(c p) m -> p c m", p=128)
        wo_hbm = prog.add_input("wo2")
        h_free = None
        l_free = None
        r1_done = r2_done = hw_done = None
        o_free = []
        b_done = {}
        zt = None

        def wi_slab(fc, wait):
            sb = fc % 2
            S.add("gpsimd",
                  E_dma(WSL[:, sb * 1024:(sb + 1) * 1024].rearrange("p (c m) -> p c m", c=8),
                        wi_hbm[:, :, fc * 128:(fc + 1) * 128]),
                  waits=wait, inc=dmaW, inck=16, dma=True)
            return prog.dmaW.tok()

        def wo_slab(fc, wait):
            sb = fc % 2
            S.add("gpsimd",
                  E_dma(WSL[:, sb * 1024:(sb + 1) * 1024], wo_hbm[fc * 128:(fc + 1) * 128, :]),
                  waits=wait, inc=dmaW, inck=16, dma=True)
            return prog.dmaW.tok()

        ntg = 1 if L0F_PART in (1, 2, 3) else 4
        for tg in range(ntg):
            og = tg * 512
            slab_tok = {}
            mm_done = {}
            w0 = [xs_toks[-1]] + (phA if tg == 0 else [b_done[30], b_done[31]])
            slab_tok[0] = wi_slab(0, w0)
            slab_tok[1] = wi_slab(1, [])
            for fc in range(32):
                sb = fc % 2
                for dc in range(NDC):
                    g_h = S.add("tensor",
                                E_mm(p_h[:, :], WSL[:, sb * 1024 + dc * 128: sb * 1024 + dc * 128 + 128],
                                     XB[:, dc * T + og: dc * T + og + 512],
                                     dc == 0, dc == NDC - 1),
                                waits=([slab_tok[fc], h_free] + (o_free if fc == 0 else [])
                                       if dc == 0 else ()),
                                inc=(pe if dc == NDC - 1 else None))
                mm_done[fc] = g_h
                if fc + 2 < 32:
                    slab_tok[fc + 2] = wi_slab(fc + 2, [mm_done[fc]])
                e_b = S.add("scalar", E_act(BASE[:, :], p_h[:, :], AF.Identity,
                                            bias=bi_col[:, fc:fc + 1]),
                            waits=[g_h], fence=True)
                h_free = e_b
                g_l1 = S.add("tensor", E_mm(p_l[:, :], BCT[:, fc * 128: fc * 128 + 128],
                                            XS1[:, og: og + 512], True, True),
                             waits=[l_free] + (o_free if fc == 0 else []), inc=pe)
                v_t1 = S.add("vector", E_tt(T1[:, :], p_l[:, :], BASE[:, :], ALU.add),
                             waits=[g_l1, e_b, r1_done], fence=True)
                g_l2 = S.add("tensor", E_mm(p_l[:, :], BCT[:, fc * 128: fc * 128 + 128],
                                            XS2[:, og: og + 512], True, True),
                             waits=[v_t1], inc=pe)
                r1_done = S.add("scalar", E_act(R1[:, :], T1[:, :], AF.Relu),
                                waits=[v_t1, hw_done], fence=True)
                v_t2 = S.add("vector", E_tt(T2[:, :], p_l[:, :], BASE[:, :], ALU.add),
                             waits=[g_l2, r2_done], fence=True)
                l_free = v_t2
                r2_done = S.add("scalar", E_act(R2[:, :], T2[:, :], AF.Relu),
                                waits=[v_t2], fence=True)
                hw_done = S.add("vector", E_tt(H[:, fc * 512: fc * 512 + 512], R1[:, :],
                                               R2[:, :], ALU.add),
                                waits=[r2_done], fence=True)
            if L0F_PART == 1:
                return S

            slab_tok = {}
            b_done = {}
            slab_tok[0] = wo_slab(0, [mm_done[30], mm_done[31]])
            slab_tok[1] = wo_slab(1, [])
            for fc in range(32):
                sb = fc % 2
                for dc in range(NDC):
                    w = []
                    if dc == 0:
                        w = [slab_tok[fc]]
                        if fc == 0:
                            w += [hw_done] + o_free
                    g_o = S.add("tensor",
                                E_mm(P8[dc][:, :],
                                     WSL[:, sb * 1024 + dc * 128: sb * 1024 + dc * 128 + 128],
                                     H[:, fc * 512: fc * 512 + 512],
                                     fc == 0, False),
                                waits=w, inc=(pe if dc == NDC - 1 else None))
                b_done[fc] = g_o
                if fc + 2 < 32:
                    slab_tok[fc + 2] = wo_slab(fc + 2, [b_done[fc]])
            o_free = []
            for dc in range(NDC):
                S.add("tensor",
                      E_mm(P8[dc][:, :], prog.rows("bo", 1024)[0:1, dc * 128: dc * 128 + 128],
                           VROW[0:1, og: og + 512], False, False),
                      waits=([b_done[31]] if dc == 0 else ()))
                g_bo = S.add("tensor",
                             E_mm(P8[dc][:, :], prog.rows("bo", 1024)[0:1, dc * 128: dc * 128 + 128],
                                  VROW[0:1, og: og + 512], False, True),
                             inc=pe)
                v_m = S.add("vector", E_tt(T1[:, :], P8[dc][:, :],
                                           VSB[:, og: og + 512], ALU.mult),
                            waits=[g_bo, t1_war], fence=True)
                zt = S.add("vector", E_tt(X[:, dc * T + og: dc * T + og + 512], T1[:, :],
                                          X[:, dc * T + og: dc * T + og + 512], ALU.add),
                           fence=True)
                o_free.append(zt)
                t1_war = zt
    if L0F_PART == 2:
        return S
    with contextlib.ExitStack() as pLN:
        lnb = dict(
            SQ=pLN.enter_context(nc.sbuf_tensor("f0_lnsq", [128, 1024], F32)),
            LNR=pLN.enter_context(nc.sbuf_tensor("f0_lnr", [33, 512], F32)),
            LNR2=pLN.enter_context(nc.sbuf_tensor("f0_lnr2", [33, 512], F32)),
            p_st=pLN.enter_context(nc.psum_tensor("f0_pst", [1, 1024], F32)),
            p_rb=pLN.enter_context(nc.psum_tensor("f0_prb", [128, 512], F32)),
            p_mb=pLN.enter_context(nc.psum_tensor("f0_pmb", [128, 512], F32)),
        )
        emit_ln(S, prog, lnb, zt)
    return S


def stage_l1_attn(prog, sctx):
    nc = prog.nc
    S = Script(prog)
    X, XB = prog.X, prog.XB
    al = lambda name, shape, dt: sctx.enter_context(nc.sbuf_tensor(name, shape, dt))
    pe, act, dve, dmaW = prog.pe, prog.act, prog.dve, prog.dmaW

    WSL = al("a1_wsl", [128, 2 * 1024], BF16)
    COS = al("a1_cos", [128, 2048], BF16)
    SIN = al("a1_sin", [128, 2048], BF16)
    QSF = al("a1_qsf", [128, 2048], F32)
    ROT = al("a1_rot", [128, 2048], F32)
    TA = al("a1_ta", [128, 2048], F32)
    Q1 = al("a1_q1", [128, 2048], BF16)
    K1 = al("a1_k1", [128, 2048], BF16)
    VB2 = al("a1_vb", [128, 2048], BF16)
    VT2 = al("a1_vt", [128, 2048], BF16)
    Y = al("a1_y", [128, 8 * 2048], BF16)
    EB = al("a1_eb", [128, 2048], BF16)
    PT = al("a1_pt", [128, 2048], BF16)
    MSK = al("a1_msk", [128, 4 * 512], BF16)
    NM = al("a1_nm", [128, 4], F32)
    SR = al("a1_sr", [128, 4], F32)

    qkv_hbm = prog.add_input("wqkv").rearrange("(c p) m -> p c m", p=128)
    S.add("gpsimd", E_dma(COS[:, :], prog.add_input("cos128")), inc=dmaW, inck=16, dma=True)
    S.add("gpsimd", E_dma(SIN[:, :], prog.add_input("sin128")), inc=dmaW, inck=16, dma=True)
    mask_ap = prog.add_input("masks")
    S.add("gpsimd", E_dma(MSK[:, :].rearrange("p (v q) -> p v q", v=4),
                          mask_ap.rearrange("v p q -> p v q")), inc=dmaW, inck=16, dma=True)
    wl = prog.dmaW.tok()

    with contextlib.ExitStack() as pA:
        p_a = pA.enter_context(nc.psum_tensor("a1_pa", [128, 512], F32))
        p_t = pA.enter_context(nc.psum_tensor("a1_pt2", [128, 1024], BF16))
        p_s = pA.enter_context(nc.psum_tensor("a1_ps", [128, 2048], F32))
        p_pv = pA.enter_context(nc.psum_tensor("a1_ppv", [64, 512], F32))

        a_free = None
        t_free = None
        s_free = None
        t2_free = None
        pv_free = None
        slab_reader = {0: None, 1: None}
        qsf_free = None
        for c in range(8):
            outs = {}
            rope_tok = {}
            for kind, col0 in [("q", c * 128), ("k", 1024 + c * 128), ("v", 2048 + c * 128)]:
                sb = 0 if kind != "v" else 1
                S.add("gpsimd",
                      E_dma(WSL[:, sb * 1024: sb * 1024 + 1024]
                            .rearrange("p (a m) -> p a m", a=8),
                            qkv_hbm[:, :, col0:col0 + 128]),
                      waits=[wl] + ([slab_reader[sb]] if slab_reader[sb] else []),
                      inc=dmaW, inck=16, dma=True)
                wtok = prog.dmaW.tok()
                for tb in range(4):
                    for dc in range(NDC):
                        g_p = S.add("tensor",
                                    E_mm(p_a[:, :],
                                         WSL[:, sb * 1024 + dc * 128: sb * 1024 + dc * 128 + 128],
                                         XB[:, dc * T + tb * 512: dc * T + tb * 512 + 512],
                                         dc == 0, dc == NDC - 1),
                                    waits=([wtok, a_free] if dc == 0 else ()),
                                    inc=(pe if dc == NDC - 1 else None))
                    if kind == "v":
                        ev = S.add("vector", E_copy(VB2[:, tb * 512: tb * 512 + 512], p_a[:, :]),
                                   waits=[g_p], fence=True)
                    else:
                        ev = S.add("scalar", E_act(QSF[:, tb * 512: tb * 512 + 512], p_a[:, :],
                                                   AF.Identity),
                                   waits=[g_p] + ([qsf_free] if tb == 0 else []), inc=act)
                    a_free = ev
                    outs.setdefault(kind, []).append(ev)
                slab_reader[sb] = g_p
                if kind == "v":
                    continue
                dst = Q1 if kind == "q" else K1
                last_ev = outs[kind][-1]
                S.add("vector", E_copy(ROT[0:32, :], QSF[32:64, :]), waits=[last_ev])
                S.add("vector", E_copy(ROT[32:64, :], QSF[0:32, :]))
                S.add("vector", E_copy(ROT[64:96, :], QSF[96:128, :]))
                S.add("vector", E_copy(ROT[96:128, :], QSF[64:96, :]), fence=True)
                S.add("vector", E_tt(TA[:, :], ROT[:, :], SIN[:, :], ALU.mult),
                      waits=[wl], fence=True)
                qsf_free = S.add("vector", E_tt(ROT[:, :], QSF[:, :], COS[:, :], ALU.mult),
                                 fence=True)
                rope_tok[kind] = S.add("vector", E_tt(dst[:, :], TA[:, :], ROT[:, :], ALU.add),
                                       fence=True)
            rope_done = rope_tok["k"]
            vdone = outs["v"][-1] if "v" in outs else None
            for grp in range(2):
                for k in range(8):
                    sc8 = grp * 8 + k
                    g_t = S.add("tensor", E_tr(p_t[:, k * 128:k * 128 + 128],
                                               VB2[:, sc8 * 128:sc8 * 128 + 128],
                                               prog.IDB[:, :]),
                                waits=([vdone, t_free] if k == 0 else ()),
                                inc=(pe if k == 7 else None))
                ev = S.add("vector",
                           E_copy(VT2[:, grp * 1024: grp * 1024 + 1024], p_t[:, :]),
                           waits=[g_t], fence=True)
                t_free = ev
            vt_done = t_free
            if DEBUG and c == 0:
                for i, ap in enumerate([Q1[:, 0:2048], K1[:, 0:2048], VT2[:, 0:2048]]):
                    S.add("gpsimd", E_dma(prog.hbm[f"dbg{i}"], ap),
                          waits=[rope_done, vt_done], inc=dmaW, inck=16, dma=True)
            for hh in range(2):
                b = 64 * hh
                for tqc in range(NTC):
                    nsg = tqc // 4 + 1
                    nsc = tqc + 1
                    v = tqc % 4
                    for sg in range(nsg):
                        w = []
                        if sg == 0:
                            w = [s_free, rope_done, vt_done]
                        g_s = S.add("tensor",
                                    E_mm(p_s[:, sg * 512:sg * 512 + 512],
                                         Q1[b:b + 64, tqc * 128: tqc * 128 + 128],
                                         K1[b:b + 64, sg * 512: sg * 512 + 512],
                                         True, True, tp=(b, 0)),
                                    waits=w, inc=(pe if sg == nsg - 1 else None))
                    t_msk = S.add("vector", E_tt(p_s[:, (nsg - 1) * 512:(nsg - 1) * 512 + 512],
                                                 p_s[:, (nsg - 1) * 512:(nsg - 1) * 512 + 512],
                                                 MSK[:, v * 512:v * 512 + 512], ALU.add),
                                  waits=[g_s], fence=True)
                    t_nm = S.add("vector", E_red(NM[:, 0:1], p_s[:, 0:nsg * 512], ALU.max,
                                                 negate=True), fence=True)
                    t_nmm = S.add("vector", E_ts(NM[:, 1:2], NM[:, 0:1], 0.125, ALU.mult),
                                  inc=dve, fence=True)
                    e_e = S.add("scalar", E_act(EB[:, 0:nsg * 512], p_s[:, 0:nsg * 512],
                                                AF.Exp, bias=NM[:, 1:2], scale=0.125,
                                                accum=SR[:, 0:1]),
                                waits=[t_nmm], inc=act, fence=True)
                    s_free = e_e
                    t_rs = S.add("vector", E_recip(SR[:, 1:2], SR[:, 0:1]),
                                 waits=[e_e], fence=True)
                    t_nrm = S.add("vector", E_ts(EB[:, 0:nsg * 512], EB[:, 0:nsg * 512],
                                                 SR[:, 1:2], ALU.mult),
                                  inc=dve, fence=True)
                    ptok = []
                    for grp in range((nsc + 7) // 8):
                        kmax = min(8, nsc - grp * 8)
                        for k in range(kmax):
                            sc8 = grp * 8 + k
                            g_t = S.add("tensor", E_tr(p_t[:, k * 128:k * 128 + 128],
                                                       EB[:, sc8 * 128:sc8 * 128 + 128],
                                                       prog.IDB[:, :]),
                                        waits=([t_nrm, t_free] if k == 0 else ()),
                                        inc=(pe if k == kmax - 1 else None))
                        ev = S.add("vector",
                                   E_copy(PT[:, grp * 8 * 128: grp * 8 * 128 + kmax * 128],
                                          p_t[:, 0:kmax * 128]),
                                   waits=[g_t], fence=True)
                        t_free = ev
                        ptok.append(ev)
                    for sc8 in range(nsc):
                        w = []
                        if sc8 == 0:
                            w = [ptok[-1], pv_free]
                        g_pv = S.add("tensor",
                                     E_mm(p_pv[0:64, 0:128],
                                          VT2[:, sc8 * 128 + b: sc8 * 128 + b + 64],
                                          PT[:, sc8 * 128:sc8 * 128 + 128],
                                          sc8 == 0, sc8 == nsc - 1),
                                     waits=w, inc=(pe if sc8 == nsc - 1 else None))
                    pv_free = S.add("vector",
                                    E_copy(Y[b:b + 64,
                                             c * 2048 + tqc * 128: c * 2048 + tqc * 128 + 128],
                                           p_pv[0:64, 0:128]),
                                    waits=[g_pv], fence=True)

    ph2 = S.fence_all()
    prj_hbm = prog.add_input("wproj").rearrange("(c p) m -> p c m", p=128)
    with contextlib.ExitStack() as p3:
        p_o = p3.enter_context(nc.psum_tensor("a1_po", [128, 512], F32))
        lnb = dict(
            SQ=QSF[:, 0:1024],
            LNR=p3.enter_context(nc.sbuf_tensor("a1_lnr", [33, 512], F32)),
            LNR2=p3.enter_context(nc.sbuf_tensor("a1_lnr2", [33, 512], F32)),
            p_st=p3.enter_context(nc.psum_tensor("a1_pst", [1, 1024], F32)),
            p_rb=p3.enter_context(nc.psum_tensor("a1_prb", [128, 512], F32)),
            p_mb=p3.enter_context(nc.psum_tensor("a1_pmb", [128, 512], F32)),
        )
        o_free = None
        zt = None
        slab = {}
        prj_done = {}

        def prj_slab(dc, wait):
            sb = dc % 2
            S.add("gpsimd", E_dma(WSL[:, sb * 1024: sb * 1024 + 1024]
                                  .rearrange("p (a m) -> p a m", a=8),
                                  prj_hbm[:, :, dc * 128: dc * 128 + 128]),
                  waits=wait, inc=dmaW, inck=16, dma=True)
            return prog.dmaW.tok()

        slab[0] = prj_slab(0, ph2)
        slab[1] = prj_slab(1, [])
        for dc in range(NDC):
            for tb in range(4):
                for yc in range(NDC):
                    w = []
                    if yc == 0:
                        w = [slab[dc]] + ([o_free] if o_free else [])
                        if dc == 0 and tb == 0:
                            w += ph2
                    g_o = S.add("tensor",
                                E_mm(p_o[:, :],
                                     WSL[:, (dc % 2) * 1024 + yc * 128: (dc % 2) * 1024 + yc * 128 + 128],
                                     Y[:, yc * 2048 + tb * 512: yc * 2048 + tb * 512 + 512],
                                     yc == 0, yc == NDC - 1),
                                waits=w, inc=(pe if yc == NDC - 1 else None))
                prj_done[dc] = g_o
                zt = S.add("vector", E_tt(X[:, dc * T + tb * 512: dc * T + tb * 512 + 512],
                                          p_o[:, :],
                                          X[:, dc * T + tb * 512: dc * T + tb * 512 + 512],
                                          ALU.add),
                           waits=[g_o], fence=True)
                o_free = zt
            if dc + 2 < NDC:
                slab[dc + 2] = prj_slab(dc + 2, [prj_done[dc]])
        emit_ln(S, prog, lnb, zt)
    return S


def stage_l1_ffn(prog, sctx):
    nc = prog.nc
    S = Script(prog)
    X, XB = prog.X, prog.XB
    al = lambda name, shape, dt: sctx.enter_context(nc.sbuf_tensor(name, shape, dt))
    pe, act, dve, dmaW = prog.pe, prog.act, prog.dve, prog.dmaW

    AFCT = al("f1_afct", [128, 128], BF16)
    APRT = al("f1_aprt", [128, 512], BF16)
    BFCT = al("f1_bfct", [16, 4096], BF16)
    BPRT = al("f1_bprt", [16, 1024], BF16)
    SCL = al("f1_scl", [128, 8], F32)
    AM1 = al("f1_am1", [1, 4], F32)
    load_fm(S, prog, AFCT, "afc")
    load_fm(S, prog, APRT, "apr")
    load_fm(S, prog, BFCT, "bfcT")
    load_fm(S, prog, BPRT, "bprT")
    wl = prog.dmaW.tok()

    with contextlib.ExitStack() as pQ:
        WF = pQ.enter_context(nc.sbuf_tensor("f1_wf", [128, 2048], F32))
        WA = pQ.enter_context(nc.sbuf_tensor("f1_wa", [128, 2048], F32))
        WB = pQ.enter_context(nc.sbuf_tensor("f1_wb", [128, 2048], F32))
        WC = pQ.enter_context(nc.sbuf_tensor("f1_wc", [128, 2048], F32))
        WI = pQ.enter_context(nc.sbuf_tensor("f1_wi", [128, 2048], I32))
        WD = pQ.enter_context(nc.sbuf_tensor("f1_wd", [128, 2048], BF16))
        AMX = pQ.enter_context(nc.sbuf_tensor("f1_amx", [128, 2], F32))
        p_b1 = pQ.enter_context(nc.psum_tensor("f1_pb1", [128, 512], F32))
        for wi_, (wname, scr, scol) in enumerate([("wfc", "wq_fc", 0), ("wpr", "wq_pr", 3)]):
            w_hbm = prog.add_input(wname).rearrange("(c p) m -> p (c m)", p=128) \
                if False else prog.add_input(wname)
            kt, mt = w_hbm.shape
            nk = kt // 128
            flat = w_hbm.rearrange("(c p) m -> p c m", p=128)
            nslab = nk * mt // 2048

            def slab_ap(s):
                if mt >= 2048:
                    per = mt // 2048
                    return flat[:, s // per: s // per + 1,
                                (s % per) * 2048:(s % per) * 2048 + 2048]
                g = 2048 // mt
                return flat[:, s * g:(s + 1) * g, :]
            am_tok = None
            ld_tok = None
            for s in range(nslab):
                S.add("gpsimd", E_dma(WF[:, :].rearrange("p (a b) -> p a b",
                                                         a=1 if mt >= 2048 else 2048 // mt),
                                      slab_ap(s)),
                      waits=[ld_tok], inc=dmaW, inck=16, dma=True)
                ld_tok0 = prog.dmaW.tok()
                rd = S.add("vector", E_red(AMX[:, 0:1], WF[:, :], ALU.max, absval=True),
                           waits=[ld_tok0], fence=True)
                ld_tok = rd
                if s == 0:
                    am_tok = S.add("vector", E_copy(AMX[:, 1:2], AMX[:, 0:1]), fence=True)
                else:
                    am_tok = S.add("vector", E_tt(AMX[:, 1:2], AMX[:, 1:2], AMX[:, 0:1],
                                                  ALU.max), fence=True)
            pm = S.add("gpsimd", E_redC(AM1[0:1, 0:1], AMX[:, 1:2], ALU.max),
                       waits=[am_tok], fence=True)
            sc1 = S.add("vector", E_ts(AM1[0:1, 1:2], AM1[0:1, 0:1], 1.0 / 127.0, ALU.mult),
                        waits=[pm], fence=True)
            rc1 = S.add("vector", E_recip(AM1[0:1, 2:3], AM1[0:1, 1:2]), fence=True)
            g_b1 = S.add("tensor", E_mm(p_b1[:, 0:1], prog.rows("ones", 128),
                                        AM1[0:1, 2:3], True, True), waits=[rc1], inc=pe)
            cb1 = S.add("vector", E_copy(SCL[:, scol:scol + 1], p_b1[:, 0:1]),
                        waits=[g_b1], fence=True)
            g_b2 = S.add("tensor", E_mm(p_b1[:, 0:1], prog.rows("ones", 128),
                                        AM1[0:1, 1:2], True, True), waits=[cb1], inc=pe)
            cb2 = S.add("vector", E_copy(SCL[:, scol + 1:scol + 2], p_b1[:, 0:1]),
                        waits=[g_b2], fence=True)
            t2 = S.add("vector", E_ts(SCL[:, scol + 2:scol + 3], SCL[:, scol:scol + 1],
                                      2.0, ALU.mult), fence=True)
            scr_ap = prog.hbm[scr]
            st_tok = None
            for s in range(nslab):
                S.add("gpsimd", E_dma(WF[:, :].rearrange("p (a b) -> p a b",
                                                         a=1 if mt >= 2048 else 2048 // mt),
                                      slab_ap(s)),
                      waits=[ld_tok], inc=dmaW, inck=16, dma=True)
                ld_tok0 = prog.dmaW.tok()
                q1 = S.add("vector", E_ts(WA[:, :], WF[:, :], SCL[:, scol:scol + 1],
                                          ALU.mult), waits=[ld_tok0, t2, st_tok], fence=True)
                ld_tok = q1
                S.add("vector", E_copy(WI[:, :], WA[:, :]), fence=True)
                S.add("vector", E_copy(WB[:, :], WI[:, :]), fence=True)
                S.add("vector", E_tt(WC[:, :], WB[:, :], WA[:, :], ALU.is_gt), fence=True)
                S.add("vector", E_tt(WB[:, :], WB[:, :], WC[:, :], ALU.subtract), fence=True)
                S.add("vector", E_tt(WC[:, :], WA[:, :], WB[:, :], ALU.subtract), fence=True)
                qf = S.add("vector", E_stt(WD[:, :], WC[:, :], 0.5, WB[:, :],
                                           ALU.is_ge, ALU.add), fence=True)
                S.add("gpsimd", E_dma(scr_ap[:, s * 2048:(s + 1) * 2048], WD[:, :]),
                      waits=[qf], inc=dmaW, inck=16, dma=True)
                st_tok = prog.dmaW.tok()

    phQ = S.fence_all()
    with contextlib.ExitStack() as pF:
        WSL = pF.enter_context(nc.sbuf_tensor("f1_wsl", [128, 2 * 1024], BF16))
        WSL2 = pF.enter_context(nc.sbuf_tensor("f1_wsl2", [128, 2 * 4096], BF16))
        H = pF.enter_context(nc.sbuf_tensor("f1_h", [128, 32 * 512], BF16))
        G1 = pF.enter_context(nc.sbuf_tensor("f1_g1", [128, 512], BF16))
        XA1 = pF.enter_context(nc.sbuf_tensor("f1_xa1", [16, 512], BF16))
        XA2 = pF.enter_context(nc.sbuf_tensor("f1_xa2", [16, 512], BF16))
        TZ = pF.enter_context(nc.sbuf_tensor("f1_tz", [128, 512], F32))
        p_h = pF.enter_context(nc.psum_tensor("f1_ph", [128, 512], F32))
        p_x = pF.enter_context(nc.psum_tensor("f1_px", [16, 512], F32))
        p_o = pF.enter_context(nc.psum_tensor("f1_po", [128, 512], F32))
        lnb = dict(
            SQ=pF.enter_context(nc.sbuf_tensor("f1_lnsq", [128, 1024], F32)),
            LNR=pF.enter_context(nc.sbuf_tensor("f1_lnr", [33, 512], F32)),
            LNR2=pF.enter_context(nc.sbuf_tensor("f1_lnr2", [33, 512], F32)),
            p_st=pF.enter_context(nc.psum_tensor("f1_pst", [1, 1024], F32)),
            p_rb=pF.enter_context(nc.psum_tensor("f1_prb", [128, 512], F32)),
            p_mb=pF.enter_context(nc.psum_tensor("f1_pmb", [128, 512], F32)),
        )
        bfc_b = prog.small("bfc_b")
        bpr_b = prog.small("bpr_b")
        fcs = prog.hbm["wq_fc"].rearrange("p (c m) -> p c m", c=8)
        prs = prog.hbm["wq_pr"].rearrange("p (c m) -> p c m", c=32)
        h_free = None
        x_free = None
        o_free = None
        g_free = None
        zt = None
        prev_mm = {30: None, 31: None}
        prj_done = {6: None, 7: None}

        def fc_slab(fc, wait):
            sb = fc % 2
            S.add("gpsimd", E_dma(WSL[:, sb * 1024: sb * 1024 + 1024]
                                  .rearrange("p (a m) -> p a m", a=8),
                                  fcs[:, :, fc * 128: fc * 128 + 128]),
                  waits=wait, inc=dmaW, inck=16, dma=True)
            return prog.dmaW.tok()

        def pr_slab(dc, wait):
            sb = dc % 2
            S.add("gpsimd", E_dma(WSL2[:, sb * 4096: sb * 4096 + 4096]
                                  .rearrange("p (a m) -> p a m", a=32),
                                  prs[:, :, dc * 128: dc * 128 + 128]),
                  waits=wait, inc=dmaW, inck=16, dma=True)
            return prog.dmaW.tok()

        for tg in range(4):
            og = tg * 512
            for dc in range(NDC):
                g_x = S.add("tensor", E_mm(p_x[0:16, :], AFCT[:, dc * 16: dc * 16 + 16],
                                           XB[:, dc * T + og: dc * T + og + 512],
                                           dc == 0, dc == NDC - 1),
                            waits=([wl, x_free] + (phQ if tg == 0 else []) if dc == 0 else ()),
                            inc=(pe if dc == NDC - 1 else None))
            e_x1 = S.add("vector", E_ts(XA1[0:16, :], p_x[0:16, :], SCL[0:16, 2:3],
                                        ALU.mult), waits=[g_x], fence=True)
            x_free = e_x1
            slab = {}
            mm_done = {}
            slab[0] = fc_slab(0, (phQ if tg == 0 else [prev_mm[30], prev_mm[31]]))
            slab[1] = fc_slab(1, [])
            for fc in range(32):
                sb = fc % 2
                for dc in range(NDC):
                    g_h = S.add("tensor",
                                E_mm(p_h[:, :],
                                     WSL[:, sb * 1024 + dc * 128: sb * 1024 + dc * 128 + 128],
                                     XB[:, dc * T + og: dc * T + og + 512],
                                     dc == 0, False),
                                waits=([slab[fc], h_free] if dc == 0 else ()))
                g_h = S.add("tensor", E_mm(p_h[:, :], BFCT[0:16, fc * 128: fc * 128 + 128],
                                           XA1[0:16, :], False, True),
                            waits=[e_x1], inc=pe)
                mm_done[fc] = g_h
                if fc + 2 < 32:
                    slab[fc + 2] = fc_slab(fc + 2, [mm_done[fc]])
                e_g = S.add("scalar", E_act(G1[:, :], p_h[:, :], AF.Gelu,
                                            bias=bfc_b[:, fc:fc + 1], scale=SCL[:, 1:2]),
                            waits=[g_h, g_free], fence=True)
                h_free = e_g
                g_free = S.add("vector", E_copy(H[:, fc * 512: fc * 512 + 512], G1[:, :]),
                               waits=[e_g], fence=True)
            prev_mm = mm_done
            for fc in range(32):
                g_x2 = S.add("tensor", E_mm(p_x[0:16, :], APRT[:, fc * 16: fc * 16 + 16],
                                            H[:, fc * 512: fc * 512 + 512],
                                            fc == 0, fc == 31),
                             waits=([g_free, e_x1] if fc == 0 else ()),
                             inc=(pe if fc == 31 else None))
            e_x2 = S.add("vector", E_ts(XA2[0:16, :], p_x[0:16, :], SCL[0:16, 5:6],
                                        ALU.mult), waits=[g_x2], fence=True)
            x_free = e_x2
            slab = {}
            slab[0] = pr_slab(0, [mm_done[30], mm_done[31]] +
                              ([prj_done[6], prj_done[7]] if tg > 0 else []))
            slab[1] = pr_slab(1, [])
            for dc in range(NDC):
                sb = dc % 2
                for fc in range(32):
                    g_o = S.add("tensor",
                                E_mm(p_o[:, :],
                                     WSL2[:, sb * 4096 + fc * 128: sb * 4096 + fc * 128 + 128],
                                     H[:, fc * 512: fc * 512 + 512],
                                     fc == 0, False),
                                waits=([slab[dc], o_free, g_free] if fc == 0 else ()))
                g_o = S.add("tensor", E_mm(p_o[:, :], BPRT[0:16, dc * 128: dc * 128 + 128],
                                           XA2[0:16, :], False, True),
                            waits=[e_x2], inc=pe)
                prj_done[dc] = g_o
                if dc + 2 < NDC:
                    slab[dc + 2] = pr_slab(dc + 2, [prj_done[dc]])
                e_z = S.add("scalar", E_act(TZ[:, :], p_o[:, :], AF.Identity,
                                            bias=bpr_b[:, dc:dc + 1], scale=SCL[:, 4:5]),
                            waits=[g_o, zt], fence=True)
                o_free = e_z
                zt = S.add("vector", E_tt(X[:, dc * T + og: dc * T + og + 512], TZ[:, :],
                                          X[:, dc * T + og: dc * T + og + 512], ALU.add),
                           waits=[e_z], fence=True)
        emit_ln(S, prog, lnb, zt, refresh_xb=False)
    return S


def stage_lntest(prog, sctx):
    nc = prog.nc
    S = Script(prog)
    lnb = dict(
        SQ=sctx.enter_context(nc.sbuf_tensor("lt_sq", [128, 1024], F32)),
        LNR=sctx.enter_context(nc.sbuf_tensor("lt_lnr", [33, 512], F32)),
        LNR2=sctx.enter_context(nc.sbuf_tensor("lt_lnr2", [33, 512], F32)),
        p_st=sctx.enter_context(nc.psum_tensor("lt_pst", [1, 1024], F32)),
        p_rb=sctx.enter_context(nc.psum_tensor("lt_prb", [128, 512], F32)),
        p_mb=sctx.enter_context(nc.psum_tensor("lt_pmb", [128, 512], F32)),
    )
    dbg = None
    if DEBUG:
        dbg = sctx.enter_context(nc.sbuf_tensor("lt_dbg", [65, 2048], F32))
    emit_ln(S, prog, lnb, None, dbg=dbg)
    if DEBUG:
        S.add("gpsimd", E_dma(prog.hbm["dbg0"][0:65, :], dbg[0:65, :]),
              waits=[prog.dve.tok()] + [prog.bmp[e].tok() for e in ENGS], inc=prog.dmaW, inck=16, dma=True)
    return S



STAGE_BUILDERS = {}


def _register():
    STAGE_BUILDERS["l0a"] = stage_l0_attn
    STAGE_BUILDERS["l0f"] = stage_l0_ffn
    STAGE_BUILDERS["l1a"] = stage_l1_attn
    STAGE_BUILDERS["l1f"] = stage_l1_ffn
    STAGE_BUILDERS["lntest"] = stage_lntest
    for nm, fn in [("l0f", "stage_l0_ffn"), ("l1a", "stage_l1_attn"),
                   ("l1f", "stage_l1_ffn")]:
        if fn in globals():
            STAGE_BUILDERS[nm] = globals()[fn]


def build_group(meta, stage_names):
    prog = Prog(meta, [])
    _register()
    run_stage(prog, stage_load(prog))
    for nm in stage_names:
        with contextlib.ExitStack() as sctx:
            s = STAGE_BUILDERS[nm](prog, sctx)
            run_stage(prog, s)
    run_stage(prog, stage_store(prog))
    prog.ctx.close()
    return prog


def run_group(prog, meta, xs):
    in_maps = []
    for c in range(NCORES):
        im = {}
        for n in prog.input_names:
            im[n] = xs[c] if n == "x_in" else meta[n]
        in_maps.append(im)
    res = run_bass_kernel_spmd(prog.nc, in_maps, core_ids=list(range(NCORES)))
    return [np.ascontiguousarray(np.asarray(res.results[c]["x_out"], np.float32))
            for c in range(NCORES)]


GROUPS = [["l0a", "l0f", "l1a", "l1f"]]


def kernel(**inputs):
    meta = host_prepare(inputs)
    x = np.asarray(inputs["L0_attn_x"], np.float32)
    xs = [np.ascontiguousarray(x[c].T) for c in range(NCORES)]
    for g in GROUPS:
        prog = build_group(meta, g)
        xs = run_group(prog, meta, xs)
    return np.stack([c.T for c in xs]).astype(np.float32)


# version 03
# speedup vs baseline: 1.0346x; 1.0346x over previous
import contextlib
import numpy as np
import ml_dtypes
import concourse.bass as bass
import concourse.mybir as mybir
from concourse.bass_utils import run_bass_kernel_spmd

F32 = mybir.dt.float32
BF16 = mybir.dt.bfloat16
I32 = mybir.dt.int32
AF = mybir.ActivationFunctionType
ALU = mybir.AluOpType
AX = mybir.AxisListType

NCORES = 8
DEBUG = False
L0F_PART = 99
L0F_G = 99
T = 2048
D = 1024
NTB = T // 512
NTC = T // 128
NDC = D // 128
BIG = 1.0e4
NEG = -1.0e30
EPS_RMS = 1e-6
EPS_LN = 1e-5



def _perm_l0_q():
    cols = []
    for c in range(4):
        for j in range(4):
            h = 4 * c + j
            cols.extend(h * 64 + r for r in range(32))
    for c in range(4):
        for j in range(4):
            h = 4 * c + j
            cols.extend(h * 64 + 32 + r for r in range(32))
    return np.array(cols)


def _perm_l1_qk():
    cols = []
    for c in range(8):
        for hh in (2 * c, 2 * c + 1):
            cols.extend(hh * 64 + 2 * i for i in range(32))
            cols.extend(hh * 64 + 2 * i + 1 for i in range(32))
    return np.array(cols)


def _rope_tables():
    dh = 64
    inv = 1.0 / (10000.0 ** (np.arange(0, dh, 2, dtype=np.float64) / dh))
    tt = np.arange(T, dtype=np.float64)[:, None] * inv[None, :]
    cos = np.cos(tt).astype(np.float32)
    sin = np.sin(tt).astype(np.float32)
    cos128 = np.tile(np.ascontiguousarray(cos.T), (4, 1))
    sin128 = np.tile(np.ascontiguousarray(sin.T), (4, 1))
    sign = np.ones((128, 1), np.float32)
    sign[0:32] = -1.0
    sign[64:96] = -1.0
    return np.ascontiguousarray(cos128), np.ascontiguousarray(sin128 * sign)


def _masks():
    out = np.zeros((4, 128, 512), np.float32)
    for v in range(4):
        for p in range(128):
            out[v, p, v * 128 + p + 1:] = NEG
    return out


def host_prepare(inp):
    f32 = lambda a: np.ascontiguousarray(np.asarray(a, dtype=np.float32))
    m = {}
    m["wq"] = f32(np.asarray(inp["L0_attn_Wq"])[:, _perm_l0_q()])
    m["wkv"] = f32(inp["L0_attn_Wkv"])
    m["wk"] = f32(inp["L0_attn_Wk"])
    m["wv"] = f32(inp["L0_attn_Wv"])
    perm_o = np.array([h * 64 + c for h in range(16) for c in range(32)])
    m["wo"] = f32(np.asarray(inp["L0_attn_Wo"])[perm_o, :])
    m["wi"] = f32(np.asarray(inp["L0_ffn_wi"]).T)
    m["wo2"] = f32(np.asarray(inp["L0_ffn_wo"]).T)
    m["acat"] = f32(np.asarray(inp["L0_ffn_lora_A"]).reshape(128, 1024).T)
    m["bcat"] = f32(np.asarray(inp["L0_ffn_lora_B"]).transpose(0, 2, 1).reshape(128, 4096))
    m["gw"] = f32(np.asarray(inp["L0_ffn_gate_w"]).T)
    wqkv = np.asarray(inp["L1_attn_Wqkv"])
    pq = _perm_l1_qk()
    m["wqkv"] = f32(np.concatenate(
        [wqkv[:, 0:1024][:, pq], wqkv[:, 1024:2048][:, pq], wqkv[:, 2048:3072]], axis=1))
    m["wproj"] = f32(inp["L1_attn_Wproj"])
    m["wfc"] = f32(np.asarray(inp["L1_ffn_W_fc"]).T)
    m["wpr"] = f32(np.asarray(inp["L1_ffn_W_proj"]).T)
    m["afc"] = f32(np.asarray(inp["L1_ffn_A_fc"]).T)
    m["bfcT"] = f32(np.asarray(inp["L1_ffn_B_fc"]).T)
    m["apr"] = f32(np.asarray(inp["L1_ffn_A_proj"]).T)
    m["bprT"] = f32(np.asarray(inp["L1_ffn_B_proj"]).T)

    sc = {}
    cols = []

    def put(name, arr):
        arr = np.asarray(arr, np.float32)
        if arr.ndim == 1:
            arr = arr[:, None]
        pad = np.zeros((128, arr.shape[1]), np.float32)
        pad[: arr.shape[0]] = arr
        sc[name] = (sum(c.shape[1] for c in cols), arr.shape[1])
        cols.append(pad)

    put("qg_rope", np.tile(np.asarray(inp["L0_attn_q_g"])[:32], 4))
    put("qg_pad", np.tile(np.asarray(inp["L0_attn_q_g"])[32:], 4))
    put("kg", np.tile(np.asarray(inp["L0_attn_k_g"]), 4))
    put("gate_b", inp["L0_ffn_gate_b"])
    put("bi", np.asarray(inp["L0_ffn_bi"]).reshape(32, 128).T)
    put("bfc_b", np.asarray(inp["L1_ffn_b_fc"]).reshape(32, 128).T)
    put("bpr_b", np.asarray(inp["L1_ffn_b_proj"]).reshape(8, 128).T)
    m["small"] = f32(np.concatenate(cols, axis=1))
    m["small_cols"] = sc

    rows = np.zeros((1, 1152), np.float32)
    rows[0, 0:1024] = np.asarray(inp["L0_ffn_bo"], np.float32)
    rows[0, 1024:1152] = 1.0
    m["rows"] = rows
    m["id128"] = np.eye(128, dtype=np.float32)
    m["iota"] = np.concatenate(
        [np.tile(np.arange(8, dtype=np.float32), (128, 1)),
         np.tile(np.arange(8, dtype=np.float32) + BIG, (128, 1))], axis=1)
    e4 = np.zeros((128, 4), np.float32)
    for p in range(128):
        e4[p, p // 32] = 1.0
    m["e4"] = e4
    m["e4t"] = f32(e4.T)
    e16 = np.zeros((8, 128), np.float32)
    for e in range(8):
        e16[e, e * 16:(e + 1) * 16] = 1.0
    m["e16"] = e16
    m["masks"] = _masks()
    cos128, sin128s = _rope_tables()
    m["cos128"] = cos128
    m["sin128"] = sin128s
    return m


CONST_NAMES = ["small", "rows", "id128", "iota", "e4", "e4t", "masks",
               "cos128", "sin128", "e16"]


ENGS = ["gpsimd", "tensor", "scalar", "vector", "sync"]


class Sem:
    def __init__(self, ctx, nc, name):
        self.h = ctx.enter_context(nc.semaphore(name))
        self.n = 0

    def tok(self):
        return (self, self.n)


class Script:
    def __init__(self, prog):
        self.prog = prog
        self.ops = {e: [] for e in ENGS}

    def add(self, eng, emit, waits=(), inc=None, inck=1, fence=False, dma=False):
        incs = []
        tok = None
        fw = None
        if fence:
            b = self.prog.bmp[eng]
            b.n += 1
            incs.append((b, 1))
            fw = (b, b.n)
            tok = fw
        elif inc is not None:
            inc.n += inck
            incs.append((inc, inck))
            tok = (inc, inc.n)
        self.ops[eng].append((emit, [w for w in waits if w is not None], incs, fw, dma, tok))
        return tok

    def fence_all(self):
        p = self.prog
        return ([p.pe.tok(), p.act.tok(), p.dve.tok(), p.pool.tok(),
                 p.dmaW.tok(), p.dmaS.tok()] +
                [p.bmp[e].tok() for e in ENGS])

    def fn_for(self, eng):
        ops = self.ops[eng]
        eng_name = eng
        if not ops:
            return None

        def f(e):
            pending = None
            last_tok = None
            last_nondma_i = None
            for i in range(len(ops) - 1, -1, -1):
                if not ops[i][4]:
                    last_nondma_i = i
                    break
            for i, (emit, waits, incs, fw, dma, tok) in enumerate(ops):
                if pending is not None:
                    e.wait_ge(pending[0].h, pending[1])
                pending = fw
                for s, v in waits:
                    e.wait_ge(s.h, v)
                ins = emit(e)
                if i == last_nondma_i and not incs:
                    b = self.prog.bmp[eng_name]
                    b.n += 1
                    incs = [(b, 1)]
                    tok = (b, b.n)
                for s, k in incs:
                    ins.then_inc(s.h, k)
                if i == last_nondma_i:
                    last_tok = tok
            return last_tok
        return f


class Prog:
    def __init__(self, meta, weight_names):
        self.nc = bass.Bass("TRN2", target_bir_lowering=False, debug=False)
        self.ctx = contextlib.ExitStack()
        self.meta = meta
        nc = self.nc
        self.hbm = {}
        self.hbm["x_in"] = nc.dram_tensor("x_in", [D, T], F32, kind="ExternalInput").ap()
        self.hbm["x_out"] = nc.dram_tensor("x_out", [D, T], F32, kind="ExternalOutput").ap()
        self.input_names = ["x_in"]
        for name in weight_names:
            self.add_input(name)
        self.hbm["scr"] = nc.dram_tensor("scr", [16, 128], F32, kind="Internal").ap()
        self.hbm["wq_fc"] = nc.dram_tensor("wq_fc", [128, 8 * 4096], BF16, kind="Internal").ap()
        self.hbm["wq_pr"] = nc.dram_tensor("wq_pr", [128, 32 * 1024], BF16, kind="Internal").ap()
        if DEBUG:
            for i in range(4):
                self.hbm[f"dbg{i}"] = nc.dram_tensor(f"dbg{i}", [128, 2048], F32,
                                                     kind="ExternalOutput").ap()
        c = self.ctx
        self.X = c.enter_context(nc.sbuf_tensor("gX", [128, 8 * T], F32))
        self.XB = c.enter_context(nc.sbuf_tensor("gXB", [128, 8 * T], BF16))
        scn = meta["small"].shape[1]
        self.SMALL = c.enter_context(nc.sbuf_tensor("gSMALL", [128, scn], F32))
        self.ROWS = c.enter_context(nc.sbuf_tensor("gROWS", [1, 1152], F32))
        self.ID = c.enter_context(nc.sbuf_tensor("gID", [128, 128], F32))
        self.IDB = c.enter_context(nc.sbuf_tensor("gIDB", [128, 128], BF16))
        self.IOTA = c.enter_context(nc.sbuf_tensor("gIOTA", [128, 16], F32))
        self.E4 = c.enter_context(nc.sbuf_tensor("gE4", [128, 4], F32))
        self.E4T = c.enter_context(nc.sbuf_tensor("gE4T", [4, 128], F32))
        self.E16 = c.enter_context(nc.sbuf_tensor("gE16", [8, 128], BF16))
        self.ONEC = c.enter_context(nc.sbuf_tensor("gONEC", [128, 1], F32))
        self.bar = Sem(c, nc, "s_bar")
        self.dmaW = Sem(c, nc, "s_dmaW")
        self.dmaS = Sem(c, nc, "s_dmaS")
        self.pe = Sem(c, nc, "s_pe")
        self.act = Sem(c, nc, "s_act")
        self.dve = Sem(c, nc, "s_dve")
        self.pool = Sem(c, nc, "s_pool")
        self.bmp = {e: Sem(c, nc, "s_bmp_" + e) for e in ENGS}

    def add_input(self, name):
        if name not in self.hbm:
            a = self.meta[name]
            self.hbm[name] = self.nc.dram_tensor(
                name, list(a.shape), F32, kind="ExternalInput").ap()
            self.input_names.append(name)
        return self.hbm[name]

    def small(self, name):
        off, w = self.meta["small_cols"][name]
        return self.SMALL[:, off:off + w]

    def rows(self, name, n):
        offs = {"bo": 0, "ones": 1024, "twos": 1152}
        return self.ROWS[0:1, offs[name]:offs[name] + n]


def run_stage(prog, script):
    bar = prog.bar
    enter = bar.n
    wN, sN = prog.dmaW.n, prog.dmaS.n

    with prog.nc.Block() as block:
        for name in ENGS:
            fn = script.fn_for(name)

            def emit(eng, fn=fn):
                eng.wait_ge(bar.h, enter)
                last_tok = fn(eng) if fn else None
                if last_tok is not None:
                    eng.wait_ge(last_tok[0].h, last_tok[1])
                eng.wait_ge(prog.dmaW.h, wN)
                eng.wait_ge(prog.dmaS.h, sN)
                eng.sem_inc(bar.h, 1)

            getattr(block, name)(emit)
    bar.n += 5




def E_dma(out, in_):
    return lambda e: e.dma_start(out=out, in_=in_)


def E_mm(out, lhsT, rhs, start, stop, tp=None):
    if tp is None:
        return lambda e: e.matmul(out, lhsT, rhs, start=start, stop=stop)
    return lambda e: e.matmul(out, lhsT, rhs, start=start, stop=stop,
                              tile_position=tp, skip_group_check=True)


def E_tr(out, in_, ident):
    return lambda e: e.transpose(out, in_, ident)


def E_act(out, in_, func, bias=0.0, scale=1.0, accum=None):
    if accum is None:
        return lambda e: e.activation(out, in_, func, bias=bias, scale=scale)
    return lambda e: e.activation(out, in_, func, bias=bias, scale=scale,
                                  accum_out=accum)


def E_tt(out, a, b, op):
    return lambda e: e.tensor_tensor(out, a, b, op)


def E_ts(out, a, s1, op0, s2=None, op1=None):
    if op1 is None:
        return lambda e: e.tensor_scalar(out, a, s1, None, op0)
    return lambda e: e.tensor_scalar(out, a, s1, s2, op0, op1)


def E_stt(out, in0, s, in1, op0, op1):
    return lambda e: e.scalar_tensor_tensor(out, in0, s, in1, op0, op1)


def E_copy(out, in_):
    return lambda e: e.tensor_copy(out, in_)


def E_red(out, in_, op, negate=False, absval=False):
    return lambda e: e.tensor_reduce(out, in_, AX.X, op,
                                     apply_absolute_value=absval, negate=negate)


def E_redC(out, in_, op):
    return lambda e: e.tensor_reduce(out, in_, AX.C, op)


def E_recip(out, in_):
    return lambda e: e.reciprocal(out, in_)


def E_memset(ap, v):
    return lambda e: e.memset(ap, v)


def load_fm(S, prog, sb, name, col0=0):
    hbm_ap = prog.add_input(name)
    ktot, mtot = hbm_ap.shape
    nk = (ktot + 127) // 128
    if ktot <= 128:
        return S.add("gpsimd", E_dma(sb[0:ktot, col0:col0 + mtot], hbm_ap),
                     inc=prog.dmaW, inck=16, dma=True)
    return S.add("gpsimd",
                 E_dma(sb[:, col0:col0 + nk * mtot].rearrange("p (c m) -> p c m", c=nk),
                       hbm_ap.rearrange("(c p) m -> p c m", p=128)),
                 inc=prog.dmaW, inck=16, dma=True)




def emit_ln(S, prog, sb, z_ready, refresh_xb=True, dbg=None):
    pe, act, dve = prog.pe, prog.act, prog.dve
    X, XB = prog.X, prog.XB
    p_st, p_rb, p_mb = sb["p_st"], sb["p_rb"], sb["p_mb"]
    SQ, LNR, LNR2 = sb["SQ"], sb["LNR"], sb["LNR2"]
    st_free = []
    b_free = []
    sq_war = None
    for tb in range(NTB):
        o = tb * 512
        prev_mmq = {0: None, 1: None}
        g_sum = g_sq = None
        for dc in range(NDC):
            xap = X[:, dc * T + o: dc * T + o + 512]
            h = (dc % 2) * 512
            w = ([z_ready, sq_war] if dc == 0 else []) + \
                ([prev_mmq[dc % 2]] if prev_mmq[dc % 2] is not None else [])
            sq_tok = S.add("vector", E_tt(SQ[:, h:h + 512], xap, xap, ALU.mult),
                           waits=w, fence=True)
            g_sum = S.add("tensor", E_mm(p_st[0:1, 0:512], prog.ONEC[:, :], xap,
                                         dc == 0, dc == NDC - 1),
                          waits=([z_ready] + st_free if dc == 0 else ()),
                          inc=pe)
            g_sq = S.add("tensor", E_mm(p_st[0:1, 512:1024], prog.ONEC[:, :],
                                        SQ[:, h:h + 512], dc == 0, dc == NDC - 1),
                         waits=[sq_tok], inc=pe)
            prev_mmq[dc % 2] = g_sq
        rstd_ap = LNR[0:1, 0:512]
        m_ap = LNR2[0:1, 0:512]
        ms_ap = LNR[32:33, 0:512]
        m2_ap = LNR2[32:33, 0:512]
        t_m = S.add("vector", E_ts(m_ap, p_st[0:1, 0:512], 1.0 / D, ALU.mult),
                    waits=[g_sum], fence=True)
        t_ms = S.add("vector", E_ts(ms_ap, p_st[0:1, 512:1024], 1.0 / D, ALU.mult),
                     waits=[g_sq], fence=True)
        st_free = [t_m, t_ms]
        S.add("vector", E_tt(m2_ap, m_ap, m_ap, ALU.mult), fence=True)
        S.add("vector", E_tt(ms_ap, ms_ap, m2_ap, ALU.subtract), fence=True)
        t_var = S.add("vector", E_ts(ms_ap, ms_ap, EPS_LN, ALU.add),
                      inc=dve, fence=True)
        S.add("scalar", E_act(m2_ap, ms_ap, AF.Ln), waits=[t_var], fence=True)
        t_re = S.add("scalar", E_act(ms_ap, m2_ap, AF.Exp, scale=-0.5),
                     fence=True)
        t_rstd = S.add("vector", E_copy(rstd_ap, ms_ap), waits=[t_re], fence=True)
        if dbg is not None:
            S.add("vector", E_copy(dbg[0:1, o:o + 512], m_ap), waits=[t_m])
            S.add("vector", E_copy(dbg[32:33, o:o + 512], ms_ap), waits=[t_var])
            S.add("vector", E_copy(dbg[64:65, o:o + 512], rstd_ap), waits=[t_rstd], fence=True)
        S.add("tensor", E_mm(p_rb[:, :], prog.rows("ones", 128), rstd_ap, True, True),
              waits=[t_rstd] + b_free)
        g_b = S.add("tensor", E_mm(p_mb[:, :], prog.rows("ones", 128), m_ap, True, True),
                    waits=[t_m], inc=pe)
        b_free = []
        for dc in range(NDC):
            xap = X[:, dc * T + o: dc * T + o + 512]
            h = (dc % 2) * 512
            S.add("vector", E_tt(SQ[:, h:h + 512], xap, p_mb[:, :], ALU.subtract),
                  waits=([g_b] if dc == 0 else ()))
            n2 = S.add("vector", E_tt(xap, SQ[:, h:h + 512], p_rb[:, :], ALU.mult),
                       inc=dve)
            if refresh_xb:
                S.add("vector", E_copy(XB[:, dc * T + o: dc * T + o + 512], xap),
                      inc=dve)
            b_free = [n2]
            sq_war = n2
    return prog.dve.tok()


def ln_buffers(prog, sctx, tag):
    nc = prog.nc
    return dict(
        SQ=sctx.enter_context(nc.sbuf_tensor(tag + "_lnsq", [128, 1024], F32)),
        LNR=sctx.enter_context(nc.sbuf_tensor(tag + "_lnr", [33, 512], F32)),
        LNR2=sctx.enter_context(nc.sbuf_tensor(tag + "_lnr2", [33, 512], F32)),
        p_st=sctx.enter_context(nc.psum_tensor(tag + "_pst", [1, 1024], F32)),
        p_rb=sctx.enter_context(nc.psum_tensor(tag + "_prb2", [128, 512], F32)),
        p_mb=sctx.enter_context(nc.psum_tensor(tag + "_pmb", [128, 512], F32)),
    )




def stage_load(prog):
    S = Script(prog)
    m = prog.meta
    dmaS, dmaW, dve = prog.dmaS, prog.dmaW, prog.dve
    S.add("sync", E_dma(prog.X[:, :].rearrange("p (c t) -> p c t", c=NDC),
                        prog.hbm["x_in"].rearrange("(c p) t -> p c t", p=128)),
          inc=dmaS, inck=16, dma=True)
    for hbname, sb in [("small", prog.SMALL), ("rows", prog.ROWS), ("id128", prog.ID),
                       ("iota", prog.IOTA), ("e4", prog.E4), ("e4t", prog.E4T)]:
        ap = prog.add_input(hbname)
        rr = sb[0:ap.shape[0], 0:ap.shape[1]]
        S.add("sync", E_dma(rr, ap), inc=dmaS, inck=16, dma=True)
    S.add("gpsimd", E_dma(prog.IDB[:, :], prog.add_input("id128")), inc=dmaW, inck=16, dma=True)
    S.add("gpsimd", E_dma(prog.E16[0:8, :], prog.add_input("e16")), inc=dmaW, inck=16, dma=True)
    xw = dmaS.tok()
    S.add("vector", E_memset(prog.ONEC[:, :], 1.0))
    S.add("vector", E_copy(prog.XB[:, :], prog.X[:, :]), waits=[xw], inc=dve)
    return S


def stage_store(prog):
    S = Script(prog)
    S.add("sync", E_dma(prog.hbm["x_out"].rearrange("(c p) t -> p c t", p=128),
                        prog.X[:, :].rearrange("p (c t) -> p c t", c=NDC)),
          inc=prog.dmaS, inck=16, dma=True)
    tok = prog.dmaS.tok()
    S.add("sync", lambda e: e.nop(), waits=[tok])
    return S




def stage_l0_attn(prog, sctx):
    nc = prog.nc
    S = Script(prog)
    X, XB = prog.X, prog.XB
    al = lambda name, shape, dt: sctx.enter_context(nc.sbuf_tensor(name, shape, dt))
    pe, act, dve, dmaW = prog.pe, prog.act, prog.dve, prog.dmaW

    WQ = al("a0_wq", [128, 8 * 1024], BF16)
    WKV = al("a0_wkv", [128, 8 * 128], BF16)
    WK = al("a0_wk", [128, 512], BF16)
    WV = al("a0_wv", [128, 512], BF16)
    WO = al("a0_wo", [128, 4 * 1024], BF16)
    QN = al("a0_qn", [128, 4 * 2048], BF16)
    KN = al("a0_kn", [128, 4 * 2048], BF16)
    VT = al("a0_vt", [128, 16 * 512], BF16)
    MSK = al("a0_msk", [128, 4 * 512], BF16)
    NM = al("a0_nm", [128, 8], F32)
    SR = al("a0_sr", [128, 8], F32)
    OC = XB
    EB = XB[:, 8192:10240]
    EBx = XB[:, 10240:12288]
    PT = XB[:, 12288:14336]

    psc = lambda name, shape, dt=F32: sctx.enter_context(nc.psum_tensor(name, shape, dt))


    for sb, name in [(WQ, "wq"), (WKV, "wkv"), (WK, "wk"), (WV, "wv"), (WO, "wo")]:
        load_fm(S, prog, sb, name)
    mask_ap = prog.add_input("masks")
    S.add("gpsimd", E_dma(MSK[:, :].rearrange("p (v q) -> p v q", v=4),
                          mask_ap.rearrange("v p q -> p v q")), inc=dmaW, inck=16, dma=True)
    wl = prog.dmaW.tok()

    qg_rope = prog.small("qg_rope")
    kg = prog.small("kg")

    with contextlib.ExitStack() as p1:
        QF = p1.enter_context(nc.sbuf_tensor("a0_qf", [128, 512], F32))
        QP = p1.enter_context(nc.sbuf_tensor("a0_qp", [128, 512], F32))
        VB = p1.enter_context(nc.sbuf_tensor("a0_vb", [128, 2048], BF16))
        KVB = p1.enter_context(nc.sbuf_tensor("a0_kvb", [128, 512], BF16))
        SQ2 = p1.enter_context(nc.sbuf_tensor("a0_sq", [128, 1024], F32))
        RST = p1.enter_context(nc.sbuf_tensor("a0_rst", [4, 512], F32))
        RSF = p1.enter_context(nc.sbuf_tensor("a0_rsf", [4, 512], F32))
        p_a = p1.enter_context(nc.psum_tensor("a0_pa", [128, 512], F32))
        p_b = p1.enter_context(nc.psum_tensor("a0_pb", [128, 512], F32))
        p_ss = p1.enter_context(nc.psum_tensor("a0_pss", [4, 512], F32))
        p_rb = p1.enter_context(nc.psum_tensor("a0_prb", [128, 512], F32))
        p_t = p1.enter_context(nc.psum_tensor("a0_ptr", [128, 1024], BF16))

        a_free = b_free = ss_free = rb_free = t_free = None
        sq_free = rst_free = qf_free = kvb_free = None
        for c in range(4):
            vdone = None
            for tb in range(4):
                oq = tb * 512
                xsl = lambda dc: XB[:, dc * T + oq: dc * T + oq + 512]
                for dc in range(NDC):
                    g_qr = S.add("tensor",
                                 E_mm(p_a[:, :], WQ[:, dc * 1024 + c * 128: dc * 1024 + c * 128 + 128],
                                      xsl(dc), dc == 0, dc == NDC - 1),
                                 waits=([wl, a_free] if dc == 0 else ()),
                                 inc=(pe if dc == NDC - 1 else None))
                for dc in range(NDC):
                    g_qp = S.add("tensor",
                                 E_mm(p_b[:, :], WQ[:, dc * 1024 + (4 + c) * 128: dc * 1024 + (4 + c) * 128 + 128],
                                      xsl(dc), dc == 0, dc == NDC - 1),
                                 waits=([b_free] if dc == 0 else ()),
                                 inc=(pe if dc == NDC - 1 else None))
                e_qr = S.add("scalar", E_act(QF[:, 0:512], p_a[:, :], AF.Identity),
                             waits=[g_qr, qf_free], inc=act)
                s_qr = S.add("vector", E_tt(SQ2[:, 0:512], QF[:, 0:512], QF[:, 0:512], ALU.mult),
                             waits=[e_qr, sq_free], fence=True)
                e_qp = S.add("vector", E_copy(QP[:, 0:512], p_b[:, :]),
                             waits=[g_qp], fence=True)
                s_qp = S.add("vector", E_tt(SQ2[:, 512:1024], QP[:, 0:512], QP[:, 0:512], ALU.mult),
                             fence=True)
                a_free = s_qr
                b_free = e_qp
                S.add("tensor", E_mm(p_ss[0:4, :], prog.E4[:, :], SQ2[:, 0:512], True, False),
                      waits=[s_qr, s_qp, ss_free])
                g_ss = S.add("tensor", E_mm(p_ss[0:4, :], prog.E4[:, :], SQ2[:, 512:1024],
                                            False, True), inc=pe)
                v1 = S.add("vector", E_ts(RSF[0:4, :], p_ss[0:4, :], 1.0 / 64, ALU.mult,
                                          EPS_RMS, ALU.add),
                           waits=[g_ss, rst_free], inc=dve, fence=True)
                r1 = S.add("scalar", E_act(RSF[0:4, :], RSF[0:4, :], AF.Ln),
                           waits=[v1], fence=True)
                r2a = S.add("scalar", E_act(RSF[0:4, :], RSF[0:4, :], AF.Exp, scale=-0.5),
                            fence=True)
                r2 = S.add("vector", E_copy(RST[0:4, :], RSF[0:4, :]),
                           waits=[r2a, rst_free], fence=True)
                ss_free = v1
                g_rb = S.add("tensor", E_mm(p_rb[:, :], prog.E4T[0:4, :], RST[0:4, :],
                                            True, True), waits=[r2, rb_free], inc=pe)
                n_q = S.add("vector",
                            E_stt(QN[:, c * 2048 + oq: c * 2048 + oq + 512],
                                  QF[:, 0:512], qg_rope, p_rb[:, :],
                                  ALU.mult, ALU.mult),
                            waits=[g_rb, e_qr], inc=dve, fence=True)
                rb_free = n_q
                for dc in range(NDC):
                    g_kv = S.add("tensor",
                                 E_mm(p_a[:, :], WKV[:, dc * 128: dc * 128 + 128],
                                      xsl(dc), dc == 0, dc == NDC - 1),
                                 waits=([e_qr] if dc == 0 else ()),
                                 inc=(pe if dc == NDC - 1 else None))
                e_kv = S.add("vector", E_copy(KVB[:, 0:512], p_a[:, :]),
                             waits=[g_kv, kvb_free], fence=True)
                a_free = e_kv
                g_k = S.add("tensor", E_mm(p_b[:, :], WK[:, c * 128: c * 128 + 128],
                                           KVB[:, 0:512], True, True),
                            waits=[e_kv, b_free, n_q], inc=pe)
                e_k = S.add("scalar", E_act(QF[:, 0:512], p_b[:, :], AF.Identity),
                            waits=[g_k], inc=act)
                s_k = S.add("vector", E_tt(SQ2[:, 0:512], QF[:, 0:512], QF[:, 0:512], ALU.mult),
                            waits=[g_ss, e_k], fence=True)
                b_free = s_k
                g_ss2 = S.add("tensor", E_mm(p_ss[0:4, :], prog.E4[:, :], SQ2[:, 0:512],
                                             True, True), waits=[s_k, ss_free], inc=pe)
                sq_free = g_ss2
                v2 = S.add("vector", E_ts(RSF[0:4, :], p_ss[0:4, :], 1.0 / 32, ALU.mult,
                                          EPS_RMS, ALU.add),
                           waits=[g_ss2, g_rb], inc=dve, fence=True)
                r3 = S.add("scalar", E_act(RSF[0:4, :], RSF[0:4, :], AF.Ln),
                           waits=[v2], fence=True)
                r4a = S.add("scalar", E_act(RSF[0:4, :], RSF[0:4, :], AF.Exp, scale=-0.5),
                            fence=True)
                r4 = S.add("vector", E_copy(RST[0:4, :], RSF[0:4, :]),
                           waits=[r4a, g_rb], fence=True)
                ss_free = v2
                g_rb2 = S.add("tensor", E_mm(p_rb[:, :], prog.E4T[0:4, :], RST[0:4, :],
                                             True, True), waits=[r4, rb_free], inc=pe)
                rst_free = g_rb2
                n_k = S.add("vector",
                            E_stt(KN[:, c * 2048 + oq: c * 2048 + oq + 512],
                                  QF[:, 0:512], kg, p_rb[:, :],
                                  ALU.mult, ALU.mult),
                            waits=[g_rb2, e_k], inc=dve, fence=True)
                rb_free = n_k
                qf_free = n_k
                g_v = S.add("tensor", E_mm(p_b[:, :], WV[:, c * 128: c * 128 + 128],
                                           KVB[:, 0:512], True, True),
                            waits=[b_free], inc=pe)
                e_v = S.add("vector", E_copy(VB[:, oq:oq + 512], p_b[:, :]),
                            waits=[g_v], fence=True)
                b_free = e_v
                kvb_free = g_v
                vdone = e_v
            for grp in range(2):
                for k in range(8):
                    sc8 = grp * 8 + k
                    g_t = S.add("tensor", E_tr(p_t[:, k * 128:k * 128 + 128],
                                               VB[:, sc8 * 128:sc8 * 128 + 128],
                                               prog.IDB[:, :]),
                                waits=([vdone, t_free] if k == 0 else ()),
                                inc=(pe if k == 7 else None))
                vt3 = VT[:, :].rearrange("p (s c r) -> p s c r", s=16, c=4)
                ev = S.add("vector",
                           E_copy(vt3[:, grp * 8:grp * 8 + 8, c, :],
                                  p_t[:, :].rearrange("p (s r) -> p s r", s=8)),
                           waits=[g_t], inc=dve)
                t_free = ev

    ph1 = S.fence_all()
    with contextlib.ExitStack() as p2:
        psA = p2.enter_context(nc.psum_tensor("a0_psA", [128, 512], F32))
        psB = p2.enter_context(nc.psum_tensor("a0_psB", [128, 512], F32))
        p_t2 = p2.enter_context(nc.psum_tensor("a0_pt2", [128, 1024], BF16))
        p_pv = p2.enter_context(nc.psum_tensor("a0_ppv", [32, 512], F32))
        SROW0 = p2.enter_context(nc.sbuf_tensor("a0_sr0", [128, 2048], F32))
        SROW1 = p2.enter_context(nc.sbuf_tensor("a0_sr1", [128, 2048], F32))
        iters = [(32 * j, tqc, c) for c in range(4) for j in range(4)
                 for tqc in range(NTC)]
        emit_attn_p2(
            S, prog, iters=iters,
            QNf=lambda b, tqc, c: QN[b:b + 32, c * 2048 + tqc * 128: c * 2048 + tqc * 128 + 128],
            KNf=lambda b, sg, c: KN[b:b + 32, c * 2048 + sg * 512: c * 2048 + sg * 512 + 512],
            VTf=lambda c, sc, b: VT[:, sc * 512 + c * 128 + b: sc * 512 + c * 128 + b + 32],
            OCf=lambda c, b, tqc: OC[b:b + 32, c * 2048 + tqc * 128: c * 2048 + tqc * 128 + 128],
            hd=32, psA=psA, psB=psB, p_t=p_t2, p_pv=p_pv,
            SROW=[SROW0, SROW1], EB=[EB, EBx], MSK=MSK, NM=NM, SR=SR, PTB=PT,
            start_waits=ph1)

    ph2 = S.fence_all()
    with contextlib.ExitStack() as p3:
        p_o = p3.enter_context(nc.psum_tensor("a0_po", [128, 512], F32))
        lnb = dict(
            SQ=p3.enter_context(nc.sbuf_tensor("a0_lnsq2", [128, 1024], F32)),
            LNR=p3.enter_context(nc.sbuf_tensor("a0_lnr", [33, 512], F32)),
            LNR2=p3.enter_context(nc.sbuf_tensor("a0_lnr2", [33, 512], F32)),
            p_st=p3.enter_context(nc.psum_tensor("a0_pst", [1, 1024], F32)),
            p_rb=p3.enter_context(nc.psum_tensor("a0_prb2", [128, 512], F32)),
            p_mb=p3.enter_context(nc.psum_tensor("a0_pmb", [128, 512], F32)),
        )
        o_free = None
        zt = None
        for tb in range(4):
            for dc in range(NDC):
                for kc in range(4):
                    w = []
                    if kc == 0:
                        w = ([o_free] if o_free else []) + (ph2 if (tb == 0 and dc == 0) else [])
                    g_o = S.add("tensor",
                                E_mm(p_o[:, :], WO[:, kc * 1024 + dc * 128: kc * 1024 + dc * 128 + 128],
                                     OC[:, kc * 2048 + tb * 512: kc * 2048 + tb * 512 + 512],
                                     kc == 0, kc == 3),
                                waits=w, inc=(pe if kc == 3 else None))
                zt = S.add("vector", E_tt(X[:, dc * T + tb * 512: dc * T + tb * 512 + 512],
                                          p_o[:, :],
                                          X[:, dc * T + tb * 512: dc * T + tb * 512 + 512],
                                          ALU.add),
                           waits=[g_o], inc=dve)
                o_free = zt
        if DEBUG:
            for i, src_ap in enumerate([XB[:, 0:2048], XB[:, 2048:4096],
                                        XB[:, 4096:6144], XB[:, 6144:8192]]):
                S.add("gpsimd", E_dma(prog.hbm[f"dbg{i}"], src_ap),
                      waits=[prog.pe.tok(), prog.act.tok(), prog.dve.tok()] + [prog.bmp[e].tok() for e in ENGS],
                      inc=prog.dmaW, inck=16, dma=True)
        emit_ln(S, prog, lnb, zt)
    return S


def stage_l0_ffn(prog, sctx):
    nc = prog.nc
    S = Script(prog)
    X, XB = prog.X, prog.XB
    al = lambda name, shape, dt: sctx.enter_context(nc.sbuf_tensor(name, shape, dt))
    pe, act, dve, dmaW = prog.pe, prog.act, prog.dve, prog.dmaW

    GWT = al("f0_gwt", [128, 64], BF16)
    ACT_ = al("f0_acat", [128, 1024], BF16)
    BCT = al("f0_bct", [128, 4096], BF16)
    WSL = al("f0_wsl", [128, 2 * 1024], BF16)
    LGT = al("f0_lgt", [128, 128], F32)
    E8 = al("f0_e8", [128, 8], F32)
    C8 = al("f0_c8", [128, 8], F32)
    L2T = al("f0_l2", [128, 8], F32)
    G1 = al("f0_g1", [128, 8], F32)
    SRW = al("f0_srw", [128, 2], F32)
    JNK = al("f0_jnk", [128, 8], F32)
    SEL1 = al("f0_sel1", [128, 128], F32)
    SEL2 = al("f0_sel2", [128, 128], F32)
    SELT1 = al("f0_selt1", [8, 2048], BF16)
    SELT2 = al("f0_selt2", [8, 2048], BF16)
    VSA = al("f0_vsa", [128, 16], F32)
    VST = al("f0_vst", [16, 128], F32)
    VSB = al("f0_vsb", [128, 2048], F32)
    VROW = VSB[0:1, 0:2048]
    XA = al("f0_xa", [128, 2048], F32)
    LG = XA[0:8, 0:2048]
    XS1 = al("f0_xs1", [128, 2048], BF16)
    XS2 = al("f0_xs2", [128, 2048], BF16)
    H = al("f0_h", [128, 32 * 512], BF16)
    BASE = al("f0_base", [128, 512], BF16)
    T1 = al("f0_t1", [128, 512], F32)
    T2 = al("f0_t2", [128, 512], F32)
    R1 = al("f0_r1", [128, 512], BF16)
    R2 = al("f0_r2", [128, 512], BF16)

    load_fm(S, prog, GWT, "gw")
    load_fm(S, prog, ACT_, "acat")
    load_fm(S, prog, BCT, "bcat")
    wl = prog.dmaW.tok()
    bi_col = prog.small("bi")
    gate_b = prog.small("gate_b")

    sel_free = None
    e_vsb = None
    e_xa = None
    e_xm = None
    t1_war = None
    with contextlib.ExitStack() as pA:
        p_g = pA.enter_context(nc.psum_tensor("f0_pg", [8, 512], F32))
        p_lgt = pA.enter_context(nc.psum_tensor("f0_plgt", [128, 128], F32))
        p_sel = pA.enter_context(nc.psum_tensor("f0_psel", [8, 512], F32))
        p_vs = pA.enter_context(nc.psum_tensor("f0_pvs", [16, 128], F32))
        p_bc = pA.enter_context(nc.psum_tensor("f0_pbc", [128, 512], F32))
        p_m = pA.enter_context(nc.psum_tensor("f0_pm", [128, 512], F32))
        p_xa = pA.enter_context(nc.psum_tensor("f0_pxa", [128, 512], F32))
        g_free = None
        for tb in range(4):
            for dc in range(NDC):
                g_g = S.add("tensor", E_mm(p_g[0:8, :], GWT[:, dc * 8: dc * 8 + 8],
                                           XB[:, dc * T + tb * 512: dc * T + tb * 512 + 512],
                                           dc == 0, dc == NDC - 1),
                            waits=([wl, g_free] if dc == 0 else ()),
                            inc=(pe if dc == NDC - 1 else None))
            g_free = S.add("vector", E_ts(LG[0:8, tb * 512:tb * 512 + 512], p_g[0:8, :],
                                          gate_b[0:8, :], ALU.add),
                           waits=[g_g], fence=True)
        tr_toks = []
        for tc in range(NTC):
            g_t = S.add("tensor", E_tr(p_lgt[:, tc * 8: tc * 8 + 8],
                                       LG[0:8, tc * 128: tc * 128 + 128],
                                       prog.ID[0:8, 0:8]),
                        waits=([g_free] if tc == 0 else ()),
                        inc=(pe if tc == NTC - 1 else None))
        if L0F_G <= 1:
            return S
        e_lgt = S.add("vector", E_copy(LGT[:, :], p_lgt[:, :]), waits=[g_t], fence=True)
        if L0F_G <= 2:
            return S
        iota = prog.IOTA[:, 0:8]
        iotaB = prog.IOTA[:, 8:16]
        for tc in range(NTC):
            lgt = LGT[:, tc * 8: tc * 8 + 8]
            t1 = S.add("vector", E_red(G1[:, 0:1], lgt, ALU.max), fence=True)
            t2 = S.add("vector", E_ts(G1[:, 1:2], G1[:, 0:1], -1.0, ALU.mult), fence=True)
            t3 = S.add("vector", E_ts(E8[:, :], lgt, G1[:, 0:1], ALU.is_equal), fence=True)
            t4 = S.add("vector", E_stt(C8[:, :], E8[:, :], -BIG, iotaB, ALU.mult, ALU.add), fence=True)
            t5 = S.add("vector", E_red(G1[:, 2:3], C8[:, :], ALU.min), fence=True)
            t6 = S.add("vector", E_ts(SEL1[:, tc * 8: tc * 8 + 8], iota, G1[:, 2:3],
                                      ALU.is_equal), fence=True)
            t7 = S.add("vector", E_stt(L2T[:, :], SEL1[:, tc * 8: tc * 8 + 8], NEG, lgt,
                                       ALU.mult, ALU.add), fence=True)
            t8 = S.add("vector", E_red(G1[:, 3:4], L2T[:, :], ALU.max), fence=True)
            t9 = S.add("vector", E_ts(E8[:, :], L2T[:, :], G1[:, 3:4], ALU.is_equal), fence=True)
            ta = S.add("vector", E_stt(C8[:, :], E8[:, :], -BIG, iotaB, ALU.mult, ALU.add), fence=True)
            tb_ = S.add("vector", E_red(G1[:, 4:5], C8[:, :], ALU.min), fence=True)
            tcx = S.add("vector", E_ts(SEL2[:, tc * 8: tc * 8 + 8], iota, G1[:, 4:5],
                                       ALU.is_equal), fence=True)
            s1 = S.add("scalar", E_act(JNK[:, :], lgt, AF.Exp, bias=G1[:, 1:2],
                                       accum=SRW[:, 0:1]), waits=[t8], fence=True)
            s2 = S.add("scalar", E_act(G1[:, 5:6], G1[:, 3:4], AF.Exp, bias=G1[:, 1:2]),
                       fence=True)
            v1 = S.add("vector", E_recip(SRW[:, 1:2], SRW[:, 0:1]), waits=[s1], fence=True)
            v2 = S.add("vector", E_stt(VSA[:, tc:tc + 1], G1[:, 5:6], 1.0, SRW[:, 1:2],
                                       ALU.add, ALU.mult), waits=[s2], fence=True)
        if L0F_G <= 3:
            return S
        for src_t, dst in [(SEL1, SELT1), (SEL2, SELT2)]:
            for grp in range(4):
                for k in range(4):
                    tc = grp * 4 + k
                    g_t = S.add("tensor", E_tr(p_sel[0:8, k * 128: k * 128 + 128],
                                               src_t[:, tc * 8: tc * 8 + 8],
                                               prog.ID[:, :]),
                                waits=([v2, sel_free] if k == 0 else ()),
                                inc=(pe if k == 3 else None))
                sel_free = S.add("vector", E_copy(dst[0:8, grp * 512: grp * 512 + 512],
                                                  p_sel[0:8, :]),
                                 waits=[g_t], fence=True)
        if L0F_G <= 4:
            return S
        g_vt = S.add("tensor", E_tr(p_vs[0:16, :], VSA[:, :], prog.ID[:, :]),
                     waits=[v2], inc=pe)
        e_vt = S.add("vector", E_copy(VST[0:16, :], p_vs[0:16, :]), waits=[g_vt], fence=True)
        d1 = S.add("gpsimd", E_dma(prog.hbm["scr"], VST[0:16, :]), waits=[e_vt],
                   inc=dmaW, inck=16, dma=True)
        d1t = prog.dmaW.tok()
        S.add("gpsimd", E_dma(VROW[0:1, :], prog.hbm["scr"].rearrange("p n -> (p n)")[None, :]),
              waits=[d1t], inc=dmaW, inck=16, dma=True)
        vrow_tok = prog.dmaW.tok()
        if L0F_G <= 5:
            return S
        xs_toks = []
        for tb in range(4):
            g_b = S.add("tensor", E_mm(p_bc[:, :], prog.rows("ones", 128),
                                       VROW[0:1, tb * 512: tb * 512 + 512], True, True),
                        waits=[vrow_tok] + ([e_vsb] if tb > 0 else []), inc=pe)
            e_vsb = S.add("vector", E_copy(VSB[:, tb * 512: tb * 512 + 512], p_bc[:, :]),
                          waits=[g_b], fence=True)
            for dc in range(NDC):
                g_xa = S.add("tensor", E_mm(p_xa[:, :], ACT_[:, dc * 128: dc * 128 + 128],
                                            XB[:, dc * T + tb * 512: dc * T + tb * 512 + 512],
                                            dc == 0, dc == NDC - 1),
                             waits=(([wl] + ([e_xa] if e_xa else [])) if dc == 0 else ()),
                             inc=(pe if dc == NDC - 1 else None))
            e_xa = S.add("vector", E_copy(XA[:, tb * 512: tb * 512 + 512], p_xa[:, :]),
                         waits=[g_xa], fence=True)
            for selt, xs in [(SELT1, XS1), (SELT2, XS2)]:
                g_m = S.add("tensor", E_mm(p_m[:, :], prog.E16[0:8, :],
                                           selt[0:8, tb * 512: tb * 512 + 512], True, True),
                            waits=[sel_free, e_xm], inc=pe)
                e_xm = S.add("vector", E_tt(xs[:, tb * 512: tb * 512 + 512], p_m[:, :],
                                            XA[:, tb * 512: tb * 512 + 512], ALU.mult),
                             waits=[g_m, e_xa], fence=True)
            xs_toks.append(e_xm)

    if L0F_PART == 0:
        return S
    phA = S.fence_all()
    with contextlib.ExitStack() as pB:
        P8 = [pB.enter_context(nc.psum_tensor(f"f0_p8_{i}", [128, 512], F32))
              for i in range(8)]
        p_h, p_l = P8[0], P8[1]
        wi_hbm = prog.add_input("wi").rearrange("(c p) m -> p c m", p=128)
        wo_hbm = prog.add_input("wo2")
        h_free = None
        l_free = None
        r1_done = r2_done = hw_done = None
        o_free = []
        b_done = {}
        zt = None

        def wi_slab(fc, wait):
            sb = fc % 2
            S.add("gpsimd",
                  E_dma(WSL[:, sb * 1024:(sb + 1) * 1024].rearrange("p (c m) -> p c m", c=8),
                        wi_hbm[:, :, fc * 128:(fc + 1) * 128]),
                  waits=wait, inc=dmaW, inck=16, dma=True)
            return prog.dmaW.tok()

        def wo_slab(fc, wait):
            sb = fc % 2
            S.add("gpsimd",
                  E_dma(WSL[:, sb * 1024:(sb + 1) * 1024], wo_hbm[fc * 128:(fc + 1) * 128, :]),
                  waits=wait, inc=dmaW, inck=16, dma=True)
            return prog.dmaW.tok()

        ntg = 1 if L0F_PART in (1, 2, 3) else 4
        for tg in range(ntg):
            og = tg * 512
            slab_tok = {}
            mm_done = {}
            w0 = [xs_toks[-1]] + (phA if tg == 0 else [b_done[30], b_done[31]])
            slab_tok[0] = wi_slab(0, w0)
            slab_tok[1] = wi_slab(1, [])
            for fc in range(32):
                sb = fc % 2
                for dc in range(NDC):
                    g_h = S.add("tensor",
                                E_mm(p_h[:, :], WSL[:, sb * 1024 + dc * 128: sb * 1024 + dc * 128 + 128],
                                     XB[:, dc * T + og: dc * T + og + 512],
                                     dc == 0, dc == NDC - 1),
                                waits=([slab_tok[fc], h_free] + (o_free if fc == 0 else [])
                                       if dc == 0 else ()),
                                inc=(pe if dc == NDC - 1 else None))
                mm_done[fc] = g_h
                if fc + 2 < 32:
                    slab_tok[fc + 2] = wi_slab(fc + 2, [mm_done[fc]])
                e_b = S.add("scalar", E_act(BASE[:, :], p_h[:, :], AF.Identity,
                                            bias=bi_col[:, fc:fc + 1]),
                            waits=[g_h], fence=True)
                h_free = e_b
                g_l1 = S.add("tensor", E_mm(p_l[:, :], BCT[:, fc * 128: fc * 128 + 128],
                                            XS1[:, og: og + 512], True, True),
                             waits=[l_free] + (o_free if fc == 0 else []), inc=pe)
                v_t1 = S.add("vector", E_tt(T1[:, :], p_l[:, :], BASE[:, :], ALU.add),
                             waits=[g_l1, e_b, r1_done], fence=True)
                g_l2 = S.add("tensor", E_mm(p_l[:, :], BCT[:, fc * 128: fc * 128 + 128],
                                            XS2[:, og: og + 512], True, True),
                             waits=[v_t1], inc=pe)
                r1_done = S.add("scalar", E_act(R1[:, :], T1[:, :], AF.Relu),
                                waits=[v_t1, hw_done], fence=True)
                v_t2 = S.add("vector", E_tt(T2[:, :], p_l[:, :], BASE[:, :], ALU.add),
                             waits=[g_l2, r2_done], fence=True)
                l_free = v_t2
                r2_done = S.add("scalar", E_act(R2[:, :], T2[:, :], AF.Relu),
                                waits=[v_t2], fence=True)
                hw_done = S.add("vector", E_tt(H[:, fc * 512: fc * 512 + 512], R1[:, :],
                                               R2[:, :], ALU.add),
                                waits=[r2_done], fence=True)
            if L0F_PART == 1:
                return S

            slab_tok = {}
            b_done = {}
            slab_tok[0] = wo_slab(0, [mm_done[30], mm_done[31]])
            slab_tok[1] = wo_slab(1, [])
            for fc in range(32):
                sb = fc % 2
                for dc in range(NDC):
                    w = []
                    if dc == 0:
                        w = [slab_tok[fc]]
                        if fc == 0:
                            w += [hw_done] + o_free
                    g_o = S.add("tensor",
                                E_mm(P8[dc][:, :],
                                     WSL[:, sb * 1024 + dc * 128: sb * 1024 + dc * 128 + 128],
                                     H[:, fc * 512: fc * 512 + 512],
                                     fc == 0, False),
                                waits=w, inc=(pe if dc == NDC - 1 else None))
                b_done[fc] = g_o
                if fc + 2 < 32:
                    slab_tok[fc + 2] = wo_slab(fc + 2, [b_done[fc]])
            o_free = []
            for dc in range(NDC):
                S.add("tensor",
                      E_mm(P8[dc][:, :], prog.rows("bo", 1024)[0:1, dc * 128: dc * 128 + 128],
                           VROW[0:1, og: og + 512], False, False),
                      waits=([b_done[31]] if dc == 0 else ()))
                g_bo = S.add("tensor",
                             E_mm(P8[dc][:, :], prog.rows("bo", 1024)[0:1, dc * 128: dc * 128 + 128],
                                  VROW[0:1, og: og + 512], False, True),
                             inc=pe)
                v_m = S.add("vector", E_tt(T1[:, :], P8[dc][:, :],
                                           VSB[:, og: og + 512], ALU.mult),
                            waits=[g_bo, t1_war], fence=True)
                zt = S.add("vector", E_tt(X[:, dc * T + og: dc * T + og + 512], T1[:, :],
                                          X[:, dc * T + og: dc * T + og + 512], ALU.add),
                           fence=True)
                o_free.append(zt)
                t1_war = zt
    if L0F_PART == 2:
        return S
    with contextlib.ExitStack() as pLN:
        lnb = dict(
            SQ=pLN.enter_context(nc.sbuf_tensor("f0_lnsq", [128, 1024], F32)),
            LNR=pLN.enter_context(nc.sbuf_tensor("f0_lnr", [33, 512], F32)),
            LNR2=pLN.enter_context(nc.sbuf_tensor("f0_lnr2", [33, 512], F32)),
            p_st=pLN.enter_context(nc.psum_tensor("f0_pst", [1, 1024], F32)),
            p_rb=pLN.enter_context(nc.psum_tensor("f0_prb", [128, 512], F32)),
            p_mb=pLN.enter_context(nc.psum_tensor("f0_pmb", [128, 512], F32)),
        )
        emit_ln(S, prog, lnb, zt)
    return S


def stage_l1_attn(prog, sctx):
    nc = prog.nc
    S = Script(prog)
    X, XB = prog.X, prog.XB
    al = lambda name, shape, dt: sctx.enter_context(nc.sbuf_tensor(name, shape, dt))
    pe, act, dve, dmaW = prog.pe, prog.act, prog.dve, prog.dmaW

    WSL = al("a1_wsl", [128, 2 * 1024], BF16)
    COS = al("a1_cos", [128, 2048], BF16)
    SIN = al("a1_sin", [128, 2048], BF16)
    QSF = al("a1_qsf", [128, 2048], F32)
    ROT = al("a1_rot", [128, 512], F32)
    TA = al("a1_ta", [128, 512], F32)
    Q1 = al("a1_q1", [128, 2048], BF16)
    K1 = al("a1_k1", [128, 2048], BF16)
    VB2 = al("a1_vb", [128, 2048], BF16)
    VT2 = al("a1_vt", [128, 2048], BF16)
    Y = al("a1_y", [128, 8 * 2048], BF16)
    EB = al("a1_eb", [128, 2048], BF16)
    PT = al("a1_pt", [128, 2048], BF16)
    MSK = al("a1_msk", [128, 4 * 512], BF16)
    NM = al("a1_nm", [128, 8], F32)
    SR = al("a1_sr", [128, 8], F32)

    qkv_hbm = prog.add_input("wqkv").rearrange("(c p) m -> p c m", p=128)
    S.add("gpsimd", E_dma(COS[:, :], prog.add_input("cos128")), inc=dmaW, inck=16, dma=True)
    S.add("gpsimd", E_dma(SIN[:, :], prog.add_input("sin128")), inc=dmaW, inck=16, dma=True)
    mask_ap = prog.add_input("masks")
    S.add("gpsimd", E_dma(MSK[:, :].rearrange("p (v q) -> p v q", v=4),
                          mask_ap.rearrange("v p q -> p v q")), inc=dmaW, inck=16, dma=True)
    wl = prog.dmaW.tok()

    with contextlib.ExitStack() as pA:
        p_a = pA.enter_context(nc.psum_tensor("a1_pa", [128, 512], F32))
        p_t = pA.enter_context(nc.psum_tensor("a1_pt2", [128, 1024], BF16))
        psA = pA.enter_context(nc.psum_tensor("a1_psA", [128, 512], F32))
        psB = pA.enter_context(nc.psum_tensor("a1_psB", [128, 512], F32))
        p_pv = pA.enter_context(nc.psum_tensor("a1_ppv", [64, 512], F32))
        SROW0 = pA.enter_context(nc.sbuf_tensor("a1_sr0", [128, 2048], F32))
        SROW1 = pA.enter_context(nc.sbuf_tensor("a1_sr1", [128, 2048], F32))
        EB2 = pA.enter_context(nc.sbuf_tensor("a1_eb2", [128, 2048], BF16))
        p2state = None

        a_free = None
        t_free = None
        s_free = None
        t2_free = None
        pv_free = None
        slab_reader = {0: None, 1: None}
        qsf_free = None
        for c in range(8):
            outs = {}
            rope_tok = {}
            for kind, col0 in [("q", c * 128), ("k", 1024 + c * 128), ("v", 2048 + c * 128)]:
                sb = 0 if kind != "v" else 1
                S.add("gpsimd",
                      E_dma(WSL[:, sb * 1024: sb * 1024 + 1024]
                            .rearrange("p (a m) -> p a m", a=8),
                            qkv_hbm[:, :, col0:col0 + 128]),
                      waits=[wl] + ([slab_reader[sb]] if slab_reader[sb] else []),
                      inc=dmaW, inck=16, dma=True)
                wtok = prog.dmaW.tok()
                for tb in range(4):
                    for dc in range(NDC):
                        g_p = S.add("tensor",
                                    E_mm(p_a[:, :],
                                         WSL[:, sb * 1024 + dc * 128: sb * 1024 + dc * 128 + 128],
                                         XB[:, dc * T + tb * 512: dc * T + tb * 512 + 512],
                                         dc == 0, dc == NDC - 1),
                                    waits=([wtok, a_free] if dc == 0 else ()),
                                    inc=(pe if dc == NDC - 1 else None))
                    if kind == "v":
                        ev = S.add("vector", E_copy(VB2[:, tb * 512: tb * 512 + 512], p_a[:, :]),
                                   waits=[g_p], fence=True)
                    else:
                        ev = S.add("scalar", E_act(QSF[:, tb * 512: tb * 512 + 512], p_a[:, :],
                                                   AF.Identity),
                                   waits=[g_p] + ([qsf_free] if tb == 0 else []), inc=act)
                    a_free = ev
                    outs.setdefault(kind, []).append(ev)
                slab_reader[sb] = g_p
                if kind == "v":
                    continue
                dst = Q1 if kind == "q" else K1
                last_ev = outs[kind][-1]
                for h0 in (0, 512, 1024, 1536):
                    hs = slice(h0, h0 + 512)
                    S.add("vector", E_copy(ROT[0:32, :], QSF[32:64, hs]),
                          waits=[last_ev] if h0 == 0 else ())
                    S.add("vector", E_copy(ROT[32:64, :], QSF[0:32, hs]))
                    S.add("vector", E_copy(ROT[64:96, :], QSF[96:128, hs]))
                    S.add("vector", E_copy(ROT[96:128, :], QSF[64:96, hs]), fence=True)
                    S.add("vector", E_tt(TA[:, :], ROT[:, :], SIN[:, hs], ALU.mult),
                          waits=[wl], fence=True)
                    qsf_free = S.add("vector", E_tt(ROT[:, :], QSF[:, hs], COS[:, hs],
                                                    ALU.mult), fence=True)
                    rope_tok[kind] = S.add("vector", E_tt(dst[:, hs], TA[:, :], ROT[:, :],
                                                          ALU.add), fence=True)
            rope_done = rope_tok["k"]
            vdone = outs["v"][-1] if "v" in outs else None
            for grp in range(2):
                for k in range(8):
                    sc8 = grp * 8 + k
                    g_t = S.add("tensor", E_tr(p_t[:, k * 128:k * 128 + 128],
                                               VB2[:, sc8 * 128:sc8 * 128 + 128],
                                               prog.IDB[:, :]),
                                waits=([vdone, t_free] if k == 0 else ()),
                                inc=(pe if k == 7 else None))
                ev = S.add("vector",
                           E_copy(VT2[:, grp * 1024: grp * 1024 + 1024], p_t[:, :]),
                           waits=[g_t], fence=True)
                t_free = ev
            vt_done = t_free
            if DEBUG and c == 0:
                for i, ap in enumerate([Q1[:, 0:2048], K1[:, 0:2048], VT2[:, 0:2048]]):
                    S.add("gpsimd", E_dma(prog.hbm[f"dbg{i}"], ap),
                          waits=[rope_done, vt_done], inc=dmaW, inck=16, dma=True)
            iters = [(64 * hh, tqc, c) for hh in range(2) for tqc in range(NTC)]
            p2state = emit_attn_p2(
                S, prog, iters=iters,
                QNf=lambda b, tqc, c: Q1[b:b + 64, tqc * 128: tqc * 128 + 128],
                KNf=lambda b, sg, c: K1[b:b + 64, sg * 512: sg * 512 + 512],
                VTf=lambda c, sc, b: VT2[:, sc * 128 + b: sc * 128 + b + 64],
                OCf=lambda c, b, tqc: Y[b:b + 64, c * 2048 + tqc * 128: c * 2048 + tqc * 128 + 128],
                hd=64, psA=psA, psB=psB, p_t=p_t, p_pv=p_pv,
                SROW=[SROW0, SROW1], EB=[EB, EB2], MSK=MSK, NM=NM, SR=SR, PTB=PT,
                start_waits=[rope_done, vt_done], state=p2state)
            s_free = p2state["sg_free"][0]

    ph2 = S.fence_all()
    prj_hbm = prog.add_input("wproj").rearrange("(c p) m -> p c m", p=128)
    with contextlib.ExitStack() as p3:
        p_o = p3.enter_context(nc.psum_tensor("a1_po", [128, 512], F32))
        lnb = dict(
            SQ=QSF[:, 0:1024],
            LNR=p3.enter_context(nc.sbuf_tensor("a1_lnr", [33, 512], F32)),
            LNR2=p3.enter_context(nc.sbuf_tensor("a1_lnr2", [33, 512], F32)),
            p_st=p3.enter_context(nc.psum_tensor("a1_pst", [1, 1024], F32)),
            p_rb=p3.enter_context(nc.psum_tensor("a1_prb", [128, 512], F32)),
            p_mb=p3.enter_context(nc.psum_tensor("a1_pmb", [128, 512], F32)),
        )
        o_free = None
        zt = None
        slab = {}
        prj_done = {}

        def prj_slab(dc, wait):
            sb = dc % 2
            S.add("gpsimd", E_dma(WSL[:, sb * 1024: sb * 1024 + 1024]
                                  .rearrange("p (a m) -> p a m", a=8),
                                  prj_hbm[:, :, dc * 128: dc * 128 + 128]),
                  waits=wait, inc=dmaW, inck=16, dma=True)
            return prog.dmaW.tok()

        slab[0] = prj_slab(0, ph2)
        slab[1] = prj_slab(1, [])
        for dc in range(NDC):
            for tb in range(4):
                for yc in range(NDC):
                    w = []
                    if yc == 0:
                        w = [slab[dc]] + ([o_free] if o_free else [])
                        if dc == 0 and tb == 0:
                            w += ph2
                    g_o = S.add("tensor",
                                E_mm(p_o[:, :],
                                     WSL[:, (dc % 2) * 1024 + yc * 128: (dc % 2) * 1024 + yc * 128 + 128],
                                     Y[:, yc * 2048 + tb * 512: yc * 2048 + tb * 512 + 512],
                                     yc == 0, yc == NDC - 1),
                                waits=w, inc=(pe if yc == NDC - 1 else None))
                prj_done[dc] = g_o
                zt = S.add("vector", E_tt(X[:, dc * T + tb * 512: dc * T + tb * 512 + 512],
                                          p_o[:, :],
                                          X[:, dc * T + tb * 512: dc * T + tb * 512 + 512],
                                          ALU.add),
                           waits=[g_o], fence=True)
                o_free = zt
            if dc + 2 < NDC:
                slab[dc + 2] = prj_slab(dc + 2, [prj_done[dc]])
        emit_ln(S, prog, lnb, zt)
    return S


def stage_l1_ffn(prog, sctx):
    nc = prog.nc
    S = Script(prog)
    X, XB = prog.X, prog.XB
    al = lambda name, shape, dt: sctx.enter_context(nc.sbuf_tensor(name, shape, dt))
    pe, act, dve, dmaW = prog.pe, prog.act, prog.dve, prog.dmaW

    AFCT = al("f1_afct", [128, 128], BF16)
    APRT = al("f1_aprt", [128, 512], BF16)
    BFCT = al("f1_bfct", [16, 4096], BF16)
    BPRT = al("f1_bprt", [16, 1024], BF16)
    SCL = al("f1_scl", [128, 8], F32)
    AM1 = al("f1_am1", [1, 4], F32)
    load_fm(S, prog, AFCT, "afc")
    load_fm(S, prog, APRT, "apr")
    load_fm(S, prog, BFCT, "bfcT")
    load_fm(S, prog, BPRT, "bprT")
    wl = prog.dmaW.tok()

    with contextlib.ExitStack() as pQ:
        WF = pQ.enter_context(nc.sbuf_tensor("f1_wf", [128, 2048], F32))
        WA = pQ.enter_context(nc.sbuf_tensor("f1_wa", [128, 2048], F32))
        WB = pQ.enter_context(nc.sbuf_tensor("f1_wb", [128, 2048], F32))
        WC = pQ.enter_context(nc.sbuf_tensor("f1_wc", [128, 2048], F32))
        WI = pQ.enter_context(nc.sbuf_tensor("f1_wi", [128, 2048], I32))
        WD = pQ.enter_context(nc.sbuf_tensor("f1_wd", [128, 2048], BF16))
        AMX = pQ.enter_context(nc.sbuf_tensor("f1_amx", [128, 2], F32))
        p_b1 = pQ.enter_context(nc.psum_tensor("f1_pb1", [128, 512], F32))
        for wi_, (wname, scr, scol) in enumerate([("wfc", "wq_fc", 0), ("wpr", "wq_pr", 3)]):
            w_hbm = prog.add_input(wname).rearrange("(c p) m -> p (c m)", p=128) \
                if False else prog.add_input(wname)
            kt, mt = w_hbm.shape
            nk = kt // 128
            flat = w_hbm.rearrange("(c p) m -> p c m", p=128)
            nslab = nk * mt // 2048

            def slab_ap(s):
                if mt >= 2048:
                    per = mt // 2048
                    return flat[:, s // per: s // per + 1,
                                (s % per) * 2048:(s % per) * 2048 + 2048]
                g = 2048 // mt
                return flat[:, s * g:(s + 1) * g, :]
            am_tok = None
            ld_tok = None
            for s in range(nslab):
                S.add("gpsimd", E_dma(WF[:, :].rearrange("p (a b) -> p a b",
                                                         a=1 if mt >= 2048 else 2048 // mt),
                                      slab_ap(s)),
                      waits=[ld_tok], inc=dmaW, inck=16, dma=True)
                ld_tok0 = prog.dmaW.tok()
                rd = S.add("vector", E_red(AMX[:, 0:1], WF[:, :], ALU.max, absval=True),
                           waits=[ld_tok0], fence=True)
                ld_tok = rd
                if s == 0:
                    am_tok = S.add("vector", E_copy(AMX[:, 1:2], AMX[:, 0:1]), fence=True)
                else:
                    am_tok = S.add("vector", E_tt(AMX[:, 1:2], AMX[:, 1:2], AMX[:, 0:1],
                                                  ALU.max), fence=True)
            pm = S.add("gpsimd", E_redC(AM1[0:1, 0:1], AMX[:, 1:2], ALU.max),
                       waits=[am_tok], fence=True)
            sc1 = S.add("vector", E_ts(AM1[0:1, 1:2], AM1[0:1, 0:1], 1.0 / 127.0, ALU.mult),
                        waits=[pm], fence=True)
            rc1 = S.add("vector", E_recip(AM1[0:1, 2:3], AM1[0:1, 1:2]), fence=True)
            g_b1 = S.add("tensor", E_mm(p_b1[:, 0:1], prog.rows("ones", 128),
                                        AM1[0:1, 2:3], True, True), waits=[rc1], inc=pe)
            cb1 = S.add("vector", E_copy(SCL[:, scol:scol + 1], p_b1[:, 0:1]),
                        waits=[g_b1], fence=True)
            g_b2 = S.add("tensor", E_mm(p_b1[:, 0:1], prog.rows("ones", 128),
                                        AM1[0:1, 1:2], True, True), waits=[cb1], inc=pe)
            cb2 = S.add("vector", E_copy(SCL[:, scol + 1:scol + 2], p_b1[:, 0:1]),
                        waits=[g_b2], fence=True)
            t2 = S.add("vector", E_ts(SCL[:, scol + 2:scol + 3], SCL[:, scol:scol + 1],
                                      2.0, ALU.mult), fence=True)
            scr_ap = prog.hbm[scr]
            st_tok = None
            for s in range(nslab):
                S.add("gpsimd", E_dma(WF[:, :].rearrange("p (a b) -> p a b",
                                                         a=1 if mt >= 2048 else 2048 // mt),
                                      slab_ap(s)),
                      waits=[ld_tok], inc=dmaW, inck=16, dma=True)
                ld_tok0 = prog.dmaW.tok()
                q1 = S.add("vector", E_ts(WA[:, :], WF[:, :], SCL[:, scol:scol + 1],
                                          ALU.mult), waits=[ld_tok0, t2, st_tok], fence=True)
                ld_tok = q1
                S.add("vector", E_copy(WI[:, :], WA[:, :]), fence=True)
                S.add("vector", E_copy(WB[:, :], WI[:, :]), fence=True)
                S.add("vector", E_tt(WC[:, :], WB[:, :], WA[:, :], ALU.is_gt), fence=True)
                S.add("vector", E_tt(WB[:, :], WB[:, :], WC[:, :], ALU.subtract), fence=True)
                S.add("vector", E_tt(WC[:, :], WA[:, :], WB[:, :], ALU.subtract), fence=True)
                qf = S.add("vector", E_stt(WD[:, :], WC[:, :], 0.5, WB[:, :],
                                           ALU.is_ge, ALU.add), fence=True)
                S.add("gpsimd", E_dma(scr_ap[:, s * 2048:(s + 1) * 2048], WD[:, :]),
                      waits=[qf], inc=dmaW, inck=16, dma=True)
                st_tok = prog.dmaW.tok()

    phQ = S.fence_all()
    with contextlib.ExitStack() as pF:
        WSL = pF.enter_context(nc.sbuf_tensor("f1_wsl", [128, 2 * 1024], BF16))
        WSL2 = pF.enter_context(nc.sbuf_tensor("f1_wsl2", [128, 2 * 4096], BF16))
        H = pF.enter_context(nc.sbuf_tensor("f1_h", [128, 32 * 512], BF16))
        G1 = pF.enter_context(nc.sbuf_tensor("f1_g1", [128, 512], BF16))
        XA1 = pF.enter_context(nc.sbuf_tensor("f1_xa1", [16, 512], BF16))
        XA2 = pF.enter_context(nc.sbuf_tensor("f1_xa2", [16, 512], BF16))
        TZ = pF.enter_context(nc.sbuf_tensor("f1_tz", [128, 512], F32))
        p_h = pF.enter_context(nc.psum_tensor("f1_ph", [128, 512], F32))
        p_x = pF.enter_context(nc.psum_tensor("f1_px", [16, 512], F32))
        p_o = pF.enter_context(nc.psum_tensor("f1_po", [128, 512], F32))
        lnb = dict(
            SQ=pF.enter_context(nc.sbuf_tensor("f1_lnsq", [128, 1024], F32)),
            LNR=pF.enter_context(nc.sbuf_tensor("f1_lnr", [33, 512], F32)),
            LNR2=pF.enter_context(nc.sbuf_tensor("f1_lnr2", [33, 512], F32)),
            p_st=pF.enter_context(nc.psum_tensor("f1_pst", [1, 1024], F32)),
            p_rb=pF.enter_context(nc.psum_tensor("f1_prb", [128, 512], F32)),
            p_mb=pF.enter_context(nc.psum_tensor("f1_pmb", [128, 512], F32)),
        )
        bfc_b = prog.small("bfc_b")
        bpr_b = prog.small("bpr_b")
        fcs = prog.hbm["wq_fc"].rearrange("p (c m) -> p c m", c=8)
        prs = prog.hbm["wq_pr"].rearrange("p (c m) -> p c m", c=32)
        h_free = None
        x_free = None
        o_free = None
        g_free = None
        zt = None
        prev_mm = {30: None, 31: None}
        prj_done = {6: None, 7: None}

        def fc_slab(fc, wait):
            sb = fc % 2
            S.add("gpsimd", E_dma(WSL[:, sb * 1024: sb * 1024 + 1024]
                                  .rearrange("p (a m) -> p a m", a=8),
                                  fcs[:, :, fc * 128: fc * 128 + 128]),
                  waits=wait, inc=dmaW, inck=16, dma=True)
            return prog.dmaW.tok()

        def pr_slab(dc, wait):
            sb = dc % 2
            S.add("gpsimd", E_dma(WSL2[:, sb * 4096: sb * 4096 + 4096]
                                  .rearrange("p (a m) -> p a m", a=32),
                                  prs[:, :, dc * 128: dc * 128 + 128]),
                  waits=wait, inc=dmaW, inck=16, dma=True)
            return prog.dmaW.tok()

        for tg in range(4):
            og = tg * 512
            for dc in range(NDC):
                g_x = S.add("tensor", E_mm(p_x[0:16, :], AFCT[:, dc * 16: dc * 16 + 16],
                                           XB[:, dc * T + og: dc * T + og + 512],
                                           dc == 0, dc == NDC - 1),
                            waits=([wl, x_free] + (phQ if tg == 0 else []) if dc == 0 else ()),
                            inc=(pe if dc == NDC - 1 else None))
            e_x1 = S.add("vector", E_ts(XA1[0:16, :], p_x[0:16, :], SCL[0:16, 2:3],
                                        ALU.mult), waits=[g_x], fence=True)
            x_free = e_x1
            slab = {}
            mm_done = {}
            slab[0] = fc_slab(0, (phQ if tg == 0 else [prev_mm[30], prev_mm[31]]))
            slab[1] = fc_slab(1, [])
            for fc in range(32):
                sb = fc % 2
                for dc in range(NDC):
                    g_h = S.add("tensor",
                                E_mm(p_h[:, :],
                                     WSL[:, sb * 1024 + dc * 128: sb * 1024 + dc * 128 + 128],
                                     XB[:, dc * T + og: dc * T + og + 512],
                                     dc == 0, False),
                                waits=([slab[fc], h_free] if dc == 0 else ()))
                g_h = S.add("tensor", E_mm(p_h[:, :], BFCT[0:16, fc * 128: fc * 128 + 128],
                                           XA1[0:16, :], False, True),
                            waits=[e_x1], inc=pe)
                mm_done[fc] = g_h
                if fc + 2 < 32:
                    slab[fc + 2] = fc_slab(fc + 2, [mm_done[fc]])
                e_g = S.add("scalar", E_act(G1[:, :], p_h[:, :], AF.Gelu,
                                            bias=bfc_b[:, fc:fc + 1], scale=SCL[:, 1:2]),
                            waits=[g_h, g_free], fence=True)
                h_free = e_g
                g_free = S.add("vector", E_copy(H[:, fc * 512: fc * 512 + 512], G1[:, :]),
                               waits=[e_g], fence=True)
            prev_mm = mm_done
            for fc in range(32):
                g_x2 = S.add("tensor", E_mm(p_x[0:16, :], APRT[:, fc * 16: fc * 16 + 16],
                                            H[:, fc * 512: fc * 512 + 512],
                                            fc == 0, fc == 31),
                             waits=([g_free, e_x1] if fc == 0 else ()),
                             inc=(pe if fc == 31 else None))
            e_x2 = S.add("vector", E_ts(XA2[0:16, :], p_x[0:16, :], SCL[0:16, 5:6],
                                        ALU.mult), waits=[g_x2], fence=True)
            x_free = e_x2
            slab = {}
            slab[0] = pr_slab(0, [mm_done[30], mm_done[31]] +
                              ([prj_done[6], prj_done[7]] if tg > 0 else []))
            slab[1] = pr_slab(1, [])
            for dc in range(NDC):
                sb = dc % 2
                for fc in range(32):
                    g_o = S.add("tensor",
                                E_mm(p_o[:, :],
                                     WSL2[:, sb * 4096 + fc * 128: sb * 4096 + fc * 128 + 128],
                                     H[:, fc * 512: fc * 512 + 512],
                                     fc == 0, False),
                                waits=([slab[dc], o_free, g_free] if fc == 0 else ()))
                g_o = S.add("tensor", E_mm(p_o[:, :], BPRT[0:16, dc * 128: dc * 128 + 128],
                                           XA2[0:16, :], False, True),
                            waits=[e_x2], inc=pe)
                prj_done[dc] = g_o
                if dc + 2 < NDC:
                    slab[dc + 2] = pr_slab(dc + 2, [prj_done[dc]])
                e_z = S.add("scalar", E_act(TZ[:, :], p_o[:, :], AF.Identity,
                                            bias=bpr_b[:, dc:dc + 1], scale=SCL[:, 4:5]),
                            waits=[g_o, zt], fence=True)
                o_free = e_z
                zt = S.add("vector", E_tt(X[:, dc * T + og: dc * T + og + 512], TZ[:, :],
                                          X[:, dc * T + og: dc * T + og + 512], ALU.add),
                           waits=[e_z], fence=True)
        emit_ln(S, prog, lnb, zt, refresh_xb=False)
    return S


def emit_attn_p2(S, prog, *, iters, QNf, KNf, VTf, OCf, hd, psA, psB, p_t, p_pv,
                SROW, EB, MSK, NM, SR, PTB, start_waits=(), state=None):
    pe, act, dve = prog.pe, prog.act, prog.dve
    n = len(iters)
    st = state if state is not None else {
        "sg_free": {0: None, 1: None}, "sr_rd": {0: None, 1: None},
        "t_free": None, "pv_free": None, "par0": 0}
    sg_free = st["sg_free"]
    sr_rd = st["sr_rd"]
    t_free = st["t_free"]
    pv_free = st["pv_free"]
    par0 = st["par0"]
    ev_done = {}

    def emit_scores(i):
        base, tqc, aux = iters[i]
        par = (par0 + i) % 2
        nsg = tqc // 4 + 1
        last = None
        for sg in range(nsg):
            ps = psA if sg % 2 == 0 else psB
            g_s = S.add("tensor",
                        E_mm(ps[:, :], QNf(base, tqc, aux), KNf(base, sg, aux),
                             True, True, tp=(base, 0)),
                        waits=[sg_free[sg % 2]] + (list(start_waits) if i == 0 and sg == 0 else []),
                        inc=pe)
            ev = S.add("scalar",
                       E_act(SROW[par][:, sg * 512: sg * 512 + 512], ps[:, :], AF.Identity),
                       waits=[g_s] + ([sr_rd[par]] if sg == 0 else []), inc=act)
            sg_free[sg % 2] = ev
            last = ev
        ev_done[i] = last

    emit_scores(0)
    for i in range(n):
        base, tqc, aux = iters[i]
        par = (par0 + i) % 2
        nsg = tqc // 4 + 1
        nsc = tqc + 1
        v = tqc % 4
        if i + 1 < n:
            emit_scores(i + 1)
        S.add("vector", E_tt(SROW[par][:, (nsg - 1) * 512:(nsg - 1) * 512 + 512],
                             SROW[par][:, (nsg - 1) * 512:(nsg - 1) * 512 + 512],
                             MSK[:, v * 512:v * 512 + 512], ALU.add),
              waits=[ev_done[i]], fence=True)
        S.add("vector", E_red(NM[:, 4 * par:4 * par + 1],
                              SROW[par][:, 0:nsc * 128], ALU.max, negate=True), fence=True)
        t_nmm = S.add("vector", E_ts(NM[:, 4 * par + 1:4 * par + 2],
                                     NM[:, 4 * par:4 * par + 1], 0.125, ALU.mult),
                      inc=dve, fence=True)
        e_e = S.add("scalar", E_act(EB[par][:, 0:nsg * 512], SROW[par][:, 0:nsg * 512],
                                    AF.Exp, bias=NM[:, 4 * par + 1:4 * par + 2], scale=0.125,
                                    accum=SR[:, 4 * par:4 * par + 1]),
                    waits=[t_nmm], inc=act, fence=True)
        S.add("vector", E_recip(SR[:, 4 * par + 1:4 * par + 2],
                                SR[:, 4 * par:4 * par + 1]),
              waits=[e_e], fence=True)
        t_nrm = S.add("vector", E_ts(EB[par][:, 0:nsg * 512], EB[par][:, 0:nsg * 512],
                                     SR[:, 4 * par + 1:4 * par + 2], ALU.mult),
                      inc=dve, fence=True)
        sr_rd[par] = t_nrm
        ptok = []
        for grp in range((nsc + 7) // 8):
            kmax = min(8, nsc - grp * 8)
            for k in range(kmax):
                sc8 = grp * 8 + k
                g_t = S.add("tensor", E_tr(p_t[:, k * 128:k * 128 + 128],
                                           EB[par][:, sc8 * 128:sc8 * 128 + 128],
                                           prog.IDB[:, :]),
                            waits=([t_nrm, t_free] if k == 0 else ()),
                            inc=(pe if k == kmax - 1 else None))
            ev = S.add("vector", E_copy(PTB[:, grp * 1024: grp * 1024 + kmax * 128],
                                        p_t[:, 0:kmax * 128]), waits=[g_t], fence=True)
            t_free = ev
            ptok.append(ev)
        for sc8 in range(nsc):
            w = []
            if sc8 == 0:
                w = ptok + [pv_free]
            g_pv = S.add("tensor",
                         E_mm(p_pv[0:hd, 0:128], VTf(aux, sc8, base),
                              PTB[:, sc8 * 128: sc8 * 128 + 128],
                              sc8 == 0, sc8 == nsc - 1),
                         waits=w, inc=(pe if sc8 == nsc - 1 else None))
        pv_free = S.add("vector", E_copy(OCf(aux, base, tqc), p_pv[0:hd, 0:128]),
                        waits=[g_pv], fence=True)
    st.update(sg_free=sg_free, sr_rd=sr_rd, t_free=t_free, pv_free=pv_free,
              par0=(par0 + n) % 2)
    st["last"] = pv_free
    return st


def stage_lntest(prog, sctx):
    nc = prog.nc
    S = Script(prog)
    lnb = dict(
        SQ=sctx.enter_context(nc.sbuf_tensor("lt_sq", [128, 1024], F32)),
        LNR=sctx.enter_context(nc.sbuf_tensor("lt_lnr", [33, 512], F32)),
        LNR2=sctx.enter_context(nc.sbuf_tensor("lt_lnr2", [33, 512], F32)),
        p_st=sctx.enter_context(nc.psum_tensor("lt_pst", [1, 1024], F32)),
        p_rb=sctx.enter_context(nc.psum_tensor("lt_prb", [128, 512], F32)),
        p_mb=sctx.enter_context(nc.psum_tensor("lt_pmb", [128, 512], F32)),
    )
    dbg = None
    if DEBUG:
        dbg = sctx.enter_context(nc.sbuf_tensor("lt_dbg", [65, 2048], F32))
    emit_ln(S, prog, lnb, None, dbg=dbg)
    if DEBUG:
        S.add("gpsimd", E_dma(prog.hbm["dbg0"][0:65, :], dbg[0:65, :]),
              waits=[prog.dve.tok()] + [prog.bmp[e].tok() for e in ENGS], inc=prog.dmaW, inck=16, dma=True)
    return S



STAGE_BUILDERS = {}


def _register():
    STAGE_BUILDERS["l0a"] = stage_l0_attn
    STAGE_BUILDERS["l0f"] = stage_l0_ffn
    STAGE_BUILDERS["l1a"] = stage_l1_attn
    STAGE_BUILDERS["l1f"] = stage_l1_ffn
    STAGE_BUILDERS["lntest"] = stage_lntest
    for nm, fn in [("l0f", "stage_l0_ffn"), ("l1a", "stage_l1_attn"),
                   ("l1f", "stage_l1_ffn")]:
        if fn in globals():
            STAGE_BUILDERS[nm] = globals()[fn]


def build_group(meta, stage_names):
    prog = Prog(meta, [])
    _register()
    run_stage(prog, stage_load(prog))
    for nm in stage_names:
        with contextlib.ExitStack() as sctx:
            s = STAGE_BUILDERS[nm](prog, sctx)
            run_stage(prog, s)
    run_stage(prog, stage_store(prog))
    prog.ctx.close()
    return prog


def run_group(prog, meta, xs):
    in_maps = []
    for c in range(NCORES):
        im = {}
        for n in prog.input_names:
            im[n] = xs[c] if n == "x_in" else meta[n]
        in_maps.append(im)
    res = run_bass_kernel_spmd(prog.nc, in_maps, core_ids=list(range(NCORES)))
    return [np.ascontiguousarray(np.asarray(res.results[c]["x_out"], np.float32))
            for c in range(NCORES)]


GROUPS = [["l0a", "l0f", "l1a", "l1f"]]


def kernel(**inputs):
    meta = host_prepare(inputs)
    x = np.asarray(inputs["L0_attn_x"], np.float32)
    xs = [np.ascontiguousarray(x[c].T) for c in range(NCORES)]
    for g in GROUPS:
        prog = build_group(meta, g)
        xs = run_group(prog, meta, xs)
    return np.stack([c.T for c in xs]).astype(np.float32)


# version 04
# speedup vs baseline: 1.0542x; 1.0190x over previous
import contextlib
import numpy as np
import ml_dtypes
import concourse.bass as bass
import concourse.mybir as mybir
from concourse.bass_utils import run_bass_kernel_spmd

F32 = mybir.dt.float32
BF16 = mybir.dt.bfloat16
I32 = mybir.dt.int32
AF = mybir.ActivationFunctionType
ALU = mybir.AluOpType
AX = mybir.AxisListType

NCORES = 8
DEBUG = False
L0F_PART = 99
L0F_G = 99
T = 2048
D = 1024
NTB = T // 512
NTC = T // 128
NDC = D // 128
BIG = 1.0e4
NEG = -1.0e30
EPS_RMS = 1e-6
EPS_LN = 1e-5



def _perm_l0_q():
    cols = []
    for c in range(4):
        for j in range(4):
            h = 4 * c + j
            cols.extend(h * 64 + r for r in range(32))
    for c in range(4):
        for j in range(4):
            h = 4 * c + j
            cols.extend(h * 64 + 32 + r for r in range(32))
    return np.array(cols)


def _perm_l1_qk():
    cols = []
    for c in range(8):
        for hh in (2 * c, 2 * c + 1):
            cols.extend(hh * 64 + 2 * i for i in range(32))
            cols.extend(hh * 64 + 2 * i + 1 for i in range(32))
    return np.array(cols)


def _rope_tables():
    dh = 64
    inv = 1.0 / (10000.0 ** (np.arange(0, dh, 2, dtype=np.float64) / dh))
    tt = np.arange(T, dtype=np.float64)[:, None] * inv[None, :]
    cos = np.cos(tt).astype(np.float32)
    sin = np.sin(tt).astype(np.float32)
    cos128 = np.tile(np.ascontiguousarray(cos.T), (4, 1))
    sin128 = np.tile(np.ascontiguousarray(sin.T), (4, 1))
    sign = np.ones((128, 1), np.float32)
    sign[0:32] = -1.0
    sign[64:96] = -1.0
    return np.ascontiguousarray(cos128), np.ascontiguousarray(sin128 * sign)


def _masks():
    out = np.zeros((4, 128, 512), np.float32)
    for v in range(4):
        for p in range(128):
            out[v, p, v * 128 + p + 1:] = NEG
    return out


def host_prepare(inp):
    f32 = lambda a: np.ascontiguousarray(np.asarray(a, dtype=np.float32))
    m = {}
    m["wq"] = f32(np.asarray(inp["L0_attn_Wq"])[:, _perm_l0_q()])
    m["wkv"] = f32(inp["L0_attn_Wkv"])
    m["wk"] = f32(inp["L0_attn_Wk"])
    m["wv"] = f32(inp["L0_attn_Wv"])
    perm_o = np.array([h * 64 + c for h in range(16) for c in range(32)])
    m["wo"] = f32(np.asarray(inp["L0_attn_Wo"])[perm_o, :])
    m["wi"] = f32(np.asarray(inp["L0_ffn_wi"]).T)
    m["wo2"] = f32(np.asarray(inp["L0_ffn_wo"]).T)
    m["acat"] = f32(np.asarray(inp["L0_ffn_lora_A"]).reshape(128, 1024).T)
    m["bcat"] = f32(np.asarray(inp["L0_ffn_lora_B"]).transpose(0, 2, 1).reshape(128, 4096))
    m["gw"] = f32(np.asarray(inp["L0_ffn_gate_w"]).T)
    wqkv = np.asarray(inp["L1_attn_Wqkv"])
    pq = _perm_l1_qk()
    m["wqkv"] = f32(np.concatenate(
        [wqkv[:, 0:1024][:, pq], wqkv[:, 1024:2048][:, pq], wqkv[:, 2048:3072]], axis=1))
    m["wproj"] = f32(inp["L1_attn_Wproj"])
    m["wfc"] = f32(np.asarray(inp["L1_ffn_W_fc"]).T)
    m["wpr"] = f32(np.asarray(inp["L1_ffn_W_proj"]).T)
    m["afc"] = f32(np.asarray(inp["L1_ffn_A_fc"]).T)
    m["bfcT"] = f32(np.asarray(inp["L1_ffn_B_fc"]).T)
    m["apr"] = f32(np.asarray(inp["L1_ffn_A_proj"]).T)
    m["bprT"] = f32(np.asarray(inp["L1_ffn_B_proj"]).T)

    sc = {}
    cols = []

    def put(name, arr):
        arr = np.asarray(arr, np.float32)
        if arr.ndim == 1:
            arr = arr[:, None]
        pad = np.zeros((128, arr.shape[1]), np.float32)
        pad[: arr.shape[0]] = arr
        sc[name] = (sum(c.shape[1] for c in cols), arr.shape[1])
        cols.append(pad)

    put("qg_rope", np.tile(np.asarray(inp["L0_attn_q_g"])[:32], 4))
    put("qg_pad", np.tile(np.asarray(inp["L0_attn_q_g"])[32:], 4))
    put("kg", np.tile(np.asarray(inp["L0_attn_k_g"]), 4))
    put("gate_b", inp["L0_ffn_gate_b"])
    put("bi", np.asarray(inp["L0_ffn_bi"]).reshape(32, 128).T)
    put("bfc_b", np.asarray(inp["L1_ffn_b_fc"]).reshape(32, 128).T)
    put("bpr_b", np.asarray(inp["L1_ffn_b_proj"]).reshape(8, 128).T)
    m["small"] = f32(np.concatenate(cols, axis=1))
    m["small_cols"] = sc

    rows = np.zeros((1, 1152), np.float32)
    rows[0, 0:1024] = np.asarray(inp["L0_ffn_bo"], np.float32)
    rows[0, 1024:1152] = 1.0
    m["rows"] = rows
    m["id128"] = np.eye(128, dtype=np.float32)
    m["iota"] = np.concatenate(
        [np.tile(np.arange(8, dtype=np.float32), (128, 1)),
         np.tile(np.arange(8, dtype=np.float32) + BIG, (128, 1))], axis=1)
    e4 = np.zeros((128, 4), np.float32)
    for p in range(128):
        e4[p, p // 32] = 1.0
    m["e4"] = e4
    m["e4t"] = f32(e4.T)
    e16 = np.zeros((8, 128), np.float32)
    for e in range(8):
        e16[e, e * 16:(e + 1) * 16] = 1.0
    m["e16"] = e16
    m["masks"] = _masks()
    cos128, sin128s = _rope_tables()
    m["cos128"] = cos128
    m["sin128"] = sin128s
    return m


CONST_NAMES = ["small", "rows", "id128", "iota", "e4", "e4t", "masks",
               "cos128", "sin128", "e16"]


ENGS = ["gpsimd", "tensor", "scalar", "vector", "sync"]


class Sem:
    def __init__(self, ctx, nc, name):
        self.h = ctx.enter_context(nc.semaphore(name))
        self.n = 0

    def tok(self):
        return (self, self.n)


class Script:
    def __init__(self, prog):
        self.prog = prog
        self.ops = {e: [] for e in ENGS}

    def add(self, eng, emit, waits=(), inc=None, inck=1, fence=False, dma=False):
        incs = []
        tok = None
        fw = None
        if fence:
            b = self.prog.bmp[eng]
            b.n += 1
            incs.append((b, 1))
            fw = (b, b.n)
            tok = fw
        elif inc is not None:
            inc.n += inck
            incs.append((inc, inck))
            tok = (inc, inc.n)
        self.ops[eng].append((emit, [w for w in waits if w is not None], incs, fw, dma, tok))
        return tok

    def fence_all(self):
        p = self.prog
        return ([p.pe.tok(), p.act.tok(), p.dve.tok(), p.pool.tok(),
                 p.dmaW.tok(), p.dmaS.tok()] +
                [p.bmp[e].tok() for e in ENGS])

    def fn_for(self, eng):
        ops = self.ops[eng]
        eng_name = eng
        if not ops:
            return None

        def f(e):
            pending = None
            last_tok = None
            last_nondma_i = None
            for i in range(len(ops) - 1, -1, -1):
                if not ops[i][4]:
                    last_nondma_i = i
                    break
            for i, (emit, waits, incs, fw, dma, tok) in enumerate(ops):
                if pending is not None:
                    e.wait_ge(pending[0].h, pending[1])
                pending = fw
                for s, v in waits:
                    e.wait_ge(s.h, v)
                ins = emit(e)
                if i == last_nondma_i and not incs:
                    b = self.prog.bmp[eng_name]
                    b.n += 1
                    incs = [(b, 1)]
                    tok = (b, b.n)
                for s, k in incs:
                    ins.then_inc(s.h, k)
                if i == last_nondma_i:
                    last_tok = tok
            return last_tok
        return f


class Prog:
    def __init__(self, meta, weight_names):
        self.nc = bass.Bass("TRN2", target_bir_lowering=False, debug=False)
        self.ctx = contextlib.ExitStack()
        self.meta = meta
        nc = self.nc
        self.hbm = {}
        self.hbm["x_in"] = nc.dram_tensor("x_in", [D, T], F32, kind="ExternalInput").ap()
        self.hbm["x_out"] = nc.dram_tensor("x_out", [D, T], F32, kind="ExternalOutput").ap()
        self.input_names = ["x_in"]
        for name in weight_names:
            self.add_input(name)
        self.hbm["scr"] = nc.dram_tensor("scr", [16, 128], F32, kind="Internal").ap()
        self.hbm["wq_fc"] = nc.dram_tensor("wq_fc", [128, 8 * 4096], BF16, kind="Internal").ap()
        self.hbm["wq_pr"] = nc.dram_tensor("wq_pr", [128, 32 * 1024], BF16, kind="Internal").ap()
        if DEBUG:
            for i in range(4):
                self.hbm[f"dbg{i}"] = nc.dram_tensor(f"dbg{i}", [128, 2048], F32,
                                                     kind="ExternalOutput").ap()
        c = self.ctx
        self.X = c.enter_context(nc.sbuf_tensor("gX", [128, 8 * T], F32))
        self.XB = c.enter_context(nc.sbuf_tensor("gXB", [128, 8 * T], BF16))
        scn = meta["small"].shape[1]
        self.SMALL = c.enter_context(nc.sbuf_tensor("gSMALL", [128, scn], F32))
        self.ROWS = c.enter_context(nc.sbuf_tensor("gROWS", [1, 1152], F32))
        self.ID = c.enter_context(nc.sbuf_tensor("gID", [128, 128], F32))
        self.IDB = c.enter_context(nc.sbuf_tensor("gIDB", [128, 128], BF16))
        self.IOTA = c.enter_context(nc.sbuf_tensor("gIOTA", [128, 16], F32))
        self.E4 = c.enter_context(nc.sbuf_tensor("gE4", [128, 4], F32))
        self.E4T = c.enter_context(nc.sbuf_tensor("gE4T", [4, 128], F32))
        self.E16 = c.enter_context(nc.sbuf_tensor("gE16", [8, 128], BF16))
        self.ONEC = c.enter_context(nc.sbuf_tensor("gONEC", [128, 1], F32))
        self.bar = Sem(c, nc, "s_bar")
        self.dmaW = Sem(c, nc, "s_dmaW")
        self.dmaS = Sem(c, nc, "s_dmaS")
        self.pe = Sem(c, nc, "s_pe")
        self.act = Sem(c, nc, "s_act")
        self.dve = Sem(c, nc, "s_dve")
        self.pool = Sem(c, nc, "s_pool")
        self.bmp = {e: Sem(c, nc, "s_bmp_" + e) for e in ENGS}

    def add_input(self, name):
        if name not in self.hbm:
            a = self.meta[name]
            self.hbm[name] = self.nc.dram_tensor(
                name, list(a.shape), F32, kind="ExternalInput").ap()
            self.input_names.append(name)
        return self.hbm[name]

    def small(self, name):
        off, w = self.meta["small_cols"][name]
        return self.SMALL[:, off:off + w]

    def rows(self, name, n):
        offs = {"bo": 0, "ones": 1024, "twos": 1152}
        return self.ROWS[0:1, offs[name]:offs[name] + n]


def run_stage(prog, script):
    bar = prog.bar
    enter = bar.n
    wN, sN = prog.dmaW.n, prog.dmaS.n

    with prog.nc.Block() as block:
        for name in ENGS:
            fn = script.fn_for(name)

            def emit(eng, fn=fn):
                eng.wait_ge(bar.h, enter)
                last_tok = fn(eng) if fn else None
                if last_tok is not None:
                    eng.wait_ge(last_tok[0].h, last_tok[1])
                eng.wait_ge(prog.dmaW.h, wN)
                eng.wait_ge(prog.dmaS.h, sN)
                eng.sem_inc(bar.h, 1)

            getattr(block, name)(emit)
    bar.n += 5




def E_dma(out, in_):
    return lambda e: e.dma_start(out=out, in_=in_)


def E_mm(out, lhsT, rhs, start, stop, tp=None):
    if tp is None:
        return lambda e: e.matmul(out, lhsT, rhs, start=start, stop=stop)
    return lambda e: e.matmul(out, lhsT, rhs, start=start, stop=stop,
                              tile_position=tp, skip_group_check=True)


def E_tr(out, in_, ident):
    return lambda e: e.transpose(out, in_, ident)


def E_act(out, in_, func, bias=0.0, scale=1.0, accum=None):
    if accum is None:
        return lambda e: e.activation(out, in_, func, bias=bias, scale=scale)
    return lambda e: e.activation(out, in_, func, bias=bias, scale=scale,
                                  accum_out=accum)


def E_tt(out, a, b, op):
    return lambda e: e.tensor_tensor(out, a, b, op)


def E_ts(out, a, s1, op0, s2=None, op1=None):
    if op1 is None:
        return lambda e: e.tensor_scalar(out, a, s1, None, op0)
    return lambda e: e.tensor_scalar(out, a, s1, s2, op0, op1)


def E_stt(out, in0, s, in1, op0, op1):
    return lambda e: e.scalar_tensor_tensor(out, in0, s, in1, op0, op1)


def E_copy(out, in_):
    return lambda e: e.tensor_copy(out, in_)


def E_red(out, in_, op, negate=False, absval=False):
    return lambda e: e.tensor_reduce(out, in_, AX.X, op,
                                     apply_absolute_value=absval, negate=negate)


def E_redC(out, in_, op):
    return lambda e: e.tensor_reduce(out, in_, AX.C, op)


def E_recip(out, in_):
    return lambda e: e.reciprocal(out, in_)


def E_memset(ap, v):
    return lambda e: e.memset(ap, v)


def load_fm(S, prog, sb, name, col0=0):
    hbm_ap = prog.add_input(name)
    ktot, mtot = hbm_ap.shape
    nk = (ktot + 127) // 128
    if ktot <= 128:
        return S.add("gpsimd", E_dma(sb[0:ktot, col0:col0 + mtot], hbm_ap),
                     inc=prog.dmaW, inck=16, dma=True)
    return S.add("gpsimd",
                 E_dma(sb[:, col0:col0 + nk * mtot].rearrange("p (c m) -> p c m", c=nk),
                       hbm_ap.rearrange("(c p) m -> p c m", p=128)),
                 inc=prog.dmaW, inck=16, dma=True)




def emit_ln(S, prog, sb, z_ready, refresh_xb=True, dbg=None):
    pe, act, dve = prog.pe, prog.act, prog.dve
    X, XB = prog.X, prog.XB
    p_st, p_rb, p_mb = sb["p_st"], sb["p_rb"], sb["p_mb"]
    SQ, LNR, LNR2 = sb["SQ"], sb["LNR"], sb["LNR2"]
    st_free = []
    b_free = []
    sq_war = None
    for tb in range(NTB):
        o = tb * 512
        prev_mmq = {0: None, 1: None}
        g_sum = g_sq = None
        for dc in range(NDC):
            xap = X[:, dc * T + o: dc * T + o + 512]
            h = (dc % 2) * 512
            w = ([z_ready, sq_war] if dc == 0 else []) + \
                ([prev_mmq[dc % 2]] if prev_mmq[dc % 2] is not None else [])
            sq_tok = S.add("vector", E_tt(SQ[:, h:h + 512], xap, xap, ALU.mult),
                           waits=w, fence=True)
            g_sum = S.add("tensor", E_mm(p_st[0:1, 0:512], prog.ONEC[:, :], xap,
                                         dc == 0, dc == NDC - 1),
                          waits=([z_ready] + st_free if dc == 0 else ()),
                          inc=pe)
            g_sq = S.add("tensor", E_mm(p_st[0:1, 512:1024], prog.ONEC[:, :],
                                        SQ[:, h:h + 512], dc == 0, dc == NDC - 1),
                         waits=[sq_tok], inc=pe)
            prev_mmq[dc % 2] = g_sq
        rstd_ap = LNR[0:1, 0:512]
        m_ap = LNR2[0:1, 0:512]
        ms_ap = LNR[32:33, 0:512]
        m2_ap = LNR2[32:33, 0:512]
        t_m = S.add("vector", E_ts(m_ap, p_st[0:1, 0:512], 1.0 / D, ALU.mult),
                    waits=[g_sum], fence=True)
        t_ms = S.add("vector", E_ts(ms_ap, p_st[0:1, 512:1024], 1.0 / D, ALU.mult),
                     waits=[g_sq], fence=True)
        st_free = [t_m, t_ms]
        S.add("vector", E_tt(m2_ap, m_ap, m_ap, ALU.mult), fence=True)
        S.add("vector", E_tt(ms_ap, ms_ap, m2_ap, ALU.subtract), fence=True)
        t_var = S.add("vector", E_ts(ms_ap, ms_ap, EPS_LN, ALU.add),
                      inc=dve, fence=True)
        S.add("scalar", E_act(m2_ap, ms_ap, AF.Ln), waits=[t_var], fence=True)
        t_re = S.add("scalar", E_act(ms_ap, m2_ap, AF.Exp, scale=-0.5),
                     fence=True)
        t_rstd = S.add("vector", E_copy(rstd_ap, ms_ap), waits=[t_re], fence=True)
        if dbg is not None:
            S.add("vector", E_copy(dbg[0:1, o:o + 512], m_ap), waits=[t_m])
            S.add("vector", E_copy(dbg[32:33, o:o + 512], ms_ap), waits=[t_var])
            S.add("vector", E_copy(dbg[64:65, o:o + 512], rstd_ap), waits=[t_rstd], fence=True)
        S.add("tensor", E_mm(p_rb[:, :], prog.rows("ones", 128), rstd_ap, True, True),
              waits=[t_rstd] + b_free)
        g_b = S.add("tensor", E_mm(p_mb[:, :], prog.rows("ones", 128), m_ap, True, True),
                    waits=[t_m], inc=pe)
        b_free = []
        for dc in range(NDC):
            xap = X[:, dc * T + o: dc * T + o + 512]
            h = (dc % 2) * 512
            S.add("vector", E_tt(SQ[:, h:h + 512], xap, p_mb[:, :], ALU.subtract),
                  waits=([g_b] if dc == 0 else ()))
            n2 = S.add("vector", E_tt(xap, SQ[:, h:h + 512], p_rb[:, :], ALU.mult),
                       inc=dve)
            if refresh_xb:
                S.add("vector", E_copy(XB[:, dc * T + o: dc * T + o + 512], xap),
                      inc=dve)
            b_free = [n2]
            sq_war = n2
    return prog.dve.tok()


def ln_buffers(prog, sctx, tag):
    nc = prog.nc
    return dict(
        SQ=sctx.enter_context(nc.sbuf_tensor(tag + "_lnsq", [128, 1024], F32)),
        LNR=sctx.enter_context(nc.sbuf_tensor(tag + "_lnr", [33, 512], F32)),
        LNR2=sctx.enter_context(nc.sbuf_tensor(tag + "_lnr2", [33, 512], F32)),
        p_st=sctx.enter_context(nc.psum_tensor(tag + "_pst", [1, 1024], F32)),
        p_rb=sctx.enter_context(nc.psum_tensor(tag + "_prb2", [128, 512], F32)),
        p_mb=sctx.enter_context(nc.psum_tensor(tag + "_pmb", [128, 512], F32)),
    )




def stage_load(prog):
    S = Script(prog)
    m = prog.meta
    dmaS, dmaW, dve = prog.dmaS, prog.dmaW, prog.dve
    S.add("sync", E_dma(prog.X[:, :].rearrange("p (c t) -> p c t", c=NDC),
                        prog.hbm["x_in"].rearrange("(c p) t -> p c t", p=128)),
          inc=dmaS, inck=16, dma=True)
    for hbname, sb in [("small", prog.SMALL), ("rows", prog.ROWS), ("id128", prog.ID),
                       ("iota", prog.IOTA), ("e4", prog.E4), ("e4t", prog.E4T)]:
        ap = prog.add_input(hbname)
        rr = sb[0:ap.shape[0], 0:ap.shape[1]]
        S.add("sync", E_dma(rr, ap), inc=dmaS, inck=16, dma=True)
    S.add("gpsimd", E_dma(prog.IDB[:, :], prog.add_input("id128")), inc=dmaW, inck=16, dma=True)
    S.add("gpsimd", E_dma(prog.E16[0:8, :], prog.add_input("e16")), inc=dmaW, inck=16, dma=True)
    xw = dmaS.tok()
    S.add("vector", E_memset(prog.ONEC[:, :], 1.0))
    S.add("vector", E_copy(prog.XB[:, :], prog.X[:, :]), waits=[xw], inc=dve)
    return S


def stage_store(prog):
    S = Script(prog)
    S.add("sync", E_dma(prog.hbm["x_out"].rearrange("(c p) t -> p c t", p=128),
                        prog.X[:, :].rearrange("p (c t) -> p c t", c=NDC)),
          inc=prog.dmaS, inck=16, dma=True)
    tok = prog.dmaS.tok()
    S.add("sync", lambda e: e.nop(), waits=[tok])
    return S




def stage_l0_attn(prog, sctx):
    nc = prog.nc
    S = Script(prog)
    X, XB = prog.X, prog.XB
    al = lambda name, shape, dt: sctx.enter_context(nc.sbuf_tensor(name, shape, dt))
    pe, act, dve, dmaW = prog.pe, prog.act, prog.dve, prog.dmaW

    WQ = al("a0_wq", [128, 8 * 1024], BF16)
    WKV = al("a0_wkv", [128, 8 * 128], BF16)
    WK = al("a0_wk", [128, 512], BF16)
    WV = al("a0_wv", [128, 512], BF16)
    WO = al("a0_wo", [128, 4 * 1024], BF16)
    QN = al("a0_qn", [128, 4 * 2048], BF16)
    KN = al("a0_kn", [128, 4 * 2048], BF16)
    VT = al("a0_vt", [128, 16 * 512], BF16)
    MSK = al("a0_msk", [128, 4 * 512], BF16)
    NM = al("a0_nm", [128, 8], F32)
    SR = al("a0_sr", [128, 8], F32)
    OC = XB
    EB = XB[:, 8192:10240]
    EBx = XB[:, 10240:12288]
    PT = XB[:, 12288:14336]

    psc = lambda name, shape, dt=F32: sctx.enter_context(nc.psum_tensor(name, shape, dt))


    for sb, name in [(WQ, "wq"), (WKV, "wkv"), (WK, "wk"), (WV, "wv"), (WO, "wo")]:
        load_fm(S, prog, sb, name)
    mask_ap = prog.add_input("masks")
    S.add("gpsimd", E_dma(MSK[:, :].rearrange("p (v q) -> p v q", v=4),
                          mask_ap.rearrange("v p q -> p v q")), inc=dmaW, inck=16, dma=True)
    wl = prog.dmaW.tok()

    qg_rope = prog.small("qg_rope")
    kg = prog.small("kg")

    with contextlib.ExitStack() as p1:
        QF = p1.enter_context(nc.sbuf_tensor("a0_qf", [128, 512], F32))
        QP = p1.enter_context(nc.sbuf_tensor("a0_qp", [128, 512], F32))
        VB = p1.enter_context(nc.sbuf_tensor("a0_vb", [128, 2048], BF16))
        KVB = p1.enter_context(nc.sbuf_tensor("a0_kvb", [128, 512], BF16))
        SQ2 = p1.enter_context(nc.sbuf_tensor("a0_sq", [128, 1024], F32))
        RST = p1.enter_context(nc.sbuf_tensor("a0_rst", [4, 512], F32))
        RSF = p1.enter_context(nc.sbuf_tensor("a0_rsf", [4, 512], F32))
        p_a = p1.enter_context(nc.psum_tensor("a0_pa", [128, 512], F32))
        p_b = p1.enter_context(nc.psum_tensor("a0_pb", [128, 512], F32))
        p_ss = p1.enter_context(nc.psum_tensor("a0_pss", [4, 512], F32))
        p_rb = p1.enter_context(nc.psum_tensor("a0_prb", [128, 512], F32))
        p_t = p1.enter_context(nc.psum_tensor("a0_ptr", [128, 1024], BF16))

        a_free = b_free = ss_free = rb_free = t_free = None
        sq_free = rst_free = qf_free = kvb_free = None
        for c in range(4):
            vdone = None
            for tb in range(4):
                oq = tb * 512
                xsl = lambda dc: XB[:, dc * T + oq: dc * T + oq + 512]
                for dc in range(NDC):
                    g_qr = S.add("tensor",
                                 E_mm(p_a[:, :], WQ[:, dc * 1024 + c * 128: dc * 1024 + c * 128 + 128],
                                      xsl(dc), dc == 0, dc == NDC - 1),
                                 waits=([wl, a_free] if dc == 0 else ()),
                                 inc=(pe if dc == NDC - 1 else None))
                for dc in range(NDC):
                    g_qp = S.add("tensor",
                                 E_mm(p_b[:, :], WQ[:, dc * 1024 + (4 + c) * 128: dc * 1024 + (4 + c) * 128 + 128],
                                      xsl(dc), dc == 0, dc == NDC - 1),
                                 waits=([b_free] if dc == 0 else ()),
                                 inc=(pe if dc == NDC - 1 else None))
                e_qr = S.add("scalar", E_act(QF[:, 0:512], p_a[:, :], AF.Identity),
                             waits=[g_qr, qf_free], inc=act)
                s_qr = S.add("vector", E_tt(SQ2[:, 0:512], QF[:, 0:512], QF[:, 0:512], ALU.mult),
                             waits=[e_qr, sq_free], fence=True)
                e_qp = S.add("vector", E_copy(QP[:, 0:512], p_b[:, :]),
                             waits=[g_qp], fence=True)
                s_qp = S.add("vector", E_tt(SQ2[:, 512:1024], QP[:, 0:512], QP[:, 0:512], ALU.mult),
                             fence=True)
                a_free = s_qr
                b_free = e_qp
                S.add("tensor", E_mm(p_ss[0:4, :], prog.E4[:, :], SQ2[:, 0:512], True, False),
                      waits=[s_qr, s_qp, ss_free])
                g_ss = S.add("tensor", E_mm(p_ss[0:4, :], prog.E4[:, :], SQ2[:, 512:1024],
                                            False, True), inc=pe)
                v1 = S.add("vector", E_ts(RSF[0:4, :], p_ss[0:4, :], 1.0 / 64, ALU.mult,
                                          EPS_RMS, ALU.add),
                           waits=[g_ss, rst_free], inc=dve, fence=True)
                r1 = S.add("scalar", E_act(RSF[0:4, :], RSF[0:4, :], AF.Ln),
                           waits=[v1], fence=True)
                r2a = S.add("scalar", E_act(RSF[0:4, :], RSF[0:4, :], AF.Exp, scale=-0.5),
                            fence=True)
                r2 = S.add("vector", E_copy(RST[0:4, :], RSF[0:4, :]),
                           waits=[r2a, rst_free], fence=True)
                ss_free = v1
                g_rb = S.add("tensor", E_mm(p_rb[:, :], prog.E4T[0:4, :], RST[0:4, :],
                                            True, True), waits=[r2, rb_free], inc=pe)
                n_q = S.add("vector",
                            E_stt(QN[:, c * 2048 + oq: c * 2048 + oq + 512],
                                  QF[:, 0:512], qg_rope, p_rb[:, :],
                                  ALU.mult, ALU.mult),
                            waits=[g_rb, e_qr], inc=dve, fence=True)
                rb_free = n_q
                for dc in range(NDC):
                    g_kv = S.add("tensor",
                                 E_mm(p_a[:, :], WKV[:, dc * 128: dc * 128 + 128],
                                      xsl(dc), dc == 0, dc == NDC - 1),
                                 waits=([e_qr] if dc == 0 else ()),
                                 inc=(pe if dc == NDC - 1 else None))
                e_kv = S.add("vector", E_copy(KVB[:, 0:512], p_a[:, :]),
                             waits=[g_kv, kvb_free], fence=True)
                a_free = e_kv
                g_k = S.add("tensor", E_mm(p_b[:, :], WK[:, c * 128: c * 128 + 128],
                                           KVB[:, 0:512], True, True),
                            waits=[e_kv, b_free, n_q], inc=pe)
                e_k = S.add("scalar", E_act(QF[:, 0:512], p_b[:, :], AF.Identity),
                            waits=[g_k], inc=act)
                s_k = S.add("vector", E_tt(SQ2[:, 0:512], QF[:, 0:512], QF[:, 0:512], ALU.mult),
                            waits=[g_ss, e_k], fence=True)
                b_free = s_k
                g_ss2 = S.add("tensor", E_mm(p_ss[0:4, :], prog.E4[:, :], SQ2[:, 0:512],
                                             True, True), waits=[s_k, ss_free], inc=pe)
                sq_free = g_ss2
                v2 = S.add("vector", E_ts(RSF[0:4, :], p_ss[0:4, :], 1.0 / 32, ALU.mult,
                                          EPS_RMS, ALU.add),
                           waits=[g_ss2, g_rb], inc=dve, fence=True)
                r3 = S.add("scalar", E_act(RSF[0:4, :], RSF[0:4, :], AF.Ln),
                           waits=[v2], fence=True)
                r4a = S.add("scalar", E_act(RSF[0:4, :], RSF[0:4, :], AF.Exp, scale=-0.5),
                            fence=True)
                r4 = S.add("vector", E_copy(RST[0:4, :], RSF[0:4, :]),
                           waits=[r4a, g_rb], fence=True)
                ss_free = v2
                g_rb2 = S.add("tensor", E_mm(p_rb[:, :], prog.E4T[0:4, :], RST[0:4, :],
                                             True, True), waits=[r4, rb_free], inc=pe)
                rst_free = g_rb2
                n_k = S.add("vector",
                            E_stt(KN[:, c * 2048 + oq: c * 2048 + oq + 512],
                                  QF[:, 0:512], kg, p_rb[:, :],
                                  ALU.mult, ALU.mult),
                            waits=[g_rb2, e_k], inc=dve, fence=True)
                rb_free = n_k
                qf_free = n_k
                g_v = S.add("tensor", E_mm(p_b[:, :], WV[:, c * 128: c * 128 + 128],
                                           KVB[:, 0:512], True, True),
                            waits=[b_free], inc=pe)
                e_v = S.add("vector", E_copy(VB[:, oq:oq + 512], p_b[:, :]),
                            waits=[g_v], fence=True)
                b_free = e_v
                kvb_free = g_v
                vdone = e_v
            for grp in range(2):
                for k in range(8):
                    sc8 = grp * 8 + k
                    g_t = S.add("tensor", E_tr(p_t[:, k * 128:k * 128 + 128],
                                               VB[:, sc8 * 128:sc8 * 128 + 128],
                                               prog.IDB[:, :]),
                                waits=([vdone, t_free] if k == 0 else ()),
                                inc=(pe if k == 7 else None))
                vt3 = VT[:, :].rearrange("p (s c r) -> p s c r", s=16, c=4)
                ev = S.add("vector",
                           E_copy(vt3[:, grp * 8:grp * 8 + 8, c, :],
                                  p_t[:, :].rearrange("p (s r) -> p s r", s=8)),
                           waits=[g_t], inc=dve)
                t_free = ev

    ph1 = S.fence_all()
    with contextlib.ExitStack() as p2:
        psA = p2.enter_context(nc.psum_tensor("a0_psA", [128, 512], F32))
        psB = p2.enter_context(nc.psum_tensor("a0_psB", [128, 512], F32))
        p_t2 = p2.enter_context(nc.psum_tensor("a0_pt2", [128, 1024], BF16))
        p_pv = p2.enter_context(nc.psum_tensor("a0_ppv", [32, 512], F32))
        SROW0 = p2.enter_context(nc.sbuf_tensor("a0_sr0", [128, 2048], F32))
        SROW1 = p2.enter_context(nc.sbuf_tensor("a0_sr1", [128, 2048], F32))
        iters = [(32 * j, tqc, c) for c in range(4) for j in range(4)
                 for tqc in range(NTC)]
        emit_attn_p2(
            S, prog, iters=iters,
            QNf=lambda b, tqc, c: QN[b:b + 32, c * 2048 + tqc * 128: c * 2048 + tqc * 128 + 128],
            KNf=lambda b, sg, c: KN[b:b + 32, c * 2048 + sg * 512: c * 2048 + sg * 512 + 512],
            VTf=lambda c, sc, b: VT[:, sc * 512 + c * 128 + b: sc * 512 + c * 128 + b + 32],
            OCf=lambda c, b, tqc: OC[b:b + 32, c * 2048 + tqc * 128: c * 2048 + tqc * 128 + 128],
            hd=32, psA=psA, psB=psB, p_t=p_t2, p_pv=p_pv,
            SROW=[SROW0, SROW1], EB=[EB, EBx], MSK=MSK, NM=NM, SR=SR, PTB=PT,
            start_waits=ph1)

    ph2 = S.fence_all()
    with contextlib.ExitStack() as p3:
        p_o = p3.enter_context(nc.psum_tensor("a0_po", [128, 512], F32))
        lnb = dict(
            SQ=p3.enter_context(nc.sbuf_tensor("a0_lnsq2", [128, 1024], F32)),
            LNR=p3.enter_context(nc.sbuf_tensor("a0_lnr", [33, 512], F32)),
            LNR2=p3.enter_context(nc.sbuf_tensor("a0_lnr2", [33, 512], F32)),
            p_st=p3.enter_context(nc.psum_tensor("a0_pst", [1, 1024], F32)),
            p_rb=p3.enter_context(nc.psum_tensor("a0_prb2", [128, 512], F32)),
            p_mb=p3.enter_context(nc.psum_tensor("a0_pmb", [128, 512], F32)),
        )
        o_free = None
        zt = None
        for tb in range(4):
            for dc in range(NDC):
                for kc in range(4):
                    w = []
                    if kc == 0:
                        w = ([o_free] if o_free else []) + (ph2 if (tb == 0 and dc == 0) else [])
                    g_o = S.add("tensor",
                                E_mm(p_o[:, :], WO[:, kc * 1024 + dc * 128: kc * 1024 + dc * 128 + 128],
                                     OC[:, kc * 2048 + tb * 512: kc * 2048 + tb * 512 + 512],
                                     kc == 0, kc == 3),
                                waits=w, inc=(pe if kc == 3 else None))
                zt = S.add("vector", E_tt(X[:, dc * T + tb * 512: dc * T + tb * 512 + 512],
                                          p_o[:, :],
                                          X[:, dc * T + tb * 512: dc * T + tb * 512 + 512],
                                          ALU.add),
                           waits=[g_o], inc=dve)
                o_free = zt
        if DEBUG:
            for i, src_ap in enumerate([XB[:, 0:2048], XB[:, 2048:4096],
                                        XB[:, 4096:6144], XB[:, 6144:8192]]):
                S.add("gpsimd", E_dma(prog.hbm[f"dbg{i}"], src_ap),
                      waits=[prog.pe.tok(), prog.act.tok(), prog.dve.tok()] + [prog.bmp[e].tok() for e in ENGS],
                      inc=prog.dmaW, inck=16, dma=True)
        emit_ln(S, prog, lnb, zt)
    return S


def stage_l0_ffn(prog, sctx):
    nc = prog.nc
    S = Script(prog)
    X, XB = prog.X, prog.XB
    al = lambda name, shape, dt: sctx.enter_context(nc.sbuf_tensor(name, shape, dt))
    pe, act, dve, dmaW = prog.pe, prog.act, prog.dve, prog.dmaW

    GWT = al("f0_gwt", [128, 64], BF16)
    ACT_ = al("f0_acat", [128, 1024], BF16)
    BCT = al("f0_bct", [128, 4096], BF16)
    WSL = al("f0_wsl", [128, 2 * 1024], BF16)
    LGT = al("f0_lgt", [128, 128], F32)
    E8 = al("f0_e8", [128, 8], F32)
    C8 = al("f0_c8", [128, 8], F32)
    L2T = al("f0_l2", [128, 8], F32)
    G1 = al("f0_g1", [128, 8], F32)
    SRW = al("f0_srw", [128, 2], F32)
    JNK = al("f0_jnk", [128, 8], F32)
    SEL1 = al("f0_sel1", [128, 128], F32)
    SEL2 = al("f0_sel2", [128, 128], F32)
    SELT1 = al("f0_selt1", [8, 2048], BF16)
    SELT2 = al("f0_selt2", [8, 2048], BF16)
    VSA = al("f0_vsa", [128, 16], F32)
    VST = al("f0_vst", [16, 128], F32)
    VSB = al("f0_vsb", [128, 2048], F32)
    VROW = VSB[0:1, 0:2048]
    XA = al("f0_xa", [128, 2048], F32)
    LG = XA[0:8, 0:2048]
    XS1 = al("f0_xs1", [128, 2048], BF16)
    XS2 = al("f0_xs2", [128, 2048], BF16)
    H = al("f0_h", [128, 32 * 512], BF16)
    BASE = al("f0_base", [128, 512], BF16)
    T1 = al("f0_t1", [128, 512], F32)
    T2 = al("f0_t2", [128, 512], F32)
    R1 = al("f0_r1", [128, 512], BF16)
    R2 = al("f0_r2", [128, 512], BF16)

    load_fm(S, prog, GWT, "gw")
    load_fm(S, prog, ACT_, "acat")
    load_fm(S, prog, BCT, "bcat")
    wl = prog.dmaW.tok()
    bi_col = prog.small("bi")
    gate_b = prog.small("gate_b")

    sel_free = None
    e_vsb = None
    e_xa = None
    e_xm = None
    t1_war = None
    with contextlib.ExitStack() as pA:
        p_g = pA.enter_context(nc.psum_tensor("f0_pg", [8, 512], F32))
        p_lgt = pA.enter_context(nc.psum_tensor("f0_plgt", [128, 128], F32))
        p_sel = pA.enter_context(nc.psum_tensor("f0_psel", [8, 512], F32))
        p_vs = pA.enter_context(nc.psum_tensor("f0_pvs", [16, 128], F32))
        p_bc = pA.enter_context(nc.psum_tensor("f0_pbc", [128, 512], F32))
        p_m = pA.enter_context(nc.psum_tensor("f0_pm", [128, 512], F32))
        p_xa = pA.enter_context(nc.psum_tensor("f0_pxa", [128, 512], F32))
        g_free = None
        for tb in range(4):
            for dc in range(NDC):
                g_g = S.add("tensor", E_mm(p_g[0:8, :], GWT[:, dc * 8: dc * 8 + 8],
                                           XB[:, dc * T + tb * 512: dc * T + tb * 512 + 512],
                                           dc == 0, dc == NDC - 1),
                            waits=([wl, g_free] if dc == 0 else ()),
                            inc=(pe if dc == NDC - 1 else None))
            g_free = S.add("vector", E_ts(LG[0:8, tb * 512:tb * 512 + 512], p_g[0:8, :],
                                          gate_b[0:8, :], ALU.add),
                           waits=[g_g], fence=True)
        tr_toks = []
        for tc in range(NTC):
            g_t = S.add("tensor", E_tr(p_lgt[:, tc * 8: tc * 8 + 8],
                                       LG[0:8, tc * 128: tc * 128 + 128],
                                       prog.ID[0:8, 0:8]),
                        waits=([g_free] if tc == 0 else ()),
                        inc=(pe if tc == NTC - 1 else None))
        if L0F_G <= 1:
            return S
        e_lgt = S.add("vector", E_copy(LGT[:, :], p_lgt[:, :]), waits=[g_t], fence=True)
        if L0F_G <= 2:
            return S
        iota = prog.IOTA[:, 0:8]
        iotaB = prog.IOTA[:, 8:16]
        for tc in range(NTC):
            lgt = LGT[:, tc * 8: tc * 8 + 8]
            t1 = S.add("vector", E_red(G1[:, 0:1], lgt, ALU.max), fence=True)
            t2 = S.add("vector", E_ts(G1[:, 1:2], G1[:, 0:1], -1.0, ALU.mult), fence=True)
            t3 = S.add("vector", E_ts(E8[:, :], lgt, G1[:, 0:1], ALU.is_equal), fence=True)
            t4 = S.add("vector", E_stt(C8[:, :], E8[:, :], -BIG, iotaB, ALU.mult, ALU.add), fence=True)
            t5 = S.add("vector", E_red(G1[:, 2:3], C8[:, :], ALU.min), fence=True)
            t6 = S.add("vector", E_ts(SEL1[:, tc * 8: tc * 8 + 8], iota, G1[:, 2:3],
                                      ALU.is_equal), fence=True)
            t7 = S.add("vector", E_stt(L2T[:, :], SEL1[:, tc * 8: tc * 8 + 8], NEG, lgt,
                                       ALU.mult, ALU.add), fence=True)
            t8 = S.add("vector", E_red(G1[:, 3:4], L2T[:, :], ALU.max), fence=True)
            t9 = S.add("vector", E_ts(E8[:, :], L2T[:, :], G1[:, 3:4], ALU.is_equal), fence=True)
            ta = S.add("vector", E_stt(C8[:, :], E8[:, :], -BIG, iotaB, ALU.mult, ALU.add), fence=True)
            tb_ = S.add("vector", E_red(G1[:, 4:5], C8[:, :], ALU.min), fence=True)
            tcx = S.add("vector", E_ts(SEL2[:, tc * 8: tc * 8 + 8], iota, G1[:, 4:5],
                                       ALU.is_equal), fence=True)
            s1 = S.add("scalar", E_act(JNK[:, :], lgt, AF.Exp, bias=G1[:, 1:2],
                                       accum=SRW[:, 0:1]), waits=[t8], fence=True)
            s2 = S.add("scalar", E_act(G1[:, 5:6], G1[:, 3:4], AF.Exp, bias=G1[:, 1:2]),
                       fence=True)
            v1 = S.add("vector", E_recip(SRW[:, 1:2], SRW[:, 0:1]), waits=[s1], fence=True)
            v2 = S.add("vector", E_stt(VSA[:, tc:tc + 1], G1[:, 5:6], 1.0, SRW[:, 1:2],
                                       ALU.add, ALU.mult), waits=[s2], fence=True)
        if L0F_G <= 3:
            return S
        for src_t, dst in [(SEL1, SELT1), (SEL2, SELT2)]:
            for grp in range(4):
                for k in range(4):
                    tc = grp * 4 + k
                    g_t = S.add("tensor", E_tr(p_sel[0:8, k * 128: k * 128 + 128],
                                               src_t[:, tc * 8: tc * 8 + 8],
                                               prog.ID[:, :]),
                                waits=([v2, sel_free] if k == 0 else ()),
                                inc=(pe if k == 3 else None))
                sel_free = S.add("vector", E_copy(dst[0:8, grp * 512: grp * 512 + 512],
                                                  p_sel[0:8, :]),
                                 waits=[g_t], fence=True)
        if L0F_G <= 4:
            return S
        g_vt = S.add("tensor", E_tr(p_vs[0:16, :], VSA[:, :], prog.ID[:, :]),
                     waits=[v2], inc=pe)
        e_vt = S.add("vector", E_copy(VST[0:16, :], p_vs[0:16, :]), waits=[g_vt], fence=True)
        d1 = S.add("gpsimd", E_dma(prog.hbm["scr"], VST[0:16, :]), waits=[e_vt],
                   inc=dmaW, inck=16, dma=True)
        d1t = prog.dmaW.tok()
        S.add("gpsimd", E_dma(VROW[0:1, :], prog.hbm["scr"].rearrange("p n -> (p n)")[None, :]),
              waits=[d1t], inc=dmaW, inck=16, dma=True)
        vrow_tok = prog.dmaW.tok()
        if L0F_G <= 5:
            return S
        xs_toks = []
        for tb in range(4):
            g_b = S.add("tensor", E_mm(p_bc[:, :], prog.rows("ones", 128),
                                       VROW[0:1, tb * 512: tb * 512 + 512], True, True),
                        waits=[vrow_tok] + ([e_vsb] if tb > 0 else []), inc=pe)
            e_vsb = S.add("vector", E_copy(VSB[:, tb * 512: tb * 512 + 512], p_bc[:, :]),
                          waits=[g_b], fence=True)
            for dc in range(NDC):
                g_xa = S.add("tensor", E_mm(p_xa[:, :], ACT_[:, dc * 128: dc * 128 + 128],
                                            XB[:, dc * T + tb * 512: dc * T + tb * 512 + 512],
                                            dc == 0, dc == NDC - 1),
                             waits=(([wl] + ([e_xa] if e_xa else [])) if dc == 0 else ()),
                             inc=(pe if dc == NDC - 1 else None))
            e_xa = S.add("vector", E_copy(XA[:, tb * 512: tb * 512 + 512], p_xa[:, :]),
                         waits=[g_xa], fence=True)
            for selt, xs in [(SELT1, XS1), (SELT2, XS2)]:
                g_m = S.add("tensor", E_mm(p_m[:, :], prog.E16[0:8, :],
                                           selt[0:8, tb * 512: tb * 512 + 512], True, True),
                            waits=[sel_free, e_xm], inc=pe)
                e_xm = S.add("vector", E_tt(xs[:, tb * 512: tb * 512 + 512], p_m[:, :],
                                            XA[:, tb * 512: tb * 512 + 512], ALU.mult),
                             waits=[g_m, e_xa], fence=True)
            xs_toks.append(e_xm)

    if L0F_PART == 0:
        return S
    phA = S.fence_all()
    with contextlib.ExitStack() as pB:
        P8 = [pB.enter_context(nc.psum_tensor(f"f0_p8_{i}", [128, 512], F32))
              for i in range(8)]
        p_h, p_l = P8[0], P8[1]
        wi_hbm = prog.add_input("wi").rearrange("(c p) m -> p c m", p=128)
        wo_hbm = prog.add_input("wo2")
        h_free = None
        l_free = None
        r1_done = r2_done = hw_done = None
        o_free = []
        b_done = {}
        zt = None

        def wi_slab(fc, wait):
            sb = fc % 2
            S.add("gpsimd",
                  E_dma(WSL[:, sb * 1024:(sb + 1) * 1024].rearrange("p (c m) -> p c m", c=8),
                        wi_hbm[:, :, fc * 128:(fc + 1) * 128]),
                  waits=wait, inc=dmaW, inck=16, dma=True)
            return prog.dmaW.tok()

        def wo_slab(fc, wait):
            sb = fc % 2
            S.add("gpsimd",
                  E_dma(WSL[:, sb * 1024:(sb + 1) * 1024], wo_hbm[fc * 128:(fc + 1) * 128, :]),
                  waits=wait, inc=dmaW, inck=16, dma=True)
            return prog.dmaW.tok()

        ntg = 1 if L0F_PART in (1, 2, 3) else 4
        for tg in range(ntg):
            og = tg * 512
            slab_tok = {}
            mm_done = {}
            w0 = [xs_toks[-1]] + (phA if tg == 0 else [b_done[30], b_done[31]])
            slab_tok[0] = wi_slab(0, w0)
            slab_tok[1] = wi_slab(1, [])
            for fc in range(32):
                sb = fc % 2
                for dc in range(NDC):
                    g_h = S.add("tensor",
                                E_mm(p_h[:, :], WSL[:, sb * 1024 + dc * 128: sb * 1024 + dc * 128 + 128],
                                     XB[:, dc * T + og: dc * T + og + 512],
                                     dc == 0, dc == NDC - 1),
                                waits=([slab_tok[fc], h_free] + (o_free if fc == 0 else [])
                                       if dc == 0 else ()),
                                inc=(pe if dc == NDC - 1 else None))
                mm_done[fc] = g_h
                if fc + 2 < 32:
                    slab_tok[fc + 2] = wi_slab(fc + 2, [mm_done[fc]])
                e_b = S.add("scalar", E_act(BASE[:, :], p_h[:, :], AF.Identity,
                                            bias=bi_col[:, fc:fc + 1]),
                            waits=[g_h], fence=True)
                h_free = e_b
                g_l1 = S.add("tensor", E_mm(p_l[:, :], BCT[:, fc * 128: fc * 128 + 128],
                                            XS1[:, og: og + 512], True, True),
                             waits=[l_free] + (o_free if fc == 0 else []), inc=pe)
                v_t1 = S.add("vector", E_tt(T1[:, :], p_l[:, :], BASE[:, :], ALU.add),
                             waits=[g_l1, e_b, r1_done], fence=True)
                g_l2 = S.add("tensor", E_mm(p_l[:, :], BCT[:, fc * 128: fc * 128 + 128],
                                            XS2[:, og: og + 512], True, True),
                             waits=[v_t1], inc=pe)
                r1_done = S.add("scalar", E_act(R1[:, :], T1[:, :], AF.Relu),
                                waits=[v_t1, hw_done], fence=True)
                v_t2 = S.add("vector", E_tt(T2[:, :], p_l[:, :], BASE[:, :], ALU.add),
                             waits=[g_l2, r2_done], fence=True)
                l_free = v_t2
                r2_done = S.add("scalar", E_act(R2[:, :], T2[:, :], AF.Relu),
                                waits=[v_t2], fence=True)
                hw_done = S.add("vector", E_tt(H[:, fc * 512: fc * 512 + 512], R1[:, :],
                                               R2[:, :], ALU.add),
                                waits=[r2_done], fence=True)
            if L0F_PART == 1:
                return S

            slab_tok = {}
            b_done = {}
            slab_tok[0] = wo_slab(0, [mm_done[30], mm_done[31]])
            slab_tok[1] = wo_slab(1, [])
            for fc in range(32):
                sb = fc % 2
                for dc in range(NDC):
                    w = []
                    if dc == 0:
                        w = [slab_tok[fc]]
                        if fc == 0:
                            w += [hw_done] + o_free
                    g_o = S.add("tensor",
                                E_mm(P8[dc][:, :],
                                     WSL[:, sb * 1024 + dc * 128: sb * 1024 + dc * 128 + 128],
                                     H[:, fc * 512: fc * 512 + 512],
                                     fc == 0, False),
                                waits=w, inc=(pe if dc == NDC - 1 else None))
                b_done[fc] = g_o
                if fc + 2 < 32:
                    slab_tok[fc + 2] = wo_slab(fc + 2, [b_done[fc]])
            o_free = []
            for dc in range(NDC):
                S.add("tensor",
                      E_mm(P8[dc][:, :], prog.rows("bo", 1024)[0:1, dc * 128: dc * 128 + 128],
                           VROW[0:1, og: og + 512], False, False),
                      waits=([b_done[31]] if dc == 0 else ()))
                g_bo = S.add("tensor",
                             E_mm(P8[dc][:, :], prog.rows("bo", 1024)[0:1, dc * 128: dc * 128 + 128],
                                  VROW[0:1, og: og + 512], False, True),
                             inc=pe)
                v_m = S.add("vector", E_tt(T1[:, :], P8[dc][:, :],
                                           VSB[:, og: og + 512], ALU.mult),
                            waits=[g_bo, t1_war], fence=True)
                zt = S.add("vector", E_tt(X[:, dc * T + og: dc * T + og + 512], T1[:, :],
                                          X[:, dc * T + og: dc * T + og + 512], ALU.add),
                           fence=True)
                o_free.append(zt)
                t1_war = zt
    if L0F_PART == 2:
        return S
    with contextlib.ExitStack() as pLN:
        lnb = dict(
            SQ=pLN.enter_context(nc.sbuf_tensor("f0_lnsq", [128, 1024], F32)),
            LNR=pLN.enter_context(nc.sbuf_tensor("f0_lnr", [33, 512], F32)),
            LNR2=pLN.enter_context(nc.sbuf_tensor("f0_lnr2", [33, 512], F32)),
            p_st=pLN.enter_context(nc.psum_tensor("f0_pst", [1, 1024], F32)),
            p_rb=pLN.enter_context(nc.psum_tensor("f0_prb", [128, 512], F32)),
            p_mb=pLN.enter_context(nc.psum_tensor("f0_pmb", [128, 512], F32)),
        )
        emit_ln(S, prog, lnb, zt)
    return S


def stage_l1_attn(prog, sctx):
    nc = prog.nc
    S = Script(prog)
    X, XB = prog.X, prog.XB
    al = lambda name, shape, dt: sctx.enter_context(nc.sbuf_tensor(name, shape, dt))
    pe, act, dve, dmaW = prog.pe, prog.act, prog.dve, prog.dmaW

    WSL = al("a1_wsl", [128, 2 * 1024], BF16)
    COS = al("a1_cos", [128, 2048], BF16)
    SIN = al("a1_sin", [128, 2048], BF16)
    QSF = al("a1_qsf", [128, 2048], F32)
    ROT = al("a1_rot", [128, 512], F32)
    TA = al("a1_ta", [128, 512], F32)
    Q1 = al("a1_q1", [128, 2048], BF16)
    K1 = al("a1_k1", [128, 2048], BF16)
    VB2 = al("a1_vb", [128, 2048], BF16)
    VT2 = al("a1_vt", [128, 2048], BF16)
    Y = al("a1_y", [128, 8 * 2048], BF16)
    EB = al("a1_eb", [128, 2048], BF16)
    PT = al("a1_pt", [128, 2048], BF16)
    MSK = al("a1_msk", [128, 4 * 512], BF16)
    NM = al("a1_nm", [128, 8], F32)
    SR = al("a1_sr", [128, 8], F32)

    qkv_hbm = prog.add_input("wqkv").rearrange("(c p) m -> p c m", p=128)
    S.add("gpsimd", E_dma(COS[:, :], prog.add_input("cos128")), inc=dmaW, inck=16, dma=True)
    S.add("gpsimd", E_dma(SIN[:, :], prog.add_input("sin128")), inc=dmaW, inck=16, dma=True)
    mask_ap = prog.add_input("masks")
    S.add("gpsimd", E_dma(MSK[:, :].rearrange("p (v q) -> p v q", v=4),
                          mask_ap.rearrange("v p q -> p v q")), inc=dmaW, inck=16, dma=True)
    wl = prog.dmaW.tok()

    with contextlib.ExitStack() as pA:
        p_a = pA.enter_context(nc.psum_tensor("a1_pa", [128, 512], F32))
        p_t = pA.enter_context(nc.psum_tensor("a1_pt2", [128, 1024], BF16))
        psA = pA.enter_context(nc.psum_tensor("a1_psA", [128, 512], F32))
        psB = pA.enter_context(nc.psum_tensor("a1_psB", [128, 512], F32))
        p_pv = pA.enter_context(nc.psum_tensor("a1_ppv", [64, 512], F32))
        SROW0 = pA.enter_context(nc.sbuf_tensor("a1_sr0", [128, 2048], F32))
        SROW1 = pA.enter_context(nc.sbuf_tensor("a1_sr1", [128, 2048], F32))
        EB2 = pA.enter_context(nc.sbuf_tensor("a1_eb2", [128, 2048], BF16))
        p2state = None

        a_free = None
        t_free = None
        s_free = None
        t2_free = None
        pv_free = None
        slab_reader = {0: None, 1: None}
        qsf_free = None
        for c in range(8):
            outs = {}
            rope_tok = {}
            for kind, col0 in [("q", c * 128), ("k", 1024 + c * 128), ("v", 2048 + c * 128)]:
                sb = 0 if kind != "v" else 1
                S.add("gpsimd",
                      E_dma(WSL[:, sb * 1024: sb * 1024 + 1024]
                            .rearrange("p (a m) -> p a m", a=8),
                            qkv_hbm[:, :, col0:col0 + 128]),
                      waits=[wl] + ([slab_reader[sb]] if slab_reader[sb] else []),
                      inc=dmaW, inck=16, dma=True)
                wtok = prog.dmaW.tok()
                for tb in range(4):
                    for dc in range(NDC):
                        g_p = S.add("tensor",
                                    E_mm(p_a[:, :],
                                         WSL[:, sb * 1024 + dc * 128: sb * 1024 + dc * 128 + 128],
                                         XB[:, dc * T + tb * 512: dc * T + tb * 512 + 512],
                                         dc == 0, dc == NDC - 1),
                                    waits=([wtok, a_free] if dc == 0 else ()),
                                    inc=(pe if dc == NDC - 1 else None))
                    if kind == "v":
                        ev = S.add("vector", E_copy(VB2[:, tb * 512: tb * 512 + 512], p_a[:, :]),
                                   waits=[g_p], fence=True)
                    else:
                        ev = S.add("scalar", E_act(QSF[:, tb * 512: tb * 512 + 512], p_a[:, :],
                                                   AF.Identity),
                                   waits=[g_p] + ([qsf_free] if tb == 0 else []), inc=act)
                    a_free = ev
                    outs.setdefault(kind, []).append(ev)
                slab_reader[sb] = g_p
                if kind == "v":
                    continue
                dst = Q1 if kind == "q" else K1
                last_ev = outs[kind][-1]
                for h0 in (0, 512, 1024, 1536):
                    hs = slice(h0, h0 + 512)
                    S.add("vector", E_copy(ROT[0:32, :], QSF[32:64, hs]),
                          waits=[last_ev] if h0 == 0 else ())
                    S.add("vector", E_copy(ROT[32:64, :], QSF[0:32, hs]))
                    S.add("vector", E_copy(ROT[64:96, :], QSF[96:128, hs]))
                    S.add("vector", E_copy(ROT[96:128, :], QSF[64:96, hs]), fence=True)
                    S.add("vector", E_tt(TA[:, :], ROT[:, :], SIN[:, hs], ALU.mult),
                          waits=[wl], fence=True)
                    qsf_free = S.add("vector", E_tt(ROT[:, :], QSF[:, hs], COS[:, hs],
                                                    ALU.mult), fence=True)
                    rope_tok[kind] = S.add("vector", E_tt(dst[:, hs], TA[:, :], ROT[:, :],
                                                          ALU.add), fence=True)
            rope_done = rope_tok["k"]
            vdone = outs["v"][-1] if "v" in outs else None
            for grp in range(2):
                for k in range(8):
                    sc8 = grp * 8 + k
                    g_t = S.add("tensor", E_tr(p_t[:, k * 128:k * 128 + 128],
                                               VB2[:, sc8 * 128:sc8 * 128 + 128],
                                               prog.IDB[:, :]),
                                waits=([vdone, t_free] if k == 0 else ()),
                                inc=(pe if k == 7 else None))
                ev = S.add("vector",
                           E_copy(VT2[:, grp * 1024: grp * 1024 + 1024], p_t[:, :]),
                           waits=[g_t], fence=True)
                t_free = ev
            vt_done = t_free
            if DEBUG and c == 0:
                for i, ap in enumerate([Q1[:, 0:2048], K1[:, 0:2048], VT2[:, 0:2048]]):
                    S.add("gpsimd", E_dma(prog.hbm[f"dbg{i}"], ap),
                          waits=[rope_done, vt_done], inc=dmaW, inck=16, dma=True)
            iters = [(64 * hh, tqc, c) for hh in range(2) for tqc in range(NTC)]
            p2state = emit_attn_p2(
                S, prog, iters=iters,
                QNf=lambda b, tqc, c: Q1[b:b + 64, tqc * 128: tqc * 128 + 128],
                KNf=lambda b, sg, c: K1[b:b + 64, sg * 512: sg * 512 + 512],
                VTf=lambda c, sc, b: VT2[:, sc * 128 + b: sc * 128 + b + 64],
                OCf=lambda c, b, tqc: Y[b:b + 64, c * 2048 + tqc * 128: c * 2048 + tqc * 128 + 128],
                hd=64, psA=psA, psB=psB, p_t=p_t, p_pv=p_pv,
                SROW=[SROW0, SROW1], EB=[EB, EB2], MSK=MSK, NM=NM, SR=SR, PTB=PT,
                start_waits=[rope_done, vt_done], state=p2state)
            s_free = p2state["sg_free"][0]

    ph2 = S.fence_all()
    prj_hbm = prog.add_input("wproj").rearrange("(c p) m -> p c m", p=128)
    with contextlib.ExitStack() as p3:
        p_o = p3.enter_context(nc.psum_tensor("a1_po", [128, 512], F32))
        lnb = dict(
            SQ=QSF[:, 0:1024],
            LNR=p3.enter_context(nc.sbuf_tensor("a1_lnr", [33, 512], F32)),
            LNR2=p3.enter_context(nc.sbuf_tensor("a1_lnr2", [33, 512], F32)),
            p_st=p3.enter_context(nc.psum_tensor("a1_pst", [1, 1024], F32)),
            p_rb=p3.enter_context(nc.psum_tensor("a1_prb", [128, 512], F32)),
            p_mb=p3.enter_context(nc.psum_tensor("a1_pmb", [128, 512], F32)),
        )
        o_free = None
        zt = None
        slab = {}
        prj_done = {}

        def prj_slab(dc, wait):
            sb = dc % 2
            S.add("gpsimd", E_dma(WSL[:, sb * 1024: sb * 1024 + 1024]
                                  .rearrange("p (a m) -> p a m", a=8),
                                  prj_hbm[:, :, dc * 128: dc * 128 + 128]),
                  waits=wait, inc=dmaW, inck=16, dma=True)
            return prog.dmaW.tok()

        slab[0] = prj_slab(0, ph2)
        slab[1] = prj_slab(1, [])
        for dc in range(NDC):
            for tb in range(4):
                for yc in range(NDC):
                    w = []
                    if yc == 0:
                        w = [slab[dc]] + ([o_free] if o_free else [])
                        if dc == 0 and tb == 0:
                            w += ph2
                    g_o = S.add("tensor",
                                E_mm(p_o[:, :],
                                     WSL[:, (dc % 2) * 1024 + yc * 128: (dc % 2) * 1024 + yc * 128 + 128],
                                     Y[:, yc * 2048 + tb * 512: yc * 2048 + tb * 512 + 512],
                                     yc == 0, yc == NDC - 1),
                                waits=w, inc=(pe if yc == NDC - 1 else None))
                prj_done[dc] = g_o
                zt = S.add("vector", E_tt(X[:, dc * T + tb * 512: dc * T + tb * 512 + 512],
                                          p_o[:, :],
                                          X[:, dc * T + tb * 512: dc * T + tb * 512 + 512],
                                          ALU.add),
                           waits=[g_o], fence=True)
                o_free = zt
            if dc + 2 < NDC:
                slab[dc + 2] = prj_slab(dc + 2, [prj_done[dc]])
        emit_ln(S, prog, lnb, zt)
    return S


def stage_l1_ffn(prog, sctx):
    nc = prog.nc
    S = Script(prog)
    X, XB = prog.X, prog.XB
    al = lambda name, shape, dt: sctx.enter_context(nc.sbuf_tensor(name, shape, dt))
    pe, act, dve, dmaW = prog.pe, prog.act, prog.dve, prog.dmaW

    AFCT = al("f1_afct", [128, 128], BF16)
    APRT = al("f1_aprt", [128, 512], BF16)
    BFCT = al("f1_bfct", [16, 4096], BF16)
    BPRT = al("f1_bprt", [16, 1024], BF16)
    SCL = al("f1_scl", [128, 8], F32)
    AM1 = al("f1_am1", [1, 4], F32)
    load_fm(S, prog, AFCT, "afc")
    load_fm(S, prog, APRT, "apr")
    load_fm(S, prog, BFCT, "bfcT")
    load_fm(S, prog, BPRT, "bprT")
    wl = prog.dmaW.tok()

    with contextlib.ExitStack() as pQ:
        WF = pQ.enter_context(nc.sbuf_tensor("f1_wf", [128, 2048], F32))
        WA = pQ.enter_context(nc.sbuf_tensor("f1_wa", [128, 2048], F32))
        WB = pQ.enter_context(nc.sbuf_tensor("f1_wb", [128, 2048], F32))
        WC = pQ.enter_context(nc.sbuf_tensor("f1_wc", [128, 2048], F32))
        WI = pQ.enter_context(nc.sbuf_tensor("f1_wi", [128, 2048], I32))
        WD = pQ.enter_context(nc.sbuf_tensor("f1_wd", [128, 2048], BF16))
        AMX = pQ.enter_context(nc.sbuf_tensor("f1_amx", [128, 2], F32))
        p_b1 = pQ.enter_context(nc.psum_tensor("f1_pb1", [128, 512], F32))
        for wi_, (wname, scr, scol) in enumerate([("wfc", "wq_fc", 0), ("wpr", "wq_pr", 3)]):
            w_hbm = prog.add_input(wname).rearrange("(c p) m -> p (c m)", p=128) \
                if False else prog.add_input(wname)
            kt, mt = w_hbm.shape
            nk = kt // 128
            flat = w_hbm.rearrange("(c p) m -> p c m", p=128)
            nslab = nk * mt // 2048

            def slab_ap(s):
                if mt >= 2048:
                    per = mt // 2048
                    return flat[:, s // per: s // per + 1,
                                (s % per) * 2048:(s % per) * 2048 + 2048]
                g = 2048 // mt
                return flat[:, s * g:(s + 1) * g, :]
            am_tok = None
            ld_tok = None
            for s in range(nslab):
                S.add("gpsimd", E_dma(WF[:, :].rearrange("p (a b) -> p a b",
                                                         a=1 if mt >= 2048 else 2048 // mt),
                                      slab_ap(s)),
                      waits=[ld_tok], inc=dmaW, inck=16, dma=True)
                ld_tok0 = prog.dmaW.tok()
                rd = S.add("vector", E_red(AMX[:, 0:1], WF[:, :], ALU.max, absval=True),
                           waits=[ld_tok0], fence=True)
                ld_tok = rd
                if s == 0:
                    am_tok = S.add("vector", E_copy(AMX[:, 1:2], AMX[:, 0:1]), fence=True)
                else:
                    am_tok = S.add("vector", E_tt(AMX[:, 1:2], AMX[:, 1:2], AMX[:, 0:1],
                                                  ALU.max), fence=True)
            pm = S.add("gpsimd", E_redC(AM1[0:1, 0:1], AMX[:, 1:2], ALU.max),
                       waits=[am_tok], fence=True)
            sc1 = S.add("vector", E_ts(AM1[0:1, 1:2], AM1[0:1, 0:1], 1.0 / 127.0, ALU.mult),
                        waits=[pm], fence=True)
            rc1 = S.add("vector", E_recip(AM1[0:1, 2:3], AM1[0:1, 1:2]), fence=True)
            g_b1 = S.add("tensor", E_mm(p_b1[:, 0:1], prog.rows("ones", 128),
                                        AM1[0:1, 2:3], True, True), waits=[rc1], inc=pe)
            cb1 = S.add("vector", E_copy(SCL[:, scol:scol + 1], p_b1[:, 0:1]),
                        waits=[g_b1], fence=True)
            g_b2 = S.add("tensor", E_mm(p_b1[:, 0:1], prog.rows("ones", 128),
                                        AM1[0:1, 1:2], True, True), waits=[cb1], inc=pe)
            cb2 = S.add("vector", E_copy(SCL[:, scol + 1:scol + 2], p_b1[:, 0:1]),
                        waits=[g_b2], fence=True)
            t2 = S.add("vector", E_ts(SCL[:, scol + 2:scol + 3], SCL[:, scol:scol + 1],
                                      2.0, ALU.mult), fence=True)
            scr_ap = prog.hbm[scr]
            st_tok = None
            for s in range(nslab):
                S.add("gpsimd", E_dma(WF[:, :].rearrange("p (a b) -> p a b",
                                                         a=1 if mt >= 2048 else 2048 // mt),
                                      slab_ap(s)),
                      waits=[ld_tok], inc=dmaW, inck=16, dma=True)
                ld_tok0 = prog.dmaW.tok()
                q1 = S.add("vector", E_ts(WA[:, :], WF[:, :], SCL[:, scol:scol + 1],
                                          ALU.mult), waits=[ld_tok0, t2, st_tok], fence=True)
                ld_tok = q1
                S.add("vector", E_copy(WI[:, :], WA[:, :]), fence=True)
                S.add("vector", E_copy(WB[:, :], WI[:, :]), fence=True)
                S.add("vector", E_tt(WC[:, :], WB[:, :], WA[:, :], ALU.is_gt), fence=True)
                S.add("vector", E_tt(WB[:, :], WB[:, :], WC[:, :], ALU.subtract), fence=True)
                S.add("vector", E_tt(WC[:, :], WA[:, :], WB[:, :], ALU.subtract), fence=True)
                qf = S.add("vector", E_stt(WD[:, :], WC[:, :], 0.5, WB[:, :],
                                           ALU.is_ge, ALU.add), fence=True)
                S.add("gpsimd", E_dma(scr_ap[:, s * 2048:(s + 1) * 2048], WD[:, :]),
                      waits=[qf], inc=dmaW, inck=16, dma=True)
                st_tok = prog.dmaW.tok()

    phQ = S.fence_all()
    with contextlib.ExitStack() as pF:
        WSL = pF.enter_context(nc.sbuf_tensor("f1_wsl", [128, 2 * 1024], BF16))
        WSL2 = pF.enter_context(nc.sbuf_tensor("f1_wsl2", [128, 2 * 4096], BF16))
        H = pF.enter_context(nc.sbuf_tensor("f1_h", [128, 32 * 512], BF16))
        G1 = pF.enter_context(nc.sbuf_tensor("f1_g1", [128, 512], BF16))
        XA1 = pF.enter_context(nc.sbuf_tensor("f1_xa1", [16, 512], BF16))
        XA2 = pF.enter_context(nc.sbuf_tensor("f1_xa2", [16, 512], BF16))
        TZ = pF.enter_context(nc.sbuf_tensor("f1_tz", [128, 512], F32))
        p_h = pF.enter_context(nc.psum_tensor("f1_ph", [128, 512], F32))
        p_x = pF.enter_context(nc.psum_tensor("f1_px", [16, 512], F32))
        p_o = pF.enter_context(nc.psum_tensor("f1_po", [128, 512], F32))
        lnb = dict(
            SQ=pF.enter_context(nc.sbuf_tensor("f1_lnsq", [128, 1024], F32)),
            LNR=pF.enter_context(nc.sbuf_tensor("f1_lnr", [33, 512], F32)),
            LNR2=pF.enter_context(nc.sbuf_tensor("f1_lnr2", [33, 512], F32)),
            p_st=pF.enter_context(nc.psum_tensor("f1_pst", [1, 1024], F32)),
            p_rb=pF.enter_context(nc.psum_tensor("f1_prb", [128, 512], F32)),
            p_mb=pF.enter_context(nc.psum_tensor("f1_pmb", [128, 512], F32)),
        )
        bfc_b = prog.small("bfc_b")
        bpr_b = prog.small("bpr_b")
        fcs = prog.hbm["wq_fc"].rearrange("p (c m) -> p c m", c=8)
        prs = prog.hbm["wq_pr"].rearrange("p (c m) -> p c m", c=32)
        h_free = None
        x_free = None
        o_free = None
        g_free = None
        zt = None
        prev_mm = {30: None, 31: None}
        prj_done = {6: None, 7: None}

        def fc_slab(fc, wait):
            sb = fc % 2
            S.add("gpsimd", E_dma(WSL[:, sb * 1024: sb * 1024 + 1024]
                                  .rearrange("p (a m) -> p a m", a=8),
                                  fcs[:, :, fc * 128: fc * 128 + 128]),
                  waits=wait, inc=dmaW, inck=16, dma=True)
            return prog.dmaW.tok()

        def pr_slab(dc, wait):
            sb = dc % 2
            S.add("gpsimd", E_dma(WSL2[:, sb * 4096: sb * 4096 + 4096]
                                  .rearrange("p (a m) -> p a m", a=32),
                                  prs[:, :, dc * 128: dc * 128 + 128]),
                  waits=wait, inc=dmaW, inck=16, dma=True)
            return prog.dmaW.tok()

        for tg in range(4):
            og = tg * 512
            for dc in range(NDC):
                g_x = S.add("tensor", E_mm(p_x[0:16, :], AFCT[:, dc * 16: dc * 16 + 16],
                                           XB[:, dc * T + og: dc * T + og + 512],
                                           dc == 0, dc == NDC - 1),
                            waits=([wl, x_free] + (phQ if tg == 0 else []) if dc == 0 else ()),
                            inc=(pe if dc == NDC - 1 else None))
            e_x1 = S.add("vector", E_ts(XA1[0:16, :], p_x[0:16, :], SCL[0:16, 2:3],
                                        ALU.mult), waits=[g_x], fence=True)
            x_free = e_x1
            slab = {}
            mm_done = {}
            slab[0] = fc_slab(0, (phQ if tg == 0 else [prev_mm[30], prev_mm[31]]))
            slab[1] = fc_slab(1, [])
            for fc in range(32):
                sb = fc % 2
                for dc in range(NDC):
                    g_h = S.add("tensor",
                                E_mm(p_h[:, :],
                                     WSL[:, sb * 1024 + dc * 128: sb * 1024 + dc * 128 + 128],
                                     XB[:, dc * T + og: dc * T + og + 512],
                                     dc == 0, False),
                                waits=([slab[fc], h_free] if dc == 0 else ()))
                g_h = S.add("tensor", E_mm(p_h[:, :], BFCT[0:16, fc * 128: fc * 128 + 128],
                                           XA1[0:16, :], False, True),
                            waits=[e_x1], inc=pe)
                mm_done[fc] = g_h
                if fc + 2 < 32:
                    slab[fc + 2] = fc_slab(fc + 2, [mm_done[fc]])
                e_g = S.add("scalar", E_act(G1[:, :], p_h[:, :], AF.Gelu,
                                            bias=bfc_b[:, fc:fc + 1], scale=SCL[:, 1:2]),
                            waits=[g_h, g_free], fence=True)
                h_free = e_g
                g_free = S.add("vector", E_copy(H[:, fc * 512: fc * 512 + 512], G1[:, :]),
                               waits=[e_g], fence=True)
            prev_mm = mm_done
            for fc in range(32):
                g_x2 = S.add("tensor", E_mm(p_x[0:16, :], APRT[:, fc * 16: fc * 16 + 16],
                                            H[:, fc * 512: fc * 512 + 512],
                                            fc == 0, fc == 31),
                             waits=([g_free, e_x1] if fc == 0 else ()),
                             inc=(pe if fc == 31 else None))
            e_x2 = S.add("vector", E_ts(XA2[0:16, :], p_x[0:16, :], SCL[0:16, 5:6],
                                        ALU.mult), waits=[g_x2], fence=True)
            x_free = e_x2
            slab = {}
            slab[0] = pr_slab(0, [mm_done[30], mm_done[31]] +
                              ([prj_done[6], prj_done[7]] if tg > 0 else []))
            slab[1] = pr_slab(1, [])
            for dc in range(NDC):
                sb = dc % 2
                for fc in range(32):
                    g_o = S.add("tensor",
                                E_mm(p_o[:, :],
                                     WSL2[:, sb * 4096 + fc * 128: sb * 4096 + fc * 128 + 128],
                                     H[:, fc * 512: fc * 512 + 512],
                                     fc == 0, False),
                                waits=([slab[dc], o_free, g_free] if fc == 0 else ()))
                g_o = S.add("tensor", E_mm(p_o[:, :], BPRT[0:16, dc * 128: dc * 128 + 128],
                                           XA2[0:16, :], False, True),
                            waits=[e_x2], inc=pe)
                prj_done[dc] = g_o
                if dc + 2 < NDC:
                    slab[dc + 2] = pr_slab(dc + 2, [prj_done[dc]])
                e_z = S.add("scalar", E_act(TZ[:, :], p_o[:, :], AF.Identity,
                                            bias=bpr_b[:, dc:dc + 1], scale=SCL[:, 4:5]),
                            waits=[g_o, zt], fence=True)
                o_free = e_z
                zt = S.add("vector", E_tt(X[:, dc * T + og: dc * T + og + 512], TZ[:, :],
                                          X[:, dc * T + og: dc * T + og + 512], ALU.add),
                           waits=[e_z], fence=True)
        emit_ln(S, prog, lnb, zt, refresh_xb=False)
    return S


def emit_attn_p2(S, prog, *, iters, QNf, KNf, VTf, OCf, hd, psA, psB, p_t, p_pv,
                SROW, EB, MSK, NM, SR, PTB, start_waits=(), state=None):
    pe, act, dve = prog.pe, prog.act, prog.dve
    n = len(iters)
    st = state if state is not None else {
        "sg_free": {0: None, 1: None}, "sr_rd": {0: None, 1: None},
        "t_free": None, "pv_free": None, "par0": 0}
    sg_free = st["sg_free"]
    sr_rd = st["sr_rd"]
    t_free = st["t_free"]
    pv_free = st["pv_free"]
    par0 = st["par0"]
    ev_done = {}

    def emit_scores(i):
        base, tqc, aux = iters[i]
        par = (par0 + i) % 2
        nsg = tqc // 4 + 1
        last = None
        for sg in range(nsg):
            ps = psA if sg % 2 == 0 else psB
            g_s = S.add("tensor",
                        E_mm(ps[:, :], QNf(base, tqc, aux), KNf(base, sg, aux),
                             True, True, tp=(base, 0)),
                        waits=[sg_free[sg % 2]] + (list(start_waits) if i == 0 and sg == 0 else []),
                        inc=pe)
            ev = S.add("scalar",
                       E_act(SROW[par][:, sg * 512: sg * 512 + 512], ps[:, :], AF.Identity),
                       waits=[g_s] + ([sr_rd[par]] if sg == 0 else []), inc=act)
            sg_free[sg % 2] = ev
            last = ev
        ev_done[i] = last

    emit_scores(0)
    for i in range(n):
        base, tqc, aux = iters[i]
        par = (par0 + i) % 2
        nsg = tqc // 4 + 1
        nsc = tqc + 1
        v = tqc % 4
        if i + 1 < n:
            emit_scores(i + 1)
        mk = S.add("gpsimd", E_tt(SROW[par][:, (nsg - 1) * 512:(nsg - 1) * 512 + 512],
                                  SROW[par][:, (nsg - 1) * 512:(nsg - 1) * 512 + 512],
                                  MSK[:, v * 512:v * 512 + 512], ALU.add),
                   waits=[ev_done[i]], fence=True)
        S.add("vector", E_red(NM[:, 4 * par:4 * par + 1],
                              SROW[par][:, 0:nsc * 128], ALU.max, negate=True),
              waits=[mk], fence=True)
        t_nmm = S.add("vector", E_ts(NM[:, 4 * par + 1:4 * par + 2],
                                     NM[:, 4 * par:4 * par + 1], 0.125, ALU.mult),
                      inc=dve, fence=True)
        e_e = S.add("scalar", E_act(EB[par][:, 0:nsg * 512], SROW[par][:, 0:nsg * 512],
                                    AF.Exp, bias=NM[:, 4 * par + 1:4 * par + 2], scale=0.125,
                                    accum=SR[:, 4 * par:4 * par + 1]),
                    waits=[t_nmm], inc=act, fence=True)
        S.add("vector", E_recip(SR[:, 4 * par + 1:4 * par + 2],
                                SR[:, 4 * par:4 * par + 1]),
              waits=[e_e], fence=True)
        t_nrm = S.add("vector", E_ts(EB[par][:, 0:nsg * 512], EB[par][:, 0:nsg * 512],
                                     SR[:, 4 * par + 1:4 * par + 2], ALU.mult),
                      inc=dve, fence=True)
        sr_rd[par] = t_nrm
        ptok = []
        for grp in range((nsc + 7) // 8):
            kmax = min(8, nsc - grp * 8)
            for k in range(kmax):
                sc8 = grp * 8 + k
                g_t = S.add("tensor", E_tr(p_t[:, k * 128:k * 128 + 128],
                                           EB[par][:, sc8 * 128:sc8 * 128 + 128],
                                           prog.IDB[:, :]),
                            waits=([t_nrm, t_free] if k == 0 else ()),
                            inc=(pe if k == kmax - 1 else None))
            if grp % 2 == 0:
                ev = S.add("vector", E_copy(PTB[:, grp * 1024: grp * 1024 + kmax * 128],
                                            p_t[:, 0:kmax * 128]), waits=[g_t], fence=True)
            else:
                ev = S.add("scalar", E_act(PTB[:, grp * 1024: grp * 1024 + kmax * 128],
                                           p_t[:, 0:kmax * 128], AF.Identity),
                           waits=[g_t], inc=prog.act)
            t_free = ev
            ptok.append(ev)
        for sc8 in range(nsc):
            w = []
            if sc8 == 0:
                w = ptok + [pv_free]
            g_pv = S.add("tensor",
                         E_mm(p_pv[0:hd, 0:128], VTf(aux, sc8, base),
                              PTB[:, sc8 * 128: sc8 * 128 + 128],
                              sc8 == 0, sc8 == nsc - 1),
                         waits=w, inc=(pe if sc8 == nsc - 1 else None))
        pv_free = S.add("vector", E_copy(OCf(aux, base, tqc), p_pv[0:hd, 0:128]),
                        waits=[g_pv], fence=True)
    st.update(sg_free=sg_free, sr_rd=sr_rd, t_free=t_free, pv_free=pv_free,
              par0=(par0 + n) % 2)
    st["last"] = pv_free
    return st


def stage_lntest(prog, sctx):
    nc = prog.nc
    S = Script(prog)
    lnb = dict(
        SQ=sctx.enter_context(nc.sbuf_tensor("lt_sq", [128, 1024], F32)),
        LNR=sctx.enter_context(nc.sbuf_tensor("lt_lnr", [33, 512], F32)),
        LNR2=sctx.enter_context(nc.sbuf_tensor("lt_lnr2", [33, 512], F32)),
        p_st=sctx.enter_context(nc.psum_tensor("lt_pst", [1, 1024], F32)),
        p_rb=sctx.enter_context(nc.psum_tensor("lt_prb", [128, 512], F32)),
        p_mb=sctx.enter_context(nc.psum_tensor("lt_pmb", [128, 512], F32)),
    )
    dbg = None
    if DEBUG:
        dbg = sctx.enter_context(nc.sbuf_tensor("lt_dbg", [65, 2048], F32))
    emit_ln(S, prog, lnb, None, dbg=dbg)
    if DEBUG:
        S.add("gpsimd", E_dma(prog.hbm["dbg0"][0:65, :], dbg[0:65, :]),
              waits=[prog.dve.tok()] + [prog.bmp[e].tok() for e in ENGS], inc=prog.dmaW, inck=16, dma=True)
    return S



STAGE_BUILDERS = {}


def _register():
    STAGE_BUILDERS["l0a"] = stage_l0_attn
    STAGE_BUILDERS["l0f"] = stage_l0_ffn
    STAGE_BUILDERS["l1a"] = stage_l1_attn
    STAGE_BUILDERS["l1f"] = stage_l1_ffn
    STAGE_BUILDERS["lntest"] = stage_lntest
    for nm, fn in [("l0f", "stage_l0_ffn"), ("l1a", "stage_l1_attn"),
                   ("l1f", "stage_l1_ffn")]:
        if fn in globals():
            STAGE_BUILDERS[nm] = globals()[fn]


def build_group(meta, stage_names):
    prog = Prog(meta, [])
    _register()
    run_stage(prog, stage_load(prog))
    for nm in stage_names:
        with contextlib.ExitStack() as sctx:
            s = STAGE_BUILDERS[nm](prog, sctx)
            run_stage(prog, s)
    run_stage(prog, stage_store(prog))
    prog.ctx.close()
    return prog


def run_group(prog, meta, xs):
    in_maps = []
    for c in range(NCORES):
        im = {}
        for n in prog.input_names:
            im[n] = xs[c] if n == "x_in" else meta[n]
        in_maps.append(im)
    res = run_bass_kernel_spmd(prog.nc, in_maps, core_ids=list(range(NCORES)))
    return [np.ascontiguousarray(np.asarray(res.results[c]["x_out"], np.float32))
            for c in range(NCORES)]


GROUPS = [["l0a", "l0f", "l1a", "l1f"]]


def kernel(**inputs):
    meta = host_prepare(inputs)
    x = np.asarray(inputs["L0_attn_x"], np.float32)
    xs = [np.ascontiguousarray(x[c].T) for c in range(NCORES)]
    for g in GROUPS:
        prog = build_group(meta, g)
        xs = run_group(prog, meta, xs)
    return np.stack([c.T for c in xs]).astype(np.float32)
